# version 33
# baseline (speedup 1.0000x reference)
import contextlib
import math
import numpy as np
import ml_dtypes
import concourse.bass as bass
import concourse.mybir as mybir
from concourse.bass import IndirectOffsetOnAxis
from concourse.bass_types import DynSlice
from concourse.expressions import RuntimeValue
from concourse.bass_utils import run_bass_kernel_spmd

F32 = mybir.dt.float32
BF16 = mybir.dt.bfloat16
I32 = mybir.dt.int32
AF = mybir.ActivationFunctionType
ALU = mybir.AluOpType
AX = mybir.AxisListType
ET = mybir.EngineType

B, T, D = 8, 4096, 1024
H, HD, HALF = 16, 64, 32
E, FF = 8, 4096
CAPACITY = 8192
SLOTB = 2048
NSLOT = E * SLOTB + 128
BIGRANK = 1.0e6
RING = 8
NT = T // 128
ND = D // 128
NF = FF // 128
EPS = 1e-5
BIGSLOT = 4_000_000.0
LN10K = math.log(10000.0)
TWO_PI = 2.0 * math.pi
CW1 = float(np.float32(6.28125))
CW2 = float(np.float32(TWO_PI - 6.28125))
CW3 = float(TWO_PI - CW1 - float(np.float32(TWO_PI - 6.28125)))
RNDC = float(np.float32(1.5 * 2 ** 23))


class Sem:
    def __init__(self, es, nc, name):
        self.h = es.enter_context(nc.semaphore(name))
        self.n = 0

    def inc(self, instr, k=1):
        self.n += k
        return instr.then_inc(self.h, k)

    def dma(self, instr):
        return self.inc(instr, 16)

    def wait(self, eng, n=None):
        v = self.n if n is None else n
        if v > 0:
            eng.wait_ge(self.h, v)


def make_identity(nc, g, ap):
    g.memset(ap, 0.0)
    g.affine_select(out=ap, in_=ap, compare_op=ALU.not_equal, fill=1.0,
                    base=0, pattern=[[-1, ap.shape[0]]], channel_multiplier=1)


class Builder:

    ENG = {"T": "tensor", "V": "vector", "A": "scalar", "G": "gpsimd", "S": "sync"}

    def __init__(self, nc, dbg=()):
        self.nc = nc
        self.dbg = dbg
        self.ap = {}
        self.c = {}

    def _start(self, semnames):
        self.es = contextlib.ExitStack()
        self.sem = {k: Sem(self.es, self.nc, k) for k in semnames}
        self.block = self.es.enter_context(self.nc.Block())

    def sec(self, which):
        return getattr(self.block, self.ENG[which])

    def sbuf(self, name, shape, dt=F32):
        es = getattr(self, "cur_es", None) or self.es
        return es.enter_context(self.nc.sbuf_tensor(name, shape, dt))

    def psum(self, name, shape, dt=F32):
        es = getattr(self, "cur_es", None) or self.es
        return es.enter_context(self.nc.psum_tensor(name, shape, dt))

    @contextlib.contextmanager
    def stage(self):
        self.barrier("T", "V", "A", "G", "S")
        prev = getattr(self, "cur_es", None)
        self.cur_es = contextlib.ExitStack()
        try:
            with self.cur_es:
                yield
        finally:
            self.cur_es = prev

    def barrier(self, *whiches):
        snapshot = [(s, s.n) for s in self.sem.values()]
        for w in whiches:
            @self.sec(w)
            def _(e, snapshot=snapshot):
                for s, n in snapshot:
                    s.wait(e, n)

    def din(self, name, shape, dt=F32):
        self.ap[name] = self.nc.dram_tensor(name, shape, dt, kind="ExternalInput").ap()

    def dout(self, name, shape, dt=F32):
        self.ap[name] = self.nc.dram_tensor(name, shape, dt, kind="ExternalOutput").ap()

    def dint(self, name, shape, dt=F32):
        self.ap[name] = self.nc.dram_tensor(name, shape, dt, kind="Internal").ap()

    def dshr(self, name, shape, dt=F32):
        self.ap[name] = self.nc.dram_tensor(
            name, shape, dt, kind="Internal", addr_space="Shared").ap()

    def mk_top2_scratch(self, prefix, G):
        sc = {}
        sc["s4"] = self.sbuf(prefix + "_s4", [128, G * 4])
        sc["s2"] = self.sbuf(prefix + "_s2", [128, G * 2])
        sc["m1"] = self.sbuf(prefix + "_m1", [128, G])
        sc["rep"] = self.sbuf(prefix + "_rep", [128, G * E])
        sc["t3"] = self.sbuf(prefix + "_t3", [128, G * E])
        sc["o1"] = self.sbuf(prefix + "_o1", [128, G * E])
        sc["o2"] = self.sbuf(prefix + "_o2", [128, G * E])
        sc["lgm"] = self.sbuf(prefix + "_lgm", [128, G * E])
        sc["iotaCr"] = self.sbuf(prefix + "_iC", [128, G * E])
        return sc

    def init_top2_iota(self, g, sc, G):
        g.iota(sc["iotaCr"][:].rearrange("p (a e) -> p a e", e=E),
               pattern=[[0, G], [-1, E]], base=E, channel_multiplier=0,
               allow_small_or_imprecise_dtypes=True)

    def _emit_top2(self, v, lgbuf, G, sc, mkout):
        def tree_max(buf):
            b3 = buf[:].rearrange("p (g e) -> p g e", e=E)
            s4v = sc["s4"][:].rearrange("p (g e) -> p g e", e=4)
            s2v = sc["s2"][:].rearrange("p (g e) -> p g e", e=2)
            m1v = sc["m1"][:].rearrange("p (g e) -> p g e", e=1)
            v.tensor_tensor(s4v, b3[:, :, 0:4], b3[:, :, 4:8], op=ALU.max)
            v.drain()
            v.tensor_tensor(s2v, s4v[:, :, 0:2], s4v[:, :, 2:4], op=ALU.max)
            v.drain()
            v.tensor_tensor(m1v, s2v[:, :, 0:1], s2v[:, :, 1:2], op=ALU.max)
            v.drain()

        def rep_m1():
            repv = sc["rep"][:].rearrange("p (g e) -> p g e", e=E)
            m1v = sc["m1"][:].rearrange("p (g e) -> p g e", e=1)
            v.tensor_copy(repv[:, :, 0:1], m1v)
            v.drain()
            v.tensor_copy(repv[:, :, 1:2], repv[:, :, 0:1])
            v.drain()
            v.tensor_copy(repv[:, :, 2:4], repv[:, :, 0:2])
            v.drain()
            v.tensor_copy(repv[:, :, 4:8], repv[:, :, 0:4])
            v.drain()

        def onehot(srcbuf, dstbuf):
            tree_max(srcbuf)
            rep_m1()
            v.tensor_tensor(sc["t3"][:], srcbuf[:], sc["rep"][:], op=ALU.is_equal)
            v.drain()
            v.tensor_tensor(sc["t3"][:], sc["t3"][:], sc["iotaCr"][:], op=ALU.mult)
            v.drain()
            tree_max(sc["t3"])
            rep_m1()
            v.tensor_tensor(dstbuf[:], sc["iotaCr"][:], sc["rep"][:],
                            op=ALU.is_equal)
            v.drain()

        onehot(lgbuf, sc["o1"])
        v.scalar_tensor_tensor(sc["lgm"][:], sc["o1"][:], -1e9, lgbuf[:],
                               op0=ALU.mult, op1=ALU.add)
        v.drain()
        onehot(sc["lgm"], sc["o2"])
        v.tensor_tensor(mkout[:], sc["o1"][:], sc["o2"][:], op=ALU.add)
        v.drain()

    def finale(self):
        nc = self.nc
        dS = self.sem["dS"]
        ap = self.ap
        dbg = self.dbg

        @self.sec("S")
        def _(sy):
            for s in self.sem.values():
                s.wait(sy)
            for name in dbg:
                if name not in ap:
                    continue
                src = ap[name]
                o = nc.dram_tensor("dbg_" + name, list(src.shape), src.dtype,
                                   kind="ExternalOutput").ap()
                dS.dma(sy.dma_start(out=o[:, :], in_=src[:, :]))
            sy.wait_ge(dS.h, dS.n)



class L1(Builder):
    def __init__(self, nc, dbg=(), upto=None):
        super().__init__(nc, dbg)
        self.upto = upto
        self._declare()

    def _declare(self):
        self.din("x", [T, D])
        self.din("wz", [D, D]); self.din("bz", [D])
        self.din("wh", [D, D]); self.din("bh", [D])
        self.din("logd", [D])
        self.din("ln1g", [D]); self.din("ln1b", [D])
        self.din("wqkv", [D, 3 * D]); self.din("wproj", [D, D])
        self.din("ln2g", [D]); self.din("ln2b", [D]); self.din("wgate", [D, E])
        self.dout("r3", [T, D])
        self.dout("h2", [T, D], BF16)
        self.dout("logits", [T, E])
        self.dout("counts", [1, E])
        self.dint("xn_h", [T, D], BF16)
        self.dint("h_h", [D, T])
        self.dint("hd_h", [T, D])
        self.dint("r2_h", [T, D])
        self.dint("xp_h", [T, D])
        self.dint("xpp_h", [T, D])
        self.dint("h1_h", [T, D], BF16)
        self.dint("qk_h", [2 * D, T], BF16)
        self.dint("v_h", [T, D], BF16)

    def build(self):
        self._start(["dS", "dG", "pe", "dve", "act", "gp"])
        stages = ["constants", "st_ln_outer", "st_mingru", "st_diffusion",
                  "st_resid_ln", "st_qkv", "st_attention", "st_proj_r3",
                  "st_ln2_logits"]
        if self.upto is not None:
            stages = stages[:stages.index(self.upto) + 1]
        with self.es:
            for st in stages:
                if st == "constants":
                    getattr(self, st)()
                else:
                    with self.stage():
                        getattr(self, st)()
            self.finale()
        return self.ap

    def constants(self):
        nc, ap, c = self.nc, self.ap, self.c
        dS, gp, dve, act, pe = (self.sem[k] for k in ["dS", "gp", "dve", "act", "pe"])
        c["bz"] = self.sbuf("c_bz", [128, ND])
        c["negbz"] = self.sbuf("c_negbz", [128, ND])
        c["bh"] = self.sbuf("c_bh", [128, ND])
        c["logd"] = self.sbuf("c_logd", [128, ND])
        c["dcoef"] = self.sbuf("c_dcoef", [128, ND])
        c["one_m2d"] = self.sbuf("c_one_m2d", [128, ND])
        c["ln1g"] = self.sbuf("c_ln1g", [128, ND])
        c["ln1b_f"] = self.sbuf("c_ln1b_f", [128, ND])
        c["ln1bg_tmp"] = self.sbuf("c_ln1bg_tmp", [128, ND])
        c["ln1b_bf"] = self.sbuf("c_ln1b_bf", [128, ND], BF16)
        c["ident_bf"] = self.sbuf("c_ident_bf", [128, 128], BF16)
        c["ident_f"] = self.sbuf("c_ident_f", [128, 128], F32)
        c["shiftc"] = self.sbuf("c_shiftc", [128, 128], BF16)
        c["corner_lo"] = self.sbuf("c_corner_lo", [128, 128], BF16)
        c["corner_hi"] = self.sbuf("c_corner_hi", [128, 128], BF16)
        c["ones_col_bf"] = self.sbuf("c_ones_col_bf", [128, 1], BF16)
        c["ones_row1"] = self.sbuf("c_ones_row1", [1, 128], BF16)
        c["ones_row1f"] = self.sbuf("c_ones_row1f", [1, 128], F32)
        c["epscol"] = self.sbuf("c_epscol", [128, 1], F32)

        @self.sec("S")
        def _(sy):
            for nm, dst in (("bz", c["bz"]), ("bh", c["bh"]), ("logd", c["logd"]),
                            ("ln1g", c["ln1g"]), ("ln1b", c["ln1b_f"])):
                src = ap[nm]
                for j in range(ND):
                    dS.dma(sy.dma_start(out=dst[:, j:j + 1],
                                        in_=src[j * 128:(j + 1) * 128].rearrange("(p one) -> p one", one=1)))

        self._const_tmp = contextlib.ExitStack()

        @self.sec("G")
        def _(g):
            make_identity(nc, g, c["ident_bf"][:])
            make_identity(nc, g, c["ident_f"][:])
            g.memset(c["shiftc"][:], 0.0)
            g.affine_select(out=c["shiftc"][:], in_=c["shiftc"][:],
                            compare_op=ALU.not_equal, fill=1.0, base=-1,
                            pattern=[[-1, 128]], channel_multiplier=1)
            g.affine_select(out=c["shiftc"][:], in_=c["shiftc"][:],
                            compare_op=ALU.not_equal, fill=1.0, base=1,
                            pattern=[[-1, 128]], channel_multiplier=1)
            g.memset(c["corner_lo"][:], 0.0)
            g.memset(c["corner_hi"][:], 0.0)
            g.affine_select(out=c["corner_lo"][:], in_=c["corner_lo"][:],
                            compare_op=ALU.not_equal, fill=1.0, base=-127,
                            pattern=[[200, 128]], channel_multiplier=1)
            g.affine_select(out=c["corner_hi"][:], in_=c["corner_hi"][:],
                            compare_op=ALU.not_equal, fill=1.0, base=200 * 127,
                            pattern=[[-200, 128]], channel_multiplier=1)
            g.memset(c["ones_col_bf"][:], 1.0)
            g.memset(c["ones_row1"][:], 1.0)
            g.memset(c["ones_row1f"][:], 1.0)
            g.memset(c["epscol"][:], EPS)
            gp.inc(g.engine_nop())


        @self.sec("A")
        def _(a):
            gp.wait(a)
            dS.wait(a)
            a.activation(c["dcoef"][:], c["logd"][:], AF.Exp)
            a.drain()
            act.inc(a.nop())

        @self.sec("V")
        def _(v):
            act.wait(v)
            v.tensor_scalar(c["one_m2d"][:], c["dcoef"][:], -2.0, 1.0,
                            op0=ALU.mult, op1=ALU.add)
            v.tensor_scalar(c["negbz"][:], c["bz"][:], -1.0, None,
                            op0=ALU.mult)
            v.drain()
            v.reciprocal(c["ln1bg_tmp"][:], c["ln1g"][:])
            v.drain()
            v.tensor_tensor(c["ln1b_bf"][:], c["ln1b_f"][:], c["ln1bg_tmp"][:],
                            op=ALU.mult)
            v.drain()
            dve.inc(v.nop())

        self._const_tmp.close()


    def _mk_rope_tables(self, ang, iot, kq, invf):
        nc, c = self.nc, self.c
        dS, gp, dve, act = (self.sem[k] for k in ["dS", "gp", "dve", "act"])

        @self.sec("G")
        def _(g):
            g.iota(iot[:], pattern=[[1, T]], base=0, channel_multiplier=0,
                   allow_small_or_imprecise_dtypes=True)
            for blk in range(4):
                g.iota(invf[blk * 32:(blk + 1) * 32, :], pattern=[[0, 1]], base=0,
                       channel_multiplier=1, allow_small_or_imprecise_dtypes=True)
            gp.inc(g.engine_nop())

        @self.sec("A")
        def _(a):
            gp.wait(a)
            a.activation(invf[:], invf[:], AF.Exp, scale=-LN10K / 32.0)
            a.drain()
            act.inc(a.nop())

        for which, phase in (("cosrep", 0.5 * math.pi), ("sinrep", 0.0)):
            @self.sec("V")
            def _(v, which=which, phase=phase):
                act.wait(v)
                v.tensor_scalar(ang[:], iot[:], invf[:, 0:1], float(phase),
                                op0=ALU.mult, op1=ALU.add)
                v.drain()
                v.tensor_scalar(kq[:], ang[:], 1.0 / TWO_PI, RNDC,
                                op0=ALU.mult, op1=ALU.add)
                v.tensor_scalar(kq[:], kq[:], RNDC, None, op0=ALU.subtract)
                v.drain()
                for cw in (CW1, CW2, CW3):
                    v.scalar_tensor_tensor(ang[:], kq[:], -cw, ang[:],
                                           op0=ALU.mult, op1=ALU.add)
                v.drain()
                v.tensor_scalar(ang[:], ang[:], float(math.pi), None, op0=ALU.min)
                v.tensor_scalar(ang[:], ang[:], float(-math.pi), None, op0=ALU.max)
                v.drain()
                dve.inc(v.nop())

            @self.sec("A")
            def _(a, which=which):
                dve.wait(a)
                a.activation(c[which][:], ang[:], AF.Sin)
                a.drain()
                act.inc(a.nop())

    def _mk_cmask(self):
        c = self.c
        gp = self.sem["gp"]
        c["cmask"] = self.sbuf("at_cmask", [128, 4 * 512], BF16)

        @self.sec("G")
        def _(g):
            g.memset(c["cmask"][:], 1.0)
            for a in range(4):
                g.affine_select(out=c["cmask"][:, a * 512:(a + 1) * 512],
                                in_=c["cmask"][:, a * 512:(a + 1) * 512],
                                compare_op=ALU.is_ge, fill=0.0,
                                base=-a * 128, pattern=[[1, 512]], channel_multiplier=-1)
            gp.inc(g.engine_nop())

    def _mk_ln2_consts(self):
        nc, ap, c = self.nc, self.ap, self.c
        dS, pe, dve, gp = (self.sem[k] for k in ["dS", "pe", "dve", "gp"])
        c["g2rep"] = self.sbuf("lz_g2rep", [128, D])
        c["b2rep"] = self.sbuf("lz_b2rep", [128, D])
        c["wgate_sb"] = self.sbuf("lz_wgate", [128, ND * E])
        g2row = self.sbuf("lz_g2row", [1, D])
        b2row = self.sbuf("lz_b2row", [1, D])
        pbc = self.psum("lz_pbc", [128, 512])

        @self.sec("S")
        def _(sy):
            dS.dma(sy.dma_start(out=g2row[:], in_=ap["ln2g"].rearrange("(one d) -> one d", one=1)))
            dS.dma(sy.dma_start(out=b2row[:], in_=ap["ln2b"].rearrange("(one d) -> one d", one=1)))
            for j in range(ND):
                dS.dma(sy.dma_start(out=c["wgate_sb"][:, j * E:(j + 1) * E],
                                    in_=ap["wgate"][j * 128:(j + 1) * 128, :]))
        ld_mark = dS.n

        for which, dst in ((g2row, "g2rep"), (b2row, "b2rep")):
            for ch in range(2):
                @self.sec("T")
                def _(t, which=which, ch=ch, ld_mark=ld_mark):
                    dS.wait(t, ld_mark)
                    dve.wait(t)
                    pe.inc(t.matmul(pbc[:, :], self.c["ones_row1f"][:, :],
                                    which[:, ch * 512:(ch + 1) * 512],
                                    start=True, stop=True))

                @self.sec("V")
                def _(v, dst=dst, ch=ch):
                    pe.wait(v)
                    v.tensor_copy(self.c[dst][:, ch * 512:(ch + 1) * 512], pbc[:, :])
                    v.drain()
                    dve.inc(v.nop())

    def st_ln_outer(self):
        ap, c = self.ap, self.c
        dS, dve, act = self.sem["dS"], self.sem["dve"], self.sem["act"]
        NB = 4
        x_sb = [self.sbuf(f"ln_x{s}", [128, NB * D]) for s in range(2)]
        xn_sb = [self.sbuf(f"ln_xn{s}", [128, NB * D], BF16) for s in range(2)]
        scr = self.sbuf("ln_scr", [128, D])
        st_sq = [self.sbuf(f"ln_sq{s}", [128, NB]) for s in range(2)]
        st_sm = self.sbuf("ln_sm", [128, NB])
        st_m = [self.sbuf(f"ln_m{s}", [128, NB]) for s in range(2)]
        st_v = [self.sbuf(f"ln_v{s}", [128, NB]) for s in range(2)]
        st_r = [self.sbuf(f"ln_r{s}", [128, NB]) for s in range(2)]
        tmp = self.sbuf("ln_tmp", [128, NB])
        xn_marks, st_marks, var_marks = {}, {}, {}
        for s in range(NT // NB):
            rows = slice(s * NB * 128, (s + 1) * NB * 128)
            p2 = s % 2

            @self.sec("S")
            def _(sy, rows=rows, s=s, p2=p2):
                if s >= 2:
                    dve.wait(sy, xn_marks[s - 2])
                    dS.wait(sy, st_marks[s - 2])
                dS.dma(sy.dma_start(
                    out=x_sb[p2][:].rearrange("p (a d) -> p a d", d=D),
                    in_=ap["x"][rows, :].rearrange("(a p) d -> p a d", p=128)))
            ld_mark = dS.n

            @self.sec("A")
            def _(a, s=s, p2=p2, ld_mark=ld_mark):
                dS.wait(a, ld_mark)
                if s >= 2:
                    dve.wait(a, var_marks[s - 2])
                for i in range(NB):
                    a.activation(scr[:], x_sb[p2][:, i * D:(i + 1) * D], AF.Square,
                                 accum_out=st_sq[p2][:, i:i + 1])
                a.drain()
                act.inc(a.nop())
            sq_mark = act.n

            @self.sec("V")
            def _(v, s=s, p2=p2, ld_mark=ld_mark, sq_mark=sq_mark):
                dS.wait(v, ld_mark)
                act.wait(v, sq_mark)
                for i in range(NB):
                    v.reduce_sum(st_sm[:, i:i + 1], x_sb[p2][:, i * D:(i + 1) * D],
                                 axis=AX.X)
                v.drain()
                v.tensor_scalar(st_m[p2][:], st_sm[:], 1.0 / D, None, op0=ALU.mult)
                v.drain()
                v.tensor_tensor(tmp[:], st_m[p2][:], st_m[p2][:], op=ALU.mult)
                v.tensor_scalar(st_v[p2][:], st_sq[p2][:], 1.0 / D, None, op0=ALU.mult)
                v.drain()
                v.tensor_tensor(st_v[p2][:], st_v[p2][:], tmp[:], op=ALU.subtract)
                v.drain()
                dve.inc(v.nop())
            var_marks[s] = dve.n

            @self.sec("A")
            def _(a, s=s, p2=p2):
                dve.wait(a, var_marks[s])
                a.activation(st_r[p2][:], st_v[p2][:], AF.Ln, bias=c["epscol"][:, 0:1])
                a.drain()
                a.activation(st_r[p2][:], st_r[p2][:], AF.Exp, scale=-0.5)
                a.drain()
                act.inc(a.nop())
            r_mark = act.n

            @self.sec("V")
            def _(v, s=s, p2=p2, r_mark=r_mark):
                act.wait(v, r_mark)
                for i in range(NB):
                    v.tensor_scalar(xn_sb[p2][:, i * D:(i + 1) * D],
                                    x_sb[p2][:, i * D:(i + 1) * D],
                                    st_m[p2][:, i:i + 1], st_r[p2][:, i:i + 1],
                                    op0=ALU.subtract, op1=ALU.mult)
                v.drain()
                dve.inc(v.nop())
            xn_marks[s] = dve.n

            @self.sec("S")
            def _(sy, rows=rows, s=s, p2=p2):
                dve.wait(sy, xn_marks[s])
                dS.dma(sy.dma_start(
                    out=ap["xn_h"][rows, :].rearrange("(a p) d -> p a d", p=128),
                    in_=xn_sb[p2][:].rearrange("p (a d) -> p a d", d=D)))
            st_marks[s] = dS.n

    def st_mingru(self):
        nc, ap, c = self.nc, self.ap, self.c
        dS, dG, pe, dve, act = (self.sem[k] for k in ["dS", "dG", "pe", "dve", "act"])
        xnT = self.sbuf("mg_xnT", [128, ND * T], BF16)
        wz_bf = self.sbuf("mg_wz", [128, ND * D], BF16)
        wh_bf = self.sbuf("mg_wh", [128, ND * D], BF16)
        aT = self.sbuf("mg_aT", [128, T])
        bT = self.sbuf("mg_bT", [128, T])
        zT = self.sbuf("mg_zT", [128, T])
        htT = self.sbuf("mg_htT", [128, T])
        hT = self.sbuf("mg_hT", [128, T])
        pz = [self.psum(f"mg_pz{i}", [128, 1024]) for i in range(2)]
        ph = [self.psum(f"mg_ph{i}", [128, 1024]) for i in range(2)]
        self.barrier("S", "G")

        @self.sec("S")
        def _(sy):
            for j in range(ND):
                dS.dma(sy.dma_start(out=xnT[:, j * T:(j + 1) * T],
                                    in_=ap["xn_h"][:, j * 128:(j + 1) * 128],
                                    transpose=True))
        xnT_mark = dS.n

        @self.sec("G")
        def _(g):
            for nm, dst in (("wz", wz_bf), ("wh", wh_bf)):
                for k in range(ND):
                    dG.dma(g.dma_start(out=dst[:, k * D:(k + 1) * D],
                                       in_=ap[nm][k * 128:(k + 1) * 128, :]))
        w_mark = dG.n

        scan_marks = []
        amarks = []
        for j in range(ND):
            for g2 in range(2):
                for q in range(2):
                    idx = len(amarks)

                    @self.sec("T")
                    def _(t, j=j, g2=g2, q=q, idx=idx):
                        if idx == 0:
                            dS.wait(t, xnT_mark)
                            dG.wait(t, w_mark)
                        if idx >= 2:
                            act.wait(t, amarks[idx - 2])
                        if g2 == 0 and q == 0 and j > 0:
                            dve.wait(t, scan_marks[-1])
                        base = g2 * 2048 + q * 1024
                        for kk in range(ND):
                            for cc in range(2):
                                rhs = xnT[:, kk * T + base + cc * 512:
                                          kk * T + base + (cc + 1) * 512]
                                _lt = t.matmul(pz[idx % 2][:, cc * 512:(cc + 1) * 512],
                                         wz_bf[:, kk * D + j * 128:kk * D + (j + 1) * 128],
                                         rhs, start=(kk == 0), stop=(kk == ND - 1))
                        for kk in range(ND):
                            for cc in range(2):
                                rhs = xnT[:, kk * T + base + cc * 512:
                                          kk * T + base + (cc + 1) * 512]
                                _lt = t.matmul(ph[idx % 2][:, cc * 512:(cc + 1) * 512],
                                         wh_bf[:, kk * D + j * 128:kk * D + (j + 1) * 128],
                                         rhs, start=(kk == 0), stop=(kk == ND - 1))
                        pe.inc(_lt)
                    mmm = pe.n

                    @self.sec("A")
                    def _(a, j=j, g2=g2, q=q, idx=idx, mmm=mmm):
                        pe.wait(a, mmm)
                        o = g2 * 2048 + q * 1024
                        a.activation(aT[:, o:o + 1024], pz[idx % 2][:, :], AF.Sigmoid,
                                     scale=-1.0, bias=c["negbz"][:, j:j + 1])
                        a.activation(zT[:, o:o + 1024], pz[idx % 2][:, :], AF.Sigmoid,
                                     bias=c["bz"][:, j:j + 1])
                        a.activation(htT[:, o:o + 1024], ph[idx % 2][:, :], AF.Identity,
                                     bias=c["bh"][:, j:j + 1])
                        a.drain()
                        act.inc(a.nop())
                    amarks.append(act.n)

                @self.sec("V")
                def _(v, j=j, g2=g2):
                    act.wait(v)
                    o = g2 * 2048
                    v.tensor_tensor(bT[:, o:o + 2048], zT[:, o:o + 2048],
                                            htT[:, o:o + 2048], op=ALU.mult)
                    v.drain()
                    dve.inc(v.nop())

            @self.sec("V")
            def _(v, j=j):
                v.drain()
                v.tensor_tensor_scan(hT[:], aT[:], bT[:], 0.0,
                                     op0=ALU.mult, op1=ALU.add)
                v.drain()
                v.nop()
                v.drain()
                dve.inc(v.nop())
            scan_marks.append(dve.n)

            @self.sec("S")
            def _(sy, j=j):
                dve.wait(sy, scan_marks[-1])
                dS.dma(sy.dma_start(out=ap["h_h"][j * 128:(j + 1) * 128, :], in_=hT[:]))

    def st_diffusion(self):
        import os
        nc, ap, c = self.nc, self.ap, self.c
        dS, pe, dve = self.sem["dS"], self.sem["pe"], self.sem["dve"]
        hf = [self.sbuf(f"df_hf{i}", [128, T]) for i in range(3)]
        hb = [self.sbuf(f"df_hb{i}", [128, T], BF16) for i in range(3)]
        hdT = self.sbuf("df_hdT", [128, T])
        tr_sb = self.sbuf("df_tr", [128, NT * 128])
        pn = [self.psum(f"df_pn{i}", [128, 512]) for i in range(2)]
        pt = [self.psum(f"df_pt{i}", [128, 512]) for i in range(2)]
        gevac = []
        gtc = []
        self.barrier("S")

        def slot(jj):
            return jj % 3

        load_marks = {}
        cast_marks = {}
        last_evac = [0]

        def emit_load(jj):
            dve_now, pe_now = dve.n, pe.n

            @self.sec("S")
            def _(sy, jj=jj, dve_now=dve_now):
                dve.wait(sy, dve_now)
                dS.dma(sy.dma_start(out=hf[slot(jj)][:],
                                    in_=ap["h_h"][(jj % ND) * 128:((jj % ND) + 1) * 128, :]))
            load_marks[jj] = dS.n

            @self.sec("V")
            def _(v, jj=jj, pe_now=pe_now):
                dS.wait(v, load_marks[jj])
                pe.wait(v, pe_now)
                v.tensor_copy(hb[slot(jj)][:], hf[slot(jj)][:])
                v.drain()
                dve.inc(v.nop())
            cast_marks[jj] = dve.n

        emit_load(-1)
        emit_load(0)
        emit_load(1)
        for j in range(ND):
            if j + 1 <= ND:
                pass
            jp, jn = j - 1, j + 1
            if jn == ND:
                jn = 0
            if (j + 1) < ND and (j + 1) not in load_marks:
                emit_load(j + 1)
            if j + 1 == ND and 8 not in load_marks:
                emit_load(8)

            jp_key = j - 1
            jn_key = j + 1
            evac_marks = []
            import os
            if int(os.environ.get("DF_PART", "3")) < 1:
                continue
            for cc in range(ND):
                gidx = len(gevac)

                @self.sec("T")
                def _(t, j=j, cc=cc, gidx=gidx, jp_key=jp_key, jn_key=jn_key):
                    for k in (j, jp_key, jn_key):
                        dve.wait(t, cast_marks[k])
                    if gidx >= 2:
                        dve.wait(t, gevac[gidx - 2])
                    ps = pn[gidx % 2]
                    sl = slice(cc * 512, (cc + 1) * 512)
                    _lt = t.matmul(ps[:, :], c["shiftc"][:], hb[slot(j)][:, sl],
                             start=True, stop=False)
                    _lt = t.matmul(ps[:, :], c["corner_lo"][:], hb[slot(jp_key)][:, sl],
                             start=False, stop=False)
                    _lt = t.matmul(ps[:, :], c["corner_hi"][:], hb[slot(jn_key)][:, sl],
                             start=False, stop=True)
                    pe.inc(_lt)
                mmk = pe.n

                @self.sec("V")
                def _(v, j=j, cc=cc, gidx=gidx, mmk=mmk):
                    pe.wait(v, mmk)
                    sl = slice(cc * 512, (cc + 1) * 512)
                    v.tensor_scalar(hdT[:, sl], hf[slot(j)][:, sl],
                                    c["one_m2d"][:, j:j + 1], None,
                                    op0=ALU.mult)
                    v.scalar_tensor_tensor(hdT[:, sl], pn[gidx % 2][:, :],
                                           c["dcoef"][:, j:j + 1], hdT[:, sl],
                                           op0=ALU.mult, op1=ALU.add)
                    v.drain()
                    dve.inc(v.nop())
                evac_marks.append(dve.n)
                gevac.append(dve.n)
                last_evac[0] = dve.n

            import os
            _dfpart = int(os.environ.get("DF_PART", "3"))
            if _dfpart < 2:
                continue
            tcopy_marks = []
            for gqs in range(ND):
                tidx = len(gtc)

                @self.sec("T")
                def _(t, gq=gqs, tidx=tidx):
                    dve.wait(t, evac_marks[-1])
                    if tidx >= 2:
                        dve.wait(t, gtc[tidx - 2])
                    for q in range(4):
                        a = gq * 4 + q
                        _lt = t.transpose(pt[tidx % 2][:, q * 128:(q + 1) * 128],
                                    hdT[:, a * 128:(a + 1) * 128], c["ident_f"][:])
                    pe.inc(_lt)
                trk = pe.n

                @self.sec("V")
                def _(v, gq=gqs, tidx=tidx, trk=trk):
                    pe.wait(v, trk)
                    v.tensor_copy(tr_sb[:, gq * 512:(gq + 1) * 512], pt[tidx % 2][:, :])
                    v.drain()
                    dve.inc(v.nop())
                tcopy_marks.append(dve.n)
                gtc.append(dve.n)

            if _dfpart < 3:
                continue

            @self.sec("S")
            def _(sy, j=j):
                dve.wait(sy, tcopy_marks[-1])
                dS.dma(sy.dma_start(
                    out=ap["hd_h"].rearrange("(a p) d -> p a d", p=128)[:, :, j * 128:(j + 1) * 128],
                    in_=tr_sb[:].rearrange("p (a d) -> p a d", d=128)))

    def st_resid_ln(self):
        ap, c = self.ap, self.c
        dS, dve, act = self.sem["dS"], self.sem["dve"], self.sem["act"]
        NB = 4
        xb = self.sbuf("rl_x", [128, NB * D])
        hdb = self.sbuf("rl_hd", [128, NB * D])
        r1b = self.sbuf("rl_r1", [128, NB * D])
        r2b = self.sbuf("rl_r2", [128, NB * D])
        xpb = self.sbuf("rl_xp", [128, NB * D])
        h1b = self.sbuf("rl_h1", [128, NB * D], BF16)
        scr = self.sbuf("rl_scr", [128, D])
        sq = self.sbuf("rl_sq", [128, NB])
        sm = self.sbuf("rl_sm", [128, NB])
        m_ = self.sbuf("rl_m", [128, NB])
        v_ = self.sbuf("rl_v", [128, NB])
        rstd = self.sbuf("rl_rstd", [128, NB])
        l1_ = self.sbuf("rl_l1", [128, NB])
        l2_ = self.sbuf("rl_l2", [128, NB])
        num = self.sbuf("rl_num", [128, NB])
        Sc = self.sbuf("rl_S", [128, NB])
        A1 = self.sbuf("rl_A1", [128, NB])
        Bv = self.sbuf("rl_Bv", [128, NB])
        mS = self.sbuf("rl_mS", [128, NB])
        tmp = self.sbuf("rl_tmp", [128, NB])
        self.barrier("S")

        store_mark = [0]
        for s in range(NT // NB):
            rows = slice(s * NB * 128, (s + 1) * NB * 128)

            @self.sec("S")
            def _(sy, s=s, rows=rows):
                if s > 0:
                    dS.wait(sy, store_mark[0])
                dS.dma(sy.dma_start(
                    out=xb[:].rearrange("p (a d) -> p a d", d=D),
                    in_=ap["x"][rows, :].rearrange("(a p) d -> p a d", p=128)))
                dS.dma(sy.dma_start(
                    out=hdb[:].rearrange("p (a d) -> p a d", d=D),
                    in_=ap["hd_h"][rows, :].rearrange("(a p) d -> p a d", p=128)))
            ld_mark = dS.n

            @self.sec("V")
            def _(v, s=s):
                dS.wait(v, ld_mark)
                act.wait(v)
                v.tensor_tensor(r1b[:], xb[:], hdb[:], op=ALU.add)
                for i in range(NB):
                    v.reduce_sum(sm[:, i:i + 1], r1b[:, i * D:(i + 1) * D], axis=AX.X)
                v.drain()
                v.nop()
                v.drain()
                dve.inc(v.nop())
            r1_mark = dve.n

            @self.sec("A")
            def _(a, s=s):
                dve.wait(a, r1_mark)
                for i in range(NB):
                    a.activation(scr[:], r1b[:, i * D:(i + 1) * D], AF.Square,
                                 accum_out=sq[:, i:i + 1])
                a.drain()
                a.nop()
                a.drain()
                act.inc(a.nop())
            sq_mark = act.n

            @self.sec("V")
            def _(v, s=s):
                act.wait(v, sq_mark)
                v.tensor_scalar(m_[:], sm[:], 1.0 / D, None, op0=ALU.mult)
                v.drain()
                v.tensor_tensor(tmp[:], m_[:], m_[:], op=ALU.mult)
                v.tensor_scalar(v_[:], sq[:], 1.0 / D, None, op0=ALU.mult)
                v.drain()
                v.tensor_tensor(v_[:], v_[:], tmp[:], op=ALU.subtract)
                v.drain()
                v.tensor_scalar(num[:], v_[:], 1.0 + EPS, EPS + EPS * EPS,
                                        op0=ALU.mult, op1=ALU.add)
                v.drain()
                dve.inc(v.nop())
            var_mark = dve.n

            @self.sec("A")
            def _(a, s=s):
                dve.wait(a, var_mark)
                a.activation(l1_[:], v_[:], AF.Ln, bias=c["epscol"][:, 0:1])
                a.activation(l2_[:], num[:], AF.Ln)
                a.activation(rstd[:], l1_[:], AF.Exp, scale=-0.5)
                a.drain()
                act.inc(a.nop())
            ln_mark = act.n

            @self.sec("V")
            def _(v, s=s):
                act.wait(v, ln_mark)
                v.tensor_tensor(tmp[:], l2_[:], l1_[:], op=ALU.subtract)
                v.drain()
                v.nop()
                v.drain()
                dve.inc(v.nop())

            @self.sec("A")
            def _(a, s=s):
                dve.wait(a)
                a.activation(tmp[:], tmp[:], AF.Exp, scale=-0.5)
                a.drain()
                act.inc(a.nop())

            @self.sec("V")
            def _(v, s=s):
                act.wait(v)
                v.tensor_tensor(Sc[:], rstd[:], tmp[:], op=ALU.mult)
                v.tensor_scalar(A1[:], rstd[:], 1.0, None, op0=ALU.add)
                v.tensor_tensor(Bv[:], m_[:], rstd[:], op=ALU.mult)
                v.drain()
                v.tensor_tensor(mS[:], m_[:], Sc[:], op=ALU.mult)
                v.drain()
                for i in range(NB):
                    sl = slice(i * D, (i + 1) * D)
                    v.tensor_scalar(r2b[:, sl], r1b[:, sl], A1[:, i:i + 1],
                                    Bv[:, i:i + 1], op0=ALU.mult, op1=ALU.subtract)
                    v.tensor_scalar(xpb[:, sl], r1b[:, sl], rstd[:, i:i + 1],
                                    Bv[:, i:i + 1], op0=ALU.mult, op1=ALU.subtract)
                    v.tensor_scalar(h1b[:, sl], r1b[:, sl], Sc[:, i:i + 1],
                                    mS[:, i:i + 1], op0=ALU.mult, op1=ALU.subtract)
                v.drain()
                v.nop()
                v.drain()
                dve.inc(v.nop())
            out_mark = dve.n

            @self.sec("S")
            def _(sy, s=s, rows=rows):
                dve.wait(sy, out_mark)
                dS.dma(sy.dma_start(
                    out=ap["r2_h"][rows, :].rearrange("(a p) d -> p a d", p=128),
                    in_=r2b[:].rearrange("p (a d) -> p a d", d=D)))
                dS.dma(sy.dma_start(
                    out=ap["xp_h"][rows, :].rearrange("(a p) d -> p a d", p=128),
                    in_=xpb[:].rearrange("p (a d) -> p a d", d=D)))
                dS.dma(sy.dma_start(
                    out=ap["h1_h"][rows, :].rearrange("(a p) d -> p a d", p=128),
                    in_=h1b[:].rearrange("p (a d) -> p a d", d=D)))
            store_mark[0] = dS.n

    def st_qkv(self):
        nc, ap, c = self.nc, self.ap, self.c
        dS, dG, pe, dve, act = (self.sem[k] for k in ["dS", "dG", "pe", "dve", "act"])
        c["cosrep"] = self.sbuf("rp_cos", [128, T], BF16)
        c["sinrep"] = self.sbuf("rp_sin", [128, T], BF16)
        wqk = self.sbuf("qk_wqk", [128, ND * 2048], BF16)
        wv = self.sbuf("qk_wv", [128, ND * D], BF16)
        biasqk = self.sbuf("qk_bias", [128, 16])
        bvrow = self.sbuf("qk_bvrow", [1, D])
        bvrep = self.sbuf("qk_bvrep", [128, D])
        qa = self.sbuf("qk_qa", [128, T], BF16)
        qb = self.sbuf("qk_qb", [128, T], BF16)
        ra = self.sbuf("qk_ra", [128, T], BF16)
        rb = self.sbuf("qk_rb", [128, T], BF16)
        tt = self.sbuf("qk_tt", [128, T], BF16)
        vblk = self.sbuf("qk_vblk", [128, 4 * D], BF16)
        pq = self.psum("qk_pq", [128, 512])
        pq2 = self.psum("qk_pq2", [128, 512])
        pb = self.psum("qk_pb", [128, 16])
        pr = self.psum("qk_pr", [1, 512])
        self._qkv_tmp = contextlib.ExitStack()
        _tmp = lambda nm, shape, dt=F32: self._qkv_tmp.enter_context(
            self.nc.sbuf_tensor(nm, shape, dt))
        wq_f = [_tmp(f"qk_wqf{s}", [128, 3 * D]) for s in range(2)]
        rp_ang = _tmp("rp_ang", [128, T])
        rp_iot = _tmp("rp_iot", [128, T])
        rp_kq = _tmp("rp_kq", [128, T])
        rp_invf = _tmp("rp_invf", [128, 1])
        self._mk_rope_tables(rp_ang, rp_iot, rp_kq, rp_invf)
        self.barrier("S", "G")

        wqf_vmarks = {}
        for kk in range(ND):
            @self.sec("S")
            def _(sy, kk=kk):
                if kk >= 2:
                    dve.wait(sy, wqf_vmarks[kk - 2])
                dS.dma(sy.dma_start(out=wq_f[kk % 2][:],
                                    in_=ap["wqkv"][kk * 128:(kk + 1) * 128, :]))
            ldm = dS.n

            @self.sec("V")
            def _(v, kk=kk, ldm=ldm):
                dS.wait(v, ldm)
                wview = wq_f[kk % 2][:].rearrange("p (h c) -> p h c", h=16)
                for mt in range(4):
                    v.tensor_scalar(
                        wqk[:, kk * 2048 + mt * 128:kk * 2048 + (mt + 1) * 128]
                        .rearrange("p (a b) -> p a b", b=32),
                        wview[:, 4 * mt:4 * mt + 4, 0:32],
                        c["ln1g"][:, kk:kk + 1], None, op0=ALU.mult)
                    v.tensor_scalar(
                        wqk[:, kk * 2048 + 512 + mt * 128:kk * 2048 + 512 + (mt + 1) * 128]
                        .rearrange("p (a b) -> p a b", b=32),
                        wview[:, 4 * mt:4 * mt + 4, 32:64],
                        c["ln1g"][:, kk:kk + 1], None, op0=ALU.mult)
                    v.tensor_scalar(
                        wqk[:, kk * 2048 + 1024 + mt * 128:kk * 2048 + 1024 + (mt + 1) * 128]
                        .rearrange("p (a b) -> p a b", b=32),
                        wview[:, 4 * mt:4 * mt + 4, 64:96],
                        c["ln1g"][:, kk:kk + 1], None, op0=ALU.mult)
                    v.tensor_scalar(
                        wqk[:, kk * 2048 + 1536 + mt * 128:kk * 2048 + 1536 + (mt + 1) * 128]
                        .rearrange("p (a b) -> p a b", b=32),
                        wview[:, 4 * mt:4 * mt + 4, 96:128],
                        c["ln1g"][:, kk:kk + 1], None, op0=ALU.mult)
                v.tensor_scalar(
                    wv[:, kk * D:(kk + 1) * D].rearrange("p (a b) -> p a b", b=64),
                    wview[:, :, 128:192],
                    c["ln1g"][:, kk:kk + 1], None, op0=ALU.mult)
                v.drain()
                v.nop()
                v.drain()
                dve.inc(v.nop())
            wqf_vmarks[kk] = dve.n
        wbf_mark = dve.n

        @self.sec("T")
        def _(t):
            dve.wait(t, wbf_mark)
            dS.wait(t)
            for mt in range(16):
                for kk in range(ND):
                    _lt = t.matmul(pb[:, mt:mt + 1],
                             wqk[:, kk * 2048 + mt * 128:kk * 2048 + (mt + 1) * 128],
                             c["ln1b_bf"][:, kk:kk + 1],
                             start=(kk == 0), stop=(kk == ND - 1))
            pe.inc(_lt)
        bias_mark = pe.n

        @self.sec("V")
        def _(v):
            pe.wait(v, bias_mark)
            v.tensor_copy(biasqk[:], pb[:, :])
            v.drain()
            dve.inc(v.nop())

        for ch in range(2):
            @self.sec("T")
            def _(t, ch=ch):
                dve.wait(t)
                for kk in range(ND):
                    _lt = t.matmul(pr[:, :], c["ln1b_bf"][:, kk:kk + 1],
                             wv[:, kk * D + ch * 512:kk * D + (ch + 1) * 512],
                             start=(kk == 0), stop=(kk == ND - 1))
                pe.inc(_lt)

            @self.sec("V")
            def _(v, ch=ch):
                pe.wait(v)
                v.tensor_copy(bvrow[:, ch * 512:(ch + 1) * 512], pr[:, :])
                v.drain()
                dve.inc(v.nop())

        for ch in range(2):
            @self.sec("T")
            def _(t, ch=ch):
                dve.wait(t)
                pe.inc(t.matmul(pq2[:, :], c["ones_row1f"][:, :],
                                bvrow[:, ch * 512:(ch + 1) * 512], start=True, stop=True))

            @self.sec("V")
            def _(v, ch=ch):
                pe.wait(v)
                v.tensor_copy(bvrep[:, ch * 512:(ch + 1) * 512], pq2[:, :])
                v.drain()
                dve.inc(v.nop())
        prep_mark = dve.n

        self.barrier("T", "V", "A", "G", "S")
        self._qkv_tmp.close()
        h1T = self.sbuf("qk_h1T", [128, ND * T], BF16)

        @self.sec("S")
        def _(sy):
            for kk in range(ND):
                dS.dma(sy.dma_start(out=h1T[:, kk * T:(kk + 1) * T],
                                    in_=ap["h1_h"][:, kk * 128:(kk + 1) * 128],
                                    transpose=True))
        h1T_mark = dS.n

        pairs = [(i, i + 4) for i in range(4)] + [(8 + i, 12 + i) for i in range(4)]
        prev_rope = [0]
        prev_store = [0]
        for (ma, mb) in pairs:
            for which, mt, dstq in (("a", ma, qa), ("b", mb, qb)):
                for ccg in range(4):
                    @self.sec("T")
                    def _(t, mt=mt, ccg=ccg):
                        dve.wait(t, prep_mark)
                        dS.wait(t, h1T_mark)
                        act.wait(t)
                        for half in range(2):
                            cc = ccg * 2 + half
                            ps = pq if half == 0 else pq2
                            for kk in range(ND):
                                _lt = t.matmul(ps[:, :],
                                         wqk[:, kk * 2048 + mt * 128:kk * 2048 + (mt + 1) * 128],
                                         h1T[:, kk * T + cc * 512:kk * T + (cc + 1) * 512],
                                         start=(kk == 0), stop=(kk == ND - 1))
                        pe.inc(_lt)

                    @self.sec("A")
                    def _(a, mt=mt, ccg=ccg, dstq=dstq):
                        pe.wait(a)
                        dve.wait(a, prev_rope[0])
                        for half in range(2):
                            cc = ccg * 2 + half
                            ps = pq if half == 0 else pq2
                            a.activation(dstq[:, cc * 512:(cc + 1) * 512], ps[:, :],
                                         AF.Identity, bias=biasqk[:, mt:mt + 1])
                        a.drain()
                        a.nop()
                        a.drain()
                        act.inc(a.nop())
            gemm_mark = act.n

            @self.sec("V")
            def _(v, ma=ma, mb=mb):
                act.wait(v, gemm_mark)
                dS.wait(v, prev_store[0])
                v.tensor_tensor(tt[:], qa[:], c["cosrep"][:], op=ALU.mult)
                v.scalar_tensor_tensor(ra[:], qb[:], -1.0, c["sinrep"][:],
                                       op0=ALU.mult, op1=ALU.mult)
                v.tensor_tensor(ra[:], ra[:], tt[:], op=ALU.add)
                v.drain()
                v.tensor_tensor(tt[:], qa[:], c["sinrep"][:], op=ALU.mult)
                v.tensor_tensor(rb[:], qb[:], c["cosrep"][:], op=ALU.mult)
                v.tensor_tensor(rb[:], rb[:], tt[:], op=ALU.add)
                v.drain()
                v.nop()
                v.drain()
                dve.inc(v.nop())
            rope_mark = dve.n
            prev_rope[0] = rope_mark

            @self.sec("S")
            def _(sy, ma=ma, mb=mb, rope_mark=rope_mark):
                dve.wait(sy, rope_mark)
                dS.dma(sy.dma_start(out=ap["qk_h"][ma * 128:(ma + 1) * 128, :], in_=ra[:]))
                dS.dma(sy.dma_start(out=ap["qk_h"][mb * 128:(mb + 1) * 128, :], in_=rb[:]))
            prev_store[0] = dS.n

        st_mark = [dS.n]
        for blk in range(8):
            for sub in range(4):
                a = blk * 4 + sub

                @self.sec("T")
                def _(t, a=a, sub=sub):
                    dve.wait(t)
                    if sub == 0:
                        dS.wait(t, st_mark[0])
                    for half in range(2):
                        ps = pq if half == 0 else pq2
                        for kk in range(ND):
                            _lt = t.matmul(ps[:, :],
                                     h1T[:, kk * T + a * 128:kk * T + (a + 1) * 128],
                                     wv[:, kk * D + half * 512:kk * D + (half + 1) * 512],
                                     start=(kk == 0), stop=(kk == ND - 1))
                    pe.inc(_lt)

                @self.sec("V")
                def _(v, a=a, sub=sub):
                    pe.wait(v)
                    for half in range(2):
                        ps = pq if half == 0 else pq2
                        sl = slice(sub * D + half * 512, sub * D + (half + 1) * 512)
                        v.scalar_tensor_tensor(vblk[:, sl], ps[:, :], 1.0,
                                               bvrep[:, half * 512:(half + 1) * 512],
                                               op0=ALU.mult, op1=ALU.add)
                    v.drain()
                    v.nop()
                    v.drain()
                    dve.inc(v.nop())
            vb_mark = dve.n

            @self.sec("S")
            def _(sy, blk=blk, vb_mark=vb_mark):
                dve.wait(sy, vb_mark)
                rows = slice(blk * 512, (blk + 1) * 512)
                dS.dma(sy.dma_start(
                    out=ap["v_h"][rows, :].rearrange("(a p) d -> p a d", p=128),
                    in_=vblk[:].rearrange("p (a d) -> p a d", d=D)))
            st_mark[0] = dS.n

    def st_attention(self):
        nc, ap, c = self.nc, self.ap, self.c
        dS, dG, pe, dve, act = (self.sem[k] for k in ["dS", "dG", "pe", "dve", "act"])
        self._attn_es = contextlib.ExitStack()
        attnT = self._attn_es.enter_context(self.nc.sbuf_tensor("at_attnT", [128, ND * T], BF16))
        qh2 = [self.sbuf(f"at_q{s}", [128, T], BF16) for s in range(2)]
        kh2 = [self.sbuf(f"at_k{s}", [128, T], BF16) for s in range(2)]
        vh2 = [self.sbuf(f"at_v{s}", [128, NT * 65], BF16) for s in range(2)]
        pt_sb = [self.sbuf(f"at_pt{i}", [128, 512], BF16) for i in range(6)]
        ptm_sb = [self.sbuf(f"at_ptm{i}", [128, 512], BF16) for i in range(4)]
        lnrow = self.sbuf("at_ln", [1, 512])
        recrow = self.sbuf("at_rec", [1, 512])
        rb_sb = self.sbuf("at_rb", [64, 512])
        self._mk_cmask()
        pst = [self.psum(f"at_pst{i}", [128, 512]) for i in range(2)]
        pot = [self.psum(f"at_pot{i}", [128, 512]) for i in range(2)]
        prb = self.psum("at_prb", [64, 512])
        self.ap["_attnT"] = attnT
        self.barrier("S", "G")

        apply_marks = [0, 0]
        bc_hold = [0]
        exp_marks = {}
        pv_marks = {}
        ld_mark = [0]

        head_end_pe = {}
        @self.sec("G")
        def _(g):
            g.memset(vh2[0][:], 1.0)
            g.memset(vh2[1][:], 1.0)

        for h in range(H):
            qh, kh, vh = qh2[h % 2], kh2[h % 2], vh2[h % 2]

            @self.sec("G")
            def _(g, h=h):
                if h >= 2:
                    pe.wait(g, head_end_pe[h - 2])
                dG.dma(g.dma_start(
                    out=vh[:].rearrange("p (a w) -> p a w", w=65)[:, :, 0:64],
                    in_=ap["v_h"].rearrange("(a p) w -> p a w", p=128)[:, :, h * 64:(h + 1) * 64]))
            vh_mark = dG.n

            @self.sec("S")
            def _(sy, h=h):
                if h >= 2:
                    pe.wait(sy, head_end_pe[h - 2])
                for half in range(2):
                    hb = 64 * half
                    dS.dma(sy.dma_start(out=qh[hb:hb + 32, :],
                                        in_=ap["qk_h"][h * 32:(h + 1) * 32, :]))
                    dS.dma(sy.dma_start(out=qh[hb + 32:hb + 64, :],
                                        in_=ap["qk_h"][512 + h * 32:512 + (h + 1) * 32, :]))
                    dS.dma(sy.dma_start(out=kh[hb:hb + 32, :],
                                        in_=ap["qk_h"][1024 + h * 32:1024 + (h + 1) * 32, :]))
                    dS.dma(sy.dma_start(out=kh[hb + 32:hb + 64, :],
                                        in_=ap["qk_h"][1536 + h * 32:1536 + (h + 1) * 32, :]))
            qk_mark = dS.n

            for cch in range(8):
                pc = cch % 2
                njt = 4 * cch + 4
                exp_marks.clear()
                pv_marks.clear()
                act_base = act.n
                dve_base = dve.n
                smarks, pvmarks = {}, {}

                @self.sec("T")
                def _(t, h=h, cch=cch, pc=pc, njt=njt, act_base=act_base, dve_base=dve_base):
                    dG.wait(t, vh_mark)
                    dS.wait(t, qk_mark)

                    def emit_pv(jj):
                        act.wait(t, act_base + jj + 1)
                        if jj >= 4 * cch:
                            dve.wait(t, dve_base + (jj - 4 * cch) + 1)
                        if jj == 0 and apply_marks[pc]:
                            dve.wait(t, apply_marks[pc])
                        src_pt = ptm_sb[jj % 4] if jj >= 4 * cch else pt_sb[jj % 6]
                        _lt = t.matmul(pot[pc][0:65, :], vh[:, jj * 65:(jj + 1) * 65],
                                 src_pt[:, :],
                                 start=(jj == 0), stop=(jj == njt - 1))
                        pe.inc(_lt)
                        pvmarks[jj] = pe.n

                    for jp in range(njt // 2):
                        for half in range(2):
                            j = 2 * jp + half
                            if j >= 2:
                                act.wait(t, act_base + j - 1)
                            elif cch > 0:
                                act.wait(t, act_base)
                            hb = 64 * half
                            _lt = t.matmul(pst[half][:, :],
                                     kh[hb:hb + 64, j * 128:(j + 1) * 128],
                                     qh[hb:hb + 64, cch * 512:(cch + 1) * 512],
                                     start=True, stop=True,
                                     tile_position=(hb, 0))
                            pe.inc(_lt)
                            smarks[j] = pe.n
                        if jp >= 2:
                            emit_pv(2 * jp - 4)
                            emit_pv(2 * jp - 3)
                    for jj in range(max(0, njt - 4), njt):
                        emit_pv(jj)

                @self.sec("A")
                def _(a, h=h, cch=cch, pc=pc, njt=njt):
                    for j in range(njt):
                        sj = j % 2
                        pe.wait(a, smarks[j])
                        if j >= 6:
                            pe.wait(a, pvmarks[j - 6])
                        a.activation(pt_sb[j % 6][:, :], pst[sj][:, :], AF.Exp, scale=0.125)
                        a.drain()
                        act.inc(a.nop())
                    pe.wait(a, pvmarks[njt - 1])
                    pe.wait(a, bc_hold[0])
                    a.activation(lnrow[:, :], pot[pc][64:65, :], AF.Ln)
                    a.drain()
                    a.nop()
                    a.drain()
                    act.inc(a.nop())
                    a.activation(recrow[:, :], lnrow[:, :], AF.Exp, scale=-1.0)
                    a.drain()
                    a.nop()
                    a.drain()
                    act.inc(a.nop())
                tail_mark = act.n

                @self.sec("V")
                def _(v, h=h, cch=cch, njt=njt, act_base=act_base):
                    for idx, j in enumerate(range(4 * cch, njt)):
                        act.wait(v, act_base + j + 1)
                        v.tensor_tensor(ptm_sb[j % 4][:, :], pt_sb[j % 6][:, :],
                                        c["cmask"][:, (j - 4 * cch) * 512:(j - 4 * cch + 1) * 512],
                                        op=ALU.mult)
                        v.drain()
                        v.nop()
                        v.drain()
                        dve.inc(v.nop())

                @self.sec("T")
                def _(t, tail_mark=tail_mark):
                    act.wait(t, tail_mark)
                    dve.wait(t)
                    _lt = t.matmul(prb[0:64, :], c["ones_row1f"][0:1, 0:64], recrow[:, :],
                             start=True, stop=True)
                    pe.inc(_lt)
                bc_mark = pe.n
                bc_hold[0] = bc_mark

                @self.sec("V")
                def _(v, h=h, cch=cch, pc=pc, bc_mark=bc_mark):
                    pe.wait(v, bc_mark)
                    v.tensor_copy(rb_sb[:, :], prb[0:64, :])
                    kb = h // 2
                    off = (h % 2) * 64
                    dst = attnT[off:off + 64, kb * T + cch * 512:kb * T + (cch + 1) * 512]
                    v.tensor_tensor(dst, pot[pc][0:64, :], rb_sb[:, :], op=ALU.mult)
                    v.drain()
                    v.nop()
                    v.drain()
                    dve.inc(v.nop())
                apply_marks[pc] = dve.n
            head_end_pe[h] = pe.n

        if "attnT_dbg" in self.dbg:
            o = nc.dram_tensor("dbg_attnT", [128, ND * T], F32, kind="ExternalOutput").ap()
            att_f = self.sbuf("at_dbgf", [128, T])
            for kb in range(max(1, H // 2)):
                @self.sec("V")
                def _(v, kb=kb):
                    for s in self.sem.values():
                        s.wait(v)
                    v.tensor_copy(att_f[:], attnT[:, kb * T:(kb + 1) * T])
                    v.drain()
                    dve.inc(v.nop())

                @self.sec("S")
                def _(sy, kb=kb):
                    dve.wait(sy)
                    dS.dma(sy.dma_start(out=o[:, kb * T:(kb + 1) * T], in_=att_f[:]))

    def st_proj_r3(self):
        nc, ap, c = self.nc, self.ap, self.c
        dS, pe, dve, act = (self.sem[k] for k in ["dS", "pe", "dve", "act"])
        attnT = ap["_attnT"]
        wp = self.sbuf("pj_wp", [128, ND * D], BF16)
        pj_sb = [self.sbuf(f"pj_sb{s}", [128, 512]) for s in range(2)]
        r2b = self.sbuf("pj_r2", [128, 4 * D])
        r3b = self.sbuf("pj_r3", [128, 4 * D])
        xpb = self.sbuf("pj_xp", [128, 4 * D])
        xppb = self.sbuf("pj_xpp", [128, 4 * D])
        pp = [self.psum(f"pj_pp{s}", [128, 512]) for s in range(2)]
        ptr = [self.psum(f"pj_ptr{s}", [128, 512]) for s in range(2)]
        ev_marks, tr_marks, add_marks = {}, {}, {}
        self.barrier("S", "G")

        @self.sec("G")
        def _(g):
            for kk in range(ND):
                self.sem["dG"].dma(g.dma_start(out=wp[:, kk * D:(kk + 1) * D],
                                               in_=ap["wproj"][kk * 128:(kk + 1) * 128, :]))
        wp_mark = self.sem["dG"].n
        dG = self.sem["dG"]

        st_mark = [0]
        for cch in range(8):
            rows = slice(cch * 512, (cch + 1) * 512)

            @self.sec("S")
            def _(sy, cch=cch, rows=rows):
                if st_mark[0]:
                    dS.wait(sy, st_mark[0])
                dS.dma(sy.dma_start(
                    out=r2b[:].rearrange("p (a d) -> p a d", d=D),
                    in_=ap["r2_h"][rows, :].rearrange("(a p) d -> p a d", p=128)))
                dS.dma(sy.dma_start(
                    out=xpb[:].rearrange("p (a d) -> p a d", d=D),
                    in_=ap["xp_h"][rows, :].rearrange("(a p) d -> p a d", p=128)))
            r2_mark = dS.n

            for j in range(ND):
                gg = cch * ND + j

                @self.sec("T")
                def _(t, cch=cch, j=j, gg=gg):
                    dG.wait(t, wp_mark)
                    if gg >= 2:
                        act.wait(t, ev_marks[gg - 2])
                    for kk in range(ND):
                        _lt = t.matmul(pp[gg % 2][:, :],
                                 wp[:, kk * D + j * 128:kk * D + (j + 1) * 128],
                                 attnT[:, kk * T + cch * 512:kk * T + (cch + 1) * 512],
                                 start=(kk == 0), stop=(kk == ND - 1))
                    pe.inc(_lt)
                mm_mark = pe.n

                @self.sec("A")
                def _(a, gg=gg, mm_mark=mm_mark):
                    pe.wait(a, mm_mark)
                    if gg >= 2:
                        pe.wait(a, tr_marks[gg - 2])
                    a.activation(pj_sb[gg % 2][:], pp[gg % 2][:, :], AF.Copy)
                    a.drain()
                    act.inc(a.nop())
                ev_marks[gg] = act.n

                @self.sec("T")
                def _(t, gg=gg):
                    act.wait(t, ev_marks[gg])
                    if gg >= 2:
                        dve.wait(t, add_marks[gg - 2])
                    for q in range(4):
                        _lt = t.transpose(ptr[gg % 2][:, q * 128:(q + 1) * 128],
                                    pj_sb[gg % 2][:, q * 128:(q + 1) * 128],
                                    c["ident_f"][:])
                    pe.inc(_lt)
                tr_marks[gg] = pe.n

                @self.sec("V")
                def _(v, cch=cch, j=j, gg=gg, r2_mark=r2_mark):
                    pe.wait(v, tr_marks[gg])
                    dS.wait(v, r2_mark)
                    for q in range(4):
                        sl = slice(q * D + j * 128, q * D + (j + 1) * 128)
                        v.tensor_tensor(r3b[:, sl], ptr[gg % 2][:, q * 128:(q + 1) * 128],
                                        r2b[:, sl], op=ALU.add)
                        v.tensor_tensor(xppb[:, sl], ptr[gg % 2][:, q * 128:(q + 1) * 128],
                                        xpb[:, sl], op=ALU.add)
                    v.drain()
                    dve.inc(v.nop())
                add_marks[gg] = dve.n
            blk_mark = dve.n

            @self.sec("S")
            def _(sy, rows=rows, blk_mark=blk_mark):
                dve.wait(sy, blk_mark)
                dS.dma(sy.dma_start(
                    out=ap["r3"][rows, :].rearrange("(a p) d -> p a d", p=128),
                    in_=r3b[:].rearrange("p (a d) -> p a d", d=D)))
                dS.dma(sy.dma_start(
                    out=ap["xpp_h"][rows, :].rearrange("(a p) d -> p a d", p=128),
                    in_=xppb[:].rearrange("p (a d) -> p a d", d=D)))
            st_mark[0] = dS.n

    def _lz_tile(self, i, h2fi, h2T, lg_all, ph2, plg, h2_mark, lgcp_marks):
        c = self.c
        pe, dve, act, gp = (self.sem[k] for k in ["pe", "dve", "act", "gp"])

        @self.sec("T")
        def _(t):
            dve.wait(t, h2_mark)
            act.wait(t)
            for q in range(8):
                _lt = t.transpose(ph2[:, q * 128:(q + 1) * 128],
                                  h2fi[:, q * 128:(q + 1) * 128], c["ident_f"][:])
            pe.inc(_lt)
        tr1 = pe.n

        @self.sec("A")
        def _(a):
            pe.wait(a, tr1)
            a.activation(h2T[:, 0:1024], ph2[:, :], AF.Copy)
            a.drain()
            act.inc(a.nop())
        ev2 = act.n

        @self.sec("T")
        def _(t):
            act.wait(t, ev2)
            gp.wait(t)
            if i > 0:
                dve.wait(t, lgcp_marks[i - 1])
            for kk in range(ND):
                _lt = t.matmul(plg[:, :], h2T[:, kk * 128:(kk + 1) * 128],
                               c["wgate_sb"][:, kk * E:(kk + 1) * E],
                               start=(kk == 0), stop=(kk == ND - 1))
            pe.inc(_lt)
        lg_mark = pe.n

        @self.sec("V")
        def _(v):
            pe.wait(v, lg_mark)
            v.tensor_copy(lg_all[:, i * E:(i + 1) * E], plg[:, :])
            v.drain()
            dve.inc(v.nop())
        lgcp_marks[i] = dve.n

    def st_ln2_logits(self):
        nc, ap, c = self.nc, self.ap, self.c
        dS, pe, dve, act = (self.sem[k] for k in ["dS", "pe", "dve", "act"])
        x4 = self.sbuf("lz_x4", [128, 4 * D])
        h2f4 = self.sbuf("lz_h2f4", [128, 4 * D])
        h2b = self.sbuf("lz_h2b", [128, 4 * D], BF16)
        h2T = self.sbuf("lz_h2T", [128, D])
        lg_all = self.sbuf("lz_lg", [128, NT * E])
        scr = self.sbuf("lz_scr", [128, D])
        sq = self.sbuf("lz_sq", [128, 4])
        sm = self.sbuf("lz_sm", [128, 4])
        m_ = self.sbuf("lz_m", [128, 4])
        v_ = self.sbuf("lz_v", [128, 4])
        r_ = self.sbuf("lz_r", [128, 4])
        tmp = self.sbuf("lz_tmp", [128, 4])
        mk = self.sbuf("lz_mk", [128, NT * E], BF16)
        cnt_sb = self.sbuf("lz_cnt", [1, E])
        crow = self.sbuf("lz_crow", [1, NT * E])
        t2sc = self.mk_top2_scratch("lz", NT)
        self._mk_ln2_consts()
        ph2 = self.psum("lz_ph2", [128, 1024])
        plg = self.psum("lz_plg", [128, E])
        pcnt = self.psum("lz_pcnt", [1, NT * E])
        self.barrier("S", "G")

        @self.sec("G")
        def _(g):
            self.init_top2_iota(g, t2sc, NT)
            self.sem["gp"].inc(g.engine_nop())
        gp = self.sem["gp"]
        lgcp_marks = {}

        h2st_mark = [0]
        for s in range(NT // 4):
            rows4 = slice(s * 512, (s + 1) * 512)

            @self.sec("S")
            def _(sy, s=s):
                if s >= 1:
                    dve.wait(sy, lgcp_marks[4 * s - 1])
                dS.dma(sy.dma_start(
                    out=x4[:].rearrange("p (a d) -> p a d", d=D),
                    in_=ap["xpp_h"][rows4, :].rearrange("(a p) d -> p a d", p=128)))
            ld_mark = dS.n

            @self.sec("A")
            def _(a, s=s, ld_mark=ld_mark):
                dS.wait(a, ld_mark)
                for ii in range(4):
                    a.activation(scr[:], x4[:, ii * D:(ii + 1) * D], AF.Square,
                                 accum_out=sq[:, ii:ii + 1])
                a.drain()
                act.inc(a.nop())
            sq_mark = act.n

            @self.sec("V")
            def _(v, s=s, ld_mark=ld_mark, sq_mark=sq_mark):
                dS.wait(v, ld_mark)
                act.wait(v, sq_mark)
                for ii in range(4):
                    v.reduce_sum(sm[:, ii:ii + 1], x4[:, ii * D:(ii + 1) * D],
                                 axis=AX.X)
                v.drain()
                v.tensor_scalar(m_[:], sm[:], 1.0 / D, None, op0=ALU.mult)
                v.drain()
                v.tensor_tensor(tmp[:], m_[:], m_[:], op=ALU.mult)
                v.tensor_scalar(v_[:], sq[:], 1.0 / D, None, op0=ALU.mult)
                v.drain()
                v.tensor_tensor(v_[:], v_[:], tmp[:], op=ALU.subtract)
                v.drain()
                dve.inc(v.nop())
            var_mark = dve.n

            @self.sec("A")
            def _(a, var_mark=var_mark):
                dve.wait(a, var_mark)
                a.activation(r_[:], v_[:], AF.Ln, bias=c["epscol"][:, 0:1])
                a.drain()
                a.activation(r_[:], r_[:], AF.Exp, scale=-0.5)
                a.drain()
                act.inc(a.nop())
            r_mark = act.n

            @self.sec("V")
            def _(v, s=s, r_mark=r_mark):
                act.wait(v, r_mark)
                for ii in range(4):
                    sl = slice(ii * D, (ii + 1) * D)
                    v.tensor_scalar(h2f4[:, sl], x4[:, sl], m_[:, ii:ii + 1],
                                    r_[:, ii:ii + 1],
                                    op0=ALU.subtract, op1=ALU.mult)
                    v.tensor_tensor(h2f4[:, sl], h2f4[:, sl], c["g2rep"][:],
                                    op=ALU.mult)
                    v.tensor_tensor(h2f4[:, sl], h2f4[:, sl], c["b2rep"][:],
                                    op=ALU.add)
                if h2st_mark[0]:
                    dS.wait(v, h2st_mark[0])
                v.tensor_copy(h2b[:], h2f4[:])
                v.drain()
                dve.inc(v.nop())
            h2_mark = dve.n

            @self.sec("S")
            def _(sy, s=s, h2_mark=h2_mark):
                dve.wait(sy, h2_mark)
                dS.dma(sy.dma_start(
                    out=ap["h2"][rows4, :].rearrange("(a p) d -> p a d", p=128),
                    in_=h2b[:].rearrange("p (a d) -> p a d", d=D)))
            h2st_mark[0] = dS.n

            for i in range(4 * s, 4 * s + 4):
                self._lz_tile(i, h2f4[:, (i % 4) * D:(i % 4 + 1) * D],
                              h2T, lg_all, ph2, plg, h2_mark, lgcp_marks)

        @self.sec("V")
        def _(v):
            self._emit_top2(v, lg_all, NT, t2sc, mk)
            dve.inc(v.nop())
        mkv_mark = dve.n

        @self.sec("T")
        def _(t):
            dve.wait(t, mkv_mark)
            pe.inc(t.matmul(pcnt[:, :], c["ones_col_bf"][:, :], mk[:],
                            start=True, stop=True))
        pct_mark = pe.n

        @self.sec("V")
        def _(v):
            pe.wait(v, pct_mark)
            v.tensor_copy(crow[0:1, :], pcnt[0:1, :])
            v.drain()
            w = NT * E
            while w > E:
                h = w // 2
                v.tensor_tensor(crow[0:1, 0:h], crow[0:1, 0:h], crow[0:1, h:w],
                                op=ALU.add)
                v.drain()
                w = h
            v.tensor_copy(cnt_sb[:, :], crow[0:1, 0:E])
            v.drain()
            dve.inc(v.nop())
        cnt_mark = dve.n

        @self.sec("S")
        def _(sy):
            dve.wait(sy, cnt_mark)
            dS.dma(sy.dma_start(out=ap["counts"][:, :], in_=cnt_sb[:]))
            dS.dma(sy.dma_start(
                out=ap["logits"].rearrange("(a p) e -> p a e", p=128),
                in_=lg_all[:].rearrange("p (a e) -> p a e", e=E)))
            if "mask_my" in ap:
                dS.dma(sy.dma_start(
                    out=ap["mask_my"].rearrange("(a p) e -> p a e", p=128),
                    in_=mk[:].rearrange("p (a e) -> p a e", e=E)))





class L2(Builder):

    CH = [(0, 512), (512, 512), (1024, 512), (1536, 512)]
    RT_EBASE = SLOTB
    RT_CLAMP = NSLOT - 128
    RT_ADD_OFFS = False
    RT_KEEP_ONLY = False
    RT_ZERO = ("x_all", NSLOT, True)

    def __init__(self, nc, dbg=(), upto=None):
        super().__init__(nc, dbg)
        self.upto = upto
        self._declare()

    def _declare(self):
        self.din("r3", [T, D])
        self.din("h2", [T, D], BF16)
        self.din("logits", [T, E])
        self.din("counts_all", [E, E])
        self.din("batch_id", [1, 1])
        self.din("w1", [E, D, FF])
        self.din("b1", [E, FF])
        self.din("w2", [E, FF, D])
        self.din("b2", [E, D])
        self.dout("out", [T, D])
        self.dint("x_all", [NSLOT, D], BF16)
        self.dint("y_all", [NSLOT, D], BF16)
        self.dint("btab", [1, NT * E], I32)

    def build(self):
        self._start(["dS", "dG", "pe", "dve", "act", "gp"])
        stages = ["st_route", "st_dispatch", "st_experts", "st_combine"]
        if self.upto is not None:
            stages = stages[:stages.index(self.upto) + 1]
        with self.es:
            self.grank_all = self.es.enter_context(
                self.nc.sbuf_tensor("l2_grank", [128, NT * E], F32))
            self.gts_all = self.es.enter_context(
                self.nc.sbuf_tensor("l2_gts", [128, NT * E], F32))
            self.c["iotaF"] = self.es.enter_context(
                self.nc.sbuf_tensor("l2_iotaF", [128, 128], F32))
            self.c["identb"] = self.es.enter_context(
                self.nc.sbuf_tensor("l2_identb", [128, 128], BF16))
            self.tld_h = self.es.enter_context(self.nc.semaphore("tld"))
            self.tld_n = 0
            for st in stages:
                with self.stage():
                    getattr(self, st)()
            self.finale()
        return self.ap

    def st_route(self):
        nc, ap, c = self.nc, self.ap, self.c
        dS, dG, pe, dve, act, gp = (self.sem[k] for k in ["dS", "dG", "pe", "dve", "act", "gp"])
        lg_all = self.sbuf("rt_lg", [128, NT * E])
        cnts = self.sbuf("rt_cnts", [E, E])
        bid = self.sbuf("rt_bid", [1, 1])
        bidr = self.sbuf("rt_bidr", [E, 1])
        sel = self.sbuf("rt_sel", [E, 1])
        offs = self.sbuf("rt_offs", [128, E])
        iota8 = self.sbuf("rt_iota8", [128, E])
        iotaC = self.sbuf("rt_iotaC", [128, E])
        ebase = self.sbuf("rt_ebase", [128, E])
        iotaP = self.sbuf("rt_iotaP", [E, 1])
        m1 = self.sbuf("rt_m1", [128, 1])
        e1 = self.sbuf("rt_e1", [128, 1])
        eq1 = self.sbuf("rt_eq1", [128, E])
        eq2 = self.sbuf("rt_eq2", [128, E])
        lgm = self.sbuf("rt_lgm", [128, E])
        gts = self.sbuf("rt_gts", [128, E])
        sume = self.sbuf("rt_sume", [128, 1])
        negm = self.sbuf("rt_negm", [128, 1])
        maskc = self.sbuf("rt_maskc", [128, E], BF16)
        maskf = self.sbuf("rt_maskf", [128, E])
        keepm = self.sbuf("rt_keepm", [128, E])
        pos_ = self.sbuf("rt_pos", [128, E])
        t3 = self.sbuf("rt_t3", [128, E])
        t4 = self.sbuf("rt_t4", [128, E])
        kf = self.sbuf("rt_kf", [128, E])
        rel = self.sbuf("rt_rel", [128, E])
        tb = self.sbuf("rt_tb", [128, E])
        relbc = self.sbuf("rt_relbc", [128, E])
        btab_row = self.sbuf("rt_btrow", [1, NT * E])
        btab_i32 = self.sbuf("rt_bti", [1, NT * E], I32)
        if getattr(self, "RT_SPLIT_Y", False):
            ebA = self.sbuf("rt_ebA", [128, E])
            ebB = self.sbuf("rt_ebB", [128, E])
            selr = self.sbuf("rt_selr", [1, E])
            tra = self.sbuf("rt_tra", [1, E])
            trb = self.sbuf("rt_trb", [1, E])
        zt = self.sbuf("rt_z", [128, 2 * D], BF16)
        sc1 = self.sbuf("rt_sc1", [128, 1])
        ustrict = self.sbuf("rt_ustrict", [128, 128], BF16)
        lincl = self.sbuf("rt_lincl", [128, 128], BF16)
        _mk_ones = "ones_row1f" not in c
        if _mk_ones:
            c["ones_row1f"] = self.sbuf("rt_ones1f", [1, 128], F32)
        pcnt = self.psum("rt_pcnt", [128, E])
        poff = self.psum("rt_poff", [128, E])

        @self.sec("S")
        def _(sy):
            dS.dma(sy.dma_start(
                out=lg_all[:].rearrange("p (a e) -> p a e", e=E),
                in_=ap["logits"].rearrange("(a p) e -> p a e", p=128)))
            dS.dma(sy.dma_start(out=cnts[:], in_=ap["counts_all"][:, :]))
            dS.dma(sy.dma_start(out=bid[:], in_=ap["batch_id"][:, :]))
        ld_mark = dS.n

        @self.sec("G")
        def _(g):
            g.iota(iota8[:], pattern=[[1, E]], base=0, channel_multiplier=0,
                   allow_small_or_imprecise_dtypes=True)
            g.iota(iotaC[:], pattern=[[-1, E]], base=E, channel_multiplier=0,
                   allow_small_or_imprecise_dtypes=True)
            g.iota(ebase[:], pattern=[[self.RT_EBASE, E]], base=0, channel_multiplier=0,
                   allow_small_or_imprecise_dtypes=True)
            if getattr(self, "RT_SPLIT_Y", False):
                g.iota(ebA[:], pattern=[[YSPLIT, E]], base=0, channel_multiplier=0,
                       allow_small_or_imprecise_dtypes=True)
                g.iota(ebB[:], pattern=[[YBROWS, E]], base=YAOFF - (YSPLIT - 128),
                       channel_multiplier=0, allow_small_or_imprecise_dtypes=True)
            g.iota(iotaP[:], pattern=[[0, 1]], base=0, channel_multiplier=1,
                   allow_small_or_imprecise_dtypes=True)
            g.memset(ustrict[:], 0.0)
            g.affine_select(out=ustrict[:], in_=ustrict[:], compare_op=ALU.is_ge,
                            fill=1.0, base=0, pattern=[[-1, 128]], channel_multiplier=1)
            if _mk_ones:
                g.memset(c["ones_row1f"][:], 1.0)
            g.memset(lincl[:], 0.0)
            g.affine_select(out=lincl[:], in_=lincl[:], compare_op=ALU.is_ge,
                            fill=1.0, base=-1, pattern=[[1, 128]], channel_multiplier=-1)
            g.iota(c["iotaF"][:], pattern=[[1, 128]], base=0, channel_multiplier=0,
                   allow_small_or_imprecise_dtypes=True)
            make_identity(self.nc, g, c["identb"][:])
            g.memset(zt[:], 0.0)
            gp.inc(g.engine_nop())

        zname, zrows, zpad = self.RT_ZERO

        @self.sec("S")
        def _(sy):
            gp.wait(sy)
            for r in range(0, (zrows // 256) * 256, 256):
                dS.dma(sy.dma_start(
                    out=ap[zname][r:r + 256, :].rearrange("(a p) d -> p a d", p=128),
                    in_=zt[:].rearrange("p (a d) -> p a d", d=D)))
            if zrows % 256:
                dS.dma(sy.dma_start(out=ap[zname][(zrows // 256) * 256:zrows, :],
                                    in_=zt[:, 0:D]))
            if zpad:
                dS.dma(sy.dma_start(out=ap["y_all"][E * SLOTB:NSLOT, :], in_=zt[:, 0:D]))

        pb1 = self.psum("rt_pb1", [E, 1])

        @self.sec("T")
        def _(t):
            dS.wait(t, ld_mark)
            gp.wait(t)
            pe.inc(t.matmul(pb1[0:E, :], c["ones_row1f"][0:1, 0:E], bid[:, :],
                            start=True, stop=True))
        bid_mark = pe.n

        @self.sec("V")
        def _(v):
            pe.wait(v, bid_mark)
            v.tensor_copy(bidr[:], pb1[0:E, :])
            v.drain()
            v.tensor_scalar(sel[:], iotaP[:], bidr[:, 0:1], None, op0=ALU.is_lt)
            v.drain()
            dve.inc(v.nop())
        sel_mark = dve.n

        @self.sec("T")
        def _(t):
            dve.wait(t, sel_mark)
            pe.inc(t.matmul(poff[0:1, :], sel[:, :], cnts[:, :], start=True, stop=True))
        offr_mark = pe.n

        pb2 = self.psum("rt_pb2", [128, E])

        @self.sec("V")
        def _(v):
            pe.wait(v, offr_mark)
            v.tensor_copy(offs[0:1, :], poff[0:1, :])
            v.drain()
            dve.inc(v.nop())
        offrow_mark = dve.n

        @self.sec("T")
        def _(t):
            dve.wait(t, offrow_mark)
            pe.inc(t.matmul(pb2[:, :], c["ones_row1f"][:, :], offs[0:1, :],
                            start=True, stop=True))
        offb_mark = pe.n

        @self.sec("V")
        def _(v):
            pe.wait(v, offb_mark)
            v.tensor_copy(offs[:], pb2[:, :])
            v.drain()
            dve.inc(v.nop())
        offs_mark = dve.n

        import os
        if os.environ.get("RT_CUT") == "1":
            return
        prev_rel_mark = [0]
        for i in range(NT):
            @self.sec("V")
            def _(v, i=i):
                if i == 0:
                    dve.wait(v, offs_mark)
                lg = lg_all[:, i * E:(i + 1) * E]
                v.reduce_max(m1[:], lg, axis=AX.X)
                v.drain()
                v.tensor_scalar(negm[:], m1[:], -1.0, None, op0=ALU.mult)
                v.drain()
                dve.inc(v.nop())
            nm_mark = dve.n

            @self.sec("A")
            def _(a, i=i, nm_mark=nm_mark):
                dve.wait(a, nm_mark)
                lg = lg_all[:, i * E:(i + 1) * E]
                a.activation(gts[:], lg, AF.Exp, bias=negm[:, 0:1], accum_out=sume[:])
                a.drain()
                act.inc(a.nop())
            exp_mark = act.n

            @self.sec("V")
            def _(v, i=i, exp_mark=exp_mark):
                act.wait(v, exp_mark)
                lg = lg_all[:, i * E:(i + 1) * E]
                v.reciprocal(sc1[:], sume[:])
                v.drain()
                v.tensor_scalar(gts[:], gts[:], sc1[:, 0:1], None, op0=ALU.mult)
                v.tensor_scalar(eq1[:], lg, m1[:, 0:1], None, op0=ALU.is_equal)
                v.drain()
                v.tensor_tensor(t3[:], eq1[:], iotaC[:], op=ALU.mult)
                v.drain()
                v.reduce_max(e1[:], t3[:], axis=AX.X)
                v.drain()
                v.tensor_scalar(e1[:], e1[:], -1.0, float(E), op0=ALU.mult, op1=ALU.add)
                v.drain()
                v.tensor_scalar(eq1[:], iota8[:], e1[:, 0:1], None, op0=ALU.is_equal)
                v.drain()
                v.scalar_tensor_tensor(lgm[:], eq1[:], -1e9, lg, op0=ALU.mult, op1=ALU.add)
                v.drain()
                v.reduce_max(m1[:], lgm[:], axis=AX.X)
                v.drain()
                v.tensor_scalar(eq2[:], lgm[:], m1[:, 0:1], None, op0=ALU.is_equal)
                v.drain()
                v.tensor_tensor(t3[:], eq2[:], iotaC[:], op=ALU.mult)
                v.drain()
                v.reduce_max(e1[:], t3[:], axis=AX.X)
                v.drain()
                v.tensor_scalar(e1[:], e1[:], -1.0, float(E), op0=ALU.mult, op1=ALU.add)
                v.drain()
                v.tensor_scalar(eq2[:], iota8[:], e1[:, 0:1], None, op0=ALU.is_equal)
                v.drain()
                v.tensor_tensor(maskf[:], eq1[:], eq2[:], op=ALU.add)
                v.drain()
                v.tensor_copy(maskc[:], maskf[:])
                v.drain()
                dve.inc(v.nop())
            mk_mark = dve.n

            @self.sec("T")
            def _(t, i=i, mk_mark=mk_mark):
                dve.wait(t, mk_mark)
                pe.inc(t.matmul(pcnt[:, :], ustrict[:], maskc[:],
                                start=(i == 0), stop=False, skip_group_check=True))
            pos_mark = pe.n

            @self.sec("V")
            def _(v, i=i, pos_mark=pos_mark):
                pe.wait(v, pos_mark)
                v.tensor_copy(pos_[:], pcnt[:, :])
                v.drain()
                dve.inc(v.nop())
            posr_mark = dve.n

            @self.sec("T")
            def _(t, i=i, posr_mark=posr_mark):
                dve.wait(t, posr_mark)
                pe.inc(t.matmul(pcnt[:, :], lincl[:], maskc[:],
                                start=False, stop=(i == NT - 1), skip_group_check=True))

            @self.sec("V")
            def _(v, i=i):
                v.tensor_tensor(t3[:], pos_[:], offs[:], op=ALU.add)
                v.drain()
                v.tensor_scalar(t3[:], t3[:], float(CAPACITY), None, op0=ALU.is_lt)
                v.drain()
                v.tensor_tensor(keepm[:], t3[:], maskf[:], op=ALU.mult)
                v.drain()
                dve.inc(v.nop())
            keep_mark = dve.n

            @self.sec("V")
            def _(v, i=i):
                if self.RT_KEEP_ONLY:
                    v.tensor_copy(kf[:], keepm[:])
                    v.drain()
                else:
                    v.tensor_scalar(t3[:], pos_[:], float(SLOTB), None, op0=ALU.is_lt)
                    v.drain()
                    v.tensor_tensor(kf[:], t3[:], keepm[:], op=ALU.mult)
                    v.drain()
                dve.inc(v.nop())
            kf_mark = dve.n

            @self.sec("T")
            def _(t, i=i, kf_mark=kf_mark, prm=prev_rel_mark[0]):
                dve.wait(t, kf_mark)
                if prm:
                    dve.wait(t, prm)
                pe.inc(t.matmul(pb2[:, :], c["ones_row1f"][0:1, :], pos_[0:1, :],
                                start=True, stop=True))
            bc_mark = pe.n

            @self.sec("V")
            def _(v, i=i, bc_mark=bc_mark):
                pe.wait(v, bc_mark)
                v.tensor_copy(relbc[:], pb2[:, :])
                v.drain()
                v.tensor_tensor(rel[:], pos_[:], relbc[:], op=ALU.subtract)
                bte = btab_row[0:1, E * i:E * (i + 1)]
                if getattr(self, "RT_SPLIT_Y", False):
                    v.tensor_tensor(bte, relbc[0:1, :], offs[0:1, :], op=ALU.add)
                    v.drain()
                    v.tensor_scalar(selr[0:1, :], bte, float(YSPLIT - 127), None,
                                    op0=ALU.is_lt)
                    v.tensor_tensor(tra[0:1, :], bte, ebA[0:1, :], op=ALU.add)
                    v.tensor_tensor(trb[0:1, :], bte, ebB[0:1, :], op=ALU.add)
                    v.drain()
                    v.tensor_tensor(tra[0:1, :], tra[0:1, :], trb[0:1, :],
                                    op=ALU.subtract)
                    v.drain()
                    v.tensor_tensor(tra[0:1, :], tra[0:1, :], selr[0:1, :],
                                    op=ALU.mult)
                    v.drain()
                    v.tensor_tensor(bte, trb[0:1, :], tra[0:1, :], op=ALU.add)
                    v.drain()
                else:
                    v.tensor_tensor(bte, relbc[0:1, :], ebase[0:1, :], op=ALU.add)
                    v.drain()
                    if self.RT_ADD_OFFS:
                        v.tensor_tensor(bte, bte, offs[0:1, :], op=ALU.add)
                        v.drain()
                v.tensor_scalar(bte, bte, float(self.RT_CLAMP), None, op0=ALU.min)
                v.tensor_scalar(tb[:], kf[:], -BIGRANK, BIGRANK,
                                op0=ALU.mult, op1=ALU.add)
                v.tensor_tensor(t4[:], rel[:], kf[:], op=ALU.mult)
                v.drain()
                v.tensor_tensor(self.grank_all[:, E * i:E * (i + 1)], t4[:], tb[:],
                                op=ALU.add)
                v.tensor_copy(self.gts_all[:, E * i:E * (i + 1)], gts[:])
                v.drain()
                dve.inc(v.nop())
            prev_rel_mark[0] = dve.n

        @self.sec("V")
        def _(v):
            v.tensor_copy(btab_i32[:], btab_row[:])
            v.drain()
            dve.inc(v.nop())
        bt_mark = self.sem["dve"].n

        @self.sec("S")
        def _(sy):
            dve.wait(sy, bt_mark)
            dS.dma(sy.dma_start(out=ap["btab"][:, :], in_=btab_i32[:]))

    def st_dispatch(self):
        nc, ap, c = self.nc, self.ap, self.c
        dS, dG, pe, dve, act, gp = (self.sem[k] for k in ["dS", "dG", "pe", "dve", "act", "gp"])
        h2t = [self.sbuf(f"dp_h2_{s}", [128, D], BF16) for s in range(2)]
        gtmp = self.sbuf("dp_gt", [128, 128])
        gd = [self.sbuf(f"dp_gd_{s}", [128, 128], BF16) for s in range(2)]
        xstage = self.sbuf("dp_xs", [128, RING * D], BF16)
        bcur = self.sbuf("dp_bcur", [1, RING], I32)
        pp = [self.psum(f"dp_pp_{s}", [128, D]) for s in range(2)]
        tld_h = self.tld_h
        tld_base = self.tld_n
        act_base = act.n
        dG_base = dG.n
        h2_marks, gb_marks, mm_marks, ev_marks = {}, {}, {}, {}
        for i in range(NT):
            @self.sec("S")
            def _(sy, i=i):
                if i >= 2:
                    pe.wait(sy, mm_marks[(i - 2) * E + E - 1])
                dS.dma(sy.dma_start(out=h2t[i % 2][:],
                                    in_=ap["h2"][i * 128:(i + 1) * 128, :]))
            h2_marks[i] = dS.n

            for e in range(E):
                g_ = i * E + e

                @self.sec("V")
                def _(v, g_=g_):
                    if g_ >= 2:
                        pe.wait(v, mm_marks[g_ - 2])
                    v.tensor_scalar(gtmp[:], c["iotaF"][:],
                                    self.grank_all[:, g_:g_ + 1], None,
                                    op0=ALU.is_equal)
                    v.drain()
                    v.tensor_copy(gd[g_ % 2][:], gtmp[:])
                    v.drain()
                    dve.inc(v.nop())
                gb_marks[g_] = dve.n

                @self.sec("T")
                def _(t, i=i, g_=g_):
                    dS.wait(t, h2_marks[i])
                    dve.wait(t, gb_marks[g_])
                    if g_ >= 2:
                        act.wait(t, ev_marks[g_ - 2])
                    t.matmul(pp[g_ % 2][:, 0:512], gd[g_ % 2][:],
                             h2t[i % 2][:, 0:512], start=True, stop=True)
                    _lt = t.matmul(pp[g_ % 2][:, 512:1024], gd[g_ % 2][:],
                                   h2t[i % 2][:, 512:1024], start=True, stop=True)
                    pe.inc(_lt)
                mm_marks[g_] = pe.n

                @self.sec("A")
                def _(a, g_=g_):
                    pe.wait(a, mm_marks[g_])
                    if g_ >= RING:
                        a.wait_ge(dG.h, dG_base + 16 * (g_ - RING + 1))
                    a.activation(xstage[:, (g_ % RING) * D:(g_ % RING + 1) * D],
                                 pp[g_ % 2][:], AF.Identity)
                    a.drain()
                    act.inc(a.nop())
                ev_marks[g_] = act.n

        regs = [nc.alloc_registers(f"dpr{k}", engines=[ET.Pool]).handles[0]
                for k in range(RING)]
        vals = [RuntimeValue(r, min_val=0, max_val=NSLOT - 128) for r in regs]

        @self.sec("G")
        def _(g):
            with g.Fori(0, NT) as ih:
                g.dma_start(out=bcur[0:1, 0:RING],
                            in_=ap["btab"][0:1, DynSlice(ih * RING, RING)]
                            ).then_inc(tld_h, 16)
                g.wait_ge(tld_h, tld_base + (ih + 1) * 16)
                g.wait_ge(act.h, act_base + (ih + 1) * RING)
                for k in range(RING):
                    nc.reg_load(regs[k], bcur[0:1, k:k + 1])
                    g.dma_start(out=ap["x_all"][DynSlice(vals[k], 128), :],
                                in_=xstage[:, k * D:(k + 1) * D],
                                accum_op=ALU.add).then_inc(dG.h, 16)
        dG.n += 16 * RING * NT
        self.tld_n += 16 * NT

    def st_experts(self):
        nc, ap, c = self.nc, self.ap, self.c
        dS, dG, pe, dve, act, gp = (self.sem[k] for k in ["dS", "dG", "pe", "dve", "act", "gp"])
        w1sb = self.sbuf("ex_w1", [128, ND * 2048], BF16)
        w2sb = self.sbuf("ex_w2", [128, NF * D], BF16)
        b1row = self.sbuf("ex_b1", [1, FF], BF16)
        b2row = self.sbuf("ex_b2", [1, D], BF16)
        onesr = self.sbuf("ex_ones", [1, 512], BF16)
        xet = self.sbuf("ex_xet", [128, ND * 640], BF16)
        midt = self.sbuf("ex_mid", [128, NF * 640], BF16)
        ysb = self.sbuf("ex_y", [128, 5 * D], BF16)
        pg = [self.psum(f"ex_pg{i}", [128, 512]) for i in range(2)]

        @self.sec("G")
        def _(g):
            g.memset(onesr[:], 1.0)
            gp.inc(g.engine_nop())

        prev_done = {"mid": 0, "y": 0, "xet": 0, "w": 0}
        for e in range(E):
            @self.sec("G")
            def _(g, e=e):
                if e > 0:
                    pe.wait(g)
                for kk in range(NF):
                    dG.dma(g.dma_start(out=w2sb[:, kk * D:(kk + 1) * D],
                                       in_=ap["w2"][e, kk * 128:(kk + 1) * 128, :]))
                dG.dma(g.dma_start(out=b1row[:], in_=ap["b1"][e:e + 1, :]))
                dG.dma(g.dma_start(out=b2row[:], in_=ap["b2"][e:e + 1, :]))
            w_mark = dG.n

            for (cb, csz) in self.CH:
                nst = csz // 128
                @self.sec("S")
                def _(sy, e=e, cb=cb, csz=csz):
                    pe.wait(sy)
                    dG.wait(sy)
                    for kk in range(ND):
                        dS.dma(sy.dma_start(
                            out=xet[:, kk * csz:(kk + 1) * csz],
                            in_=ap["x_all"][e * SLOTB + cb:e * SLOTB + cb + csz,
                                            kk * 128:(kk + 1) * 128],
                            transpose=True))
                x_mark = dS.n

                nch = (csz + 511) // 512
                for half in range(2):
                    @self.sec("G")
                    def _(g, e=e, half=half):
                        pe.wait(g)
                        for kk in range(ND):
                            dG.dma(g.dma_start(
                                out=w1sb[:, kk * 2048:(kk + 1) * 2048],
                                in_=ap["w1"][e, kk * 128:(kk + 1) * 128,
                                             half * 2048:(half + 1) * 2048]))
                    w1_mark = dG.n
                    for mtl in range(16):
                        mt = half * 16 + mtl

                        @self.sec("T")
                        def _(t, e=e, mt=mt, mtl=mtl, csz=csz, nch=nch,
                              x_mark=x_mark, w_mark=w_mark, w1_mark=w1_mark):
                            dS.wait(t, x_mark)
                            dG.wait(t, max(w_mark, w1_mark))
                            act.wait(t)
                            for ci in range(nch):
                                cw = min(512, csz - ci * 512)
                                ps = pg[ci % 2]
                                for kk in range(ND):
                                    _lt = t.matmul(
                                        ps[:, 0:cw],
                                        w1sb[:, kk * 2048 + mtl * 128:kk * 2048 + (mtl + 1) * 128],
                                        xet[:, kk * csz + ci * 512:kk * csz + ci * 512 + cw],
                                        start=(kk == 0), stop=False)
                                _lt = t.matmul(ps[:, 0:cw],
                                               b1row[0:1, mt * 128:(mt + 1) * 128],
                                               onesr[0:1, 0:cw], start=False, stop=True)
                            pe.inc(_lt)
                        mm_mark = pe.n

                        @self.sec("A")
                        def _(a, mt=mt, csz=csz, nch=nch, mm_mark=mm_mark):
                            pe.wait(a, mm_mark)
                            for ci in range(nch):
                                cw = min(512, csz - ci * 512)
                                ps = pg[ci % 2]
                                a.activation(midt[:, mt * csz + ci * 512:mt * csz + ci * 512 + cw],
                                             ps[:, 0:cw], AF.Gelu)
                            a.drain()
                            act.inc(a.nop())

                gelu_done = self.sem["act"].n
                for st in range(nst):
                    for c2 in range(2):
                        @self.sec("T")
                        def _(t, e=e, st=st, c2=c2, csz=csz, gelu_done=gelu_done):
                            act.wait(t, gelu_done)
                            dve.wait(t)
                            ps = pg[c2]
                            for kk in range(NF):
                                _lt = t.matmul(
                                    ps[:, :],
                                    midt[:, kk * csz + st * 128:kk * csz + (st + 1) * 128],
                                    w2sb[:, kk * D + c2 * 512:kk * D + (c2 + 1) * 512],
                                    start=(kk == 0), stop=False)
                            _lt = t.matmul(ps[:, :], onesr[0:1, 0:128],
                                           b2row[0:1, c2 * 512:(c2 + 1) * 512],
                                           start=False, stop=True)
                            pe.inc(_lt)
                        y_mark = pe.n

                        @self.sec("V")
                        def _(v, st=st, c2=c2, y_mark=y_mark):
                            pe.wait(v, y_mark)
                            v.tensor_copy(ysb[:, st * D + c2 * 512:st * D + (c2 + 1) * 512],
                                          pg[c2][:, :])
                            v.drain()
                            dve.inc(v.nop())
                ych_mark = self.sem["dve"].n

                @self.sec("S")
                def _(sy, e=e, cb=cb, csz=csz, nst=nst, ych_mark=ych_mark):
                    dve.wait(sy, ych_mark)
                    dS.dma(sy.dma_start(
                        out=ap["y_all"][e * SLOTB + cb:e * SLOTB + cb + csz, :]
                        .rearrange("(a p) d -> p a d", p=128),
                        in_=ysb[:, 0:nst * D].rearrange("p (a d) -> p a d", d=D)))

    def st_combine(self):
        nc, ap, c = self.nc, self.ap, self.c
        dS, dG, pe, dve, act, gp = (self.sem[k] for k in ["dS", "dG", "pe", "dve", "act", "gp"])
        ystage = self.sbuf("cm_ys", [128, RING * D], BF16)
        gtmp = self.sbuf("cm_gt", [128, 128])
        gtmp2 = self.sbuf("cm_gt2", [128, 128])
        NDEEP = 4
        gc = [self.sbuf(f"cm_gc_{s}", [128, 128], BF16) for s in range(NDEEP)]
        gct = [self.sbuf(f"cm_gct_{s}", [128, 128], BF16) for s in range(NDEEP)]
        r3r = [self.sbuf(f"cm_r3_{s}", [128, D]) for s in range(2)]
        ost = [self.sbuf(f"cm_o_{s}", [128, D]) for s in range(2)]
        bcur = self.sbuf("cm_bcur", [1, RING], I32)
        pp = [self.psum(f"cm_pp_{s}", [128, D]) for s in range(2)]
        ptr = [self.psum(f"cm_ptr_{s}", [128, 1024], BF16) for s in range(NDEEP)]
        tld_h = self.tld_h
        tld_base = self.tld_n
        dG_base = dG.n
        pe_base = pe.n

        regs = [nc.alloc_registers(f"cmr{k}", engines=[ET.Activation]).handles[0]
                for k in range(RING)]
        vals = [RuntimeValue(r, min_val=0, max_val=self.RT_CLAMP) for r in regs]
        ysrc = getattr(self, "CM_YSRC", "y_all")

        @self.sec("A")
        def _(a):
            cc2 = getattr(self, "cc2_h", None)
            if cc2 is not None:
                a.wait_ge(cc2, 4)
            with a.Fori(0, NT) as ih:
                a.dma_start(out=bcur[0:1, 0:RING],
                            in_=ap["btab"][0:1, DynSlice(ih * RING, RING)]
                            ).then_inc(tld_h, 16)
                a.wait_ge(tld_h, tld_base + (ih + 1) * 16)
                a.wait_ge(pe.h, pe_base + ih * 2 * RING)
                for k in range(RING):
                    nc.reg_load(regs[k], bcur[0:1, k:k + 1])
                    a.dma_start(out=ystage[:, k * D:(k + 1) * D],
                                in_=ap[ysrc][DynSlice(vals[k], 128), :]
                                ).then_inc(dG.h, 16)
        dG.n += 16 * RING * NT
        self.tld_n += 16 * NT

        r3_marks, gb_marks, tr_marks, ge_marks, mm_marks = {}, {}, {}, {}, {}
        out_marks, cons_marks = {}, {}
        for i in range(NT):
            @self.sec("S")
            def _(sy, i=i):
                if i >= 2:
                    dve.wait(sy, cons_marks[i - 2])
                dS.dma(sy.dma_start(out=r3r[i % 2][:],
                                    in_=ap["r3"][i * 128:(i + 1) * 128, :]))
            r3_marks[i] = dS.n

            for e in range(E):
                jj = i * E + e

                @self.sec("V")
                def _(v, jj=jj):
                    if jj >= NDEEP:
                        pe.wait(v, tr_marks[jj - NDEEP])
                    v.tensor_scalar(gtmp[:], c["iotaF"][:],
                                    self.grank_all[:, jj:jj + 1], None,
                                    op0=ALU.is_equal)
                    v.drain()
                    v.tensor_scalar(gtmp2[:], gtmp[:],
                                    self.gts_all[:, jj:jj + 1], None,
                                    op0=ALU.mult)
                    v.drain()
                    v.tensor_copy(gc[jj % NDEEP][:], gtmp2[:])
                    v.drain()
                    dve.inc(v.nop())
                gb_marks[jj] = dve.n

                @self.sec("T")
                def _(t, jj=jj):
                    dve.wait(t, gb_marks[jj])
                    if jj >= NDEEP:
                        dve.wait(t, ge_marks[jj - NDEEP])
                    pe.inc(t.transpose(ptr[jj % NDEEP][:, 0:128], gc[jj % NDEEP][:], c["identb"][:]))
                tr_marks[jj] = pe.n

                @self.sec("V")
                def _(v, jj=jj):
                    pe.wait(v, tr_marks[jj])
                    if jj >= NDEEP:
                        pe.wait(v, mm_marks[jj - NDEEP])
                    v.tensor_copy(gct[jj % NDEEP][:], ptr[jj % NDEEP][:, 0:128])
                    v.drain()
                    dve.inc(v.nop())
                ge_marks[jj] = dve.n

                @self.sec("T")
                def _(t, i=i, e=e, jj=jj):
                    dve.wait(t, ge_marks[jj])
                    t.wait_ge(dG.h, dG_base + 16 * RING * (jj // RING + 1))
                    if e == 0 and i >= 2:
                        dve.wait(t, cons_marks[i - 2])
                    ys = ystage[:, (jj % RING) * D:(jj % RING + 1) * D]
                    t.matmul(pp[i % 2][:, 0:512], gct[jj % NDEEP][:], ys[:, 0:512],
                             start=(e == 0), stop=(e == E - 1))
                    _lt = t.matmul(pp[i % 2][:, 512:1024], gct[jj % NDEEP][:],
                                   ys[:, 512:1024],
                                   start=(e == 0), stop=(e == E - 1))
                    pe.inc(_lt)
                mm_marks[jj] = pe.n

            @self.sec("V")
            def _(v, i=i):
                pe.wait(v, mm_marks[i * E + E - 1])
                dS.wait(v, r3_marks[i])
                if i >= 2:
                    dS.wait(v, out_marks[i - 2])
                v.tensor_tensor(ost[i % 2][:], pp[i % 2][:], r3r[i % 2][:],
                                op=ALU.add)
                v.drain()
                dve.inc(v.nop())
            cons_marks[i] = dve.n

            @self.sec("S")
            def _(sy, i=i):
                dve.wait(sy, cons_marks[i])
                dS.dma(sy.dma_start(out=ap["out"][i * 128:(i + 1) * 128, :],
                                    in_=ost[i % 2][:]))
            out_marks[i] = dS.n



NTOK = B * T
NTILE = NTOK // 128
CAP = CAPACITY
ESTRIDE = CAP + 128
YG = E * ESTRIDE
YSPLIT = 4096
YBROWS = ESTRIDE - (YSPLIT - 128)
YAOFF = E * YSPLIT
YG2 = YAOFF + E * YBROWS


class MK(Builder):

    CH = L2.CH
    RT_EBASE = ESTRIDE
    RT_CLAMP = YG2 - 128
    RT_ADD_OFFS = True
    RT_KEEP_ONLY = True
    RT_SPLIT_Y = True
    RT_ZERO = ("x_ex", CAP + 256, False)
    CM_YSRC = "y_gat2"

    def __init__(self, nc, dbg=(), upto=None):
        super().__init__(nc, dbg)
        self.upto = upto
        self._declare()

    def _declare(self):
        self.din("x", [T, D])
        self.din("wz", [D, D]); self.din("bz", [D])
        self.din("wh", [D, D]); self.din("bh", [D])
        self.din("logd", [D])
        self.din("ln1g", [D]); self.din("ln1b", [D])
        self.din("wqkv", [D, 3 * D]); self.din("wproj", [D, D])
        self.din("ln2g", [D]); self.din("ln2b", [D]); self.din("wgate", [D, E])
        self.din("batch_id", [1, 1])
        self.din("w1", [D, FF])
        self.din("b1", [FF])
        self.din("w2", [FF, D])
        self.din("b2", [D])
        self.dout("out", [T, D])
        self.dint("xn_h", [T, D], BF16)
        self.dint("h_h", [D, T])
        self.dint("hd_h", [T, D])
        self.dint("r2_h", [T, D])
        self.dint("xp_h", [T, D])
        self.dint("xpp_h", [T, D])
        self.dint("h1_h", [T, D], BF16)
        self.dint("qk_h", [2 * D, T], BF16)
        self.dint("v_h", [T, D], BF16)
        self.dint("r3", [T, D])
        self.dint("h2", [T, D], BF16)
        self.dint("logits", [T, E])
        self.dint("counts", [1, E])
        self.dshr("counts_all", [E, E])
        self.dint("mask_my", [T, E], BF16)
        self.dshr("mask_all", [NTOK, E], BF16)
        self.dshr("h2_all", [NTOK, D], BF16)
        self.dint("x_ex", [CAP + 256, D], BF16)
        self.dint("y_ex", [ESTRIDE, D], BF16)
        self.dshr("y_gat2", [YG2, D], BF16)
        self.dint("btab", [1, NT * E], I32)
        self.dint("btab2", [1, NTILE // 2], I32)

    constants = L1.constants
    _mk_rope_tables = L1._mk_rope_tables
    _mk_cmask = L1._mk_cmask
    _mk_ln2_consts = L1._mk_ln2_consts
    st_ln_outer = L1.st_ln_outer
    st_mingru = L1.st_mingru
    st_diffusion = L1.st_diffusion
    st_resid_ln = L1.st_resid_ln
    st_qkv = L1.st_qkv
    st_attention = L1.st_attention
    st_proj_r3 = L1.st_proj_r3
    st_ln2_logits = L1.st_ln2_logits
    _lz_tile = L1._lz_tile
    st_route = L2.st_route
    st_combine = L2.st_combine

    def st_gather_moe(self):
        cc = self.sem["cc"]
        rg = [list(range(B))]

        @self.sec("G")
        def _(g):
            cc.inc(g.collective_compute(
                "AllGather", ALU.bypass, replica_groups=rg,
                ins=[self.ap["counts"].opt()], outs=[self.ap["counts_all"].opt()]))
            cc.wait(g)
            g.collective_compute(
                "AllGather", ALU.bypass, replica_groups=rg,
                ins=[self.ap["mask_my"].opt()],
                outs=[self.ap["mask_all"].opt()]).then_inc(self.cc2_h, 1)
            g.collective_compute(
                "AllGather", ALU.bypass, replica_groups=rg,
                ins=[self.ap["h2"].opt()],
                outs=[self.ap["h2_all"].opt()]).then_inc(self.cc2_h, 1)

    def st_dispatch_ex(self):
        nc, ap, c = self.nc, self.ap, self.c
        dS, dG, pe, dve, act, gp = (self.sem[k] for k in
                                    ["dS", "dG", "pe", "dve", "act", "gp"])
        msb = self.sbuf("dx_msb", [128, NTILE * E], BF16)
        mall = self.sbuf("dx_mall", [128, NTILE * E])
        iotaEr = self.sbuf("dx_iotaEr", [128, NTILE * E])
        me1 = self.sbuf("dx_me", [128, 1])
        mcolP = self.sbuf("dx_mcolP", [128, NTILE])
        mcol = self.sbuf("dx_mcol", [128, NTILE])
        mcolb = self.sbuf("dx_mcolb", [128, NTILE], BF16)
        posin = self.sbuf("dx_posin", [128, NTILE])
        tsrow = self.sbuf("dx_tsrow", [1, NTILE])
        tsT = self.sbuf("dx_tsT", [128, 2], BF16)
        pbase = self.sbuf("dx_pbase", [128, 2])
        tot0 = self.sbuf("dx_tot0", [128, 1])
        brow = self.sbuf("dx_brow", [1, NTILE])
        bbc = self.sbuf("dx_bbc", [128, NTILE])
        posg = self.sbuf("dx_posg", [128, NTILE])
        keepg = self.sbuf("dx_keepg", [128, NTILE])
        grankd = self.sbuf("dx_grankd", [128, NTILE])
        t5 = self.sbuf("dx_t5", [128, NTILE])
        thalf = self.sbuf("dx_thalf", [1, 2])
        ttb = self.sbuf("dx_ttb", [128, NTILE])
        pairadd = self.sbuf("dx_pairadd", [128, NTILE])
        btab2_row = self.sbuf("dx_bt2", [1, NTILE // 2])
        btab2_i32 = self.sbuf("dx_bt2i", [1, NTILE // 2], I32)
        tri1 = self.sbuf("dx_tri1", [128, 128], BF16)
        bid = self.sbuf("dx_bid", [1, 1])
        self._dx_ps = contextlib.ExitStack()
        _ps = lambda nm, shape: self._dx_ps.enter_context(
            self.nc.psum_tensor(nm, shape, F32))
        ppos = _ps("dx_ppos", [128, NTILE])
        pts = _ps("dx_pts", [1, NTILE])
        pcol = _ps("dx_pcol", [128, 2])
        prow1 = _ps("dx_prow1", [1, 128])
        pb128 = _ps("dx_pb128", [128, 1])

        @self.sec("G")
        def _(g):
            g.iota(iotaEr[:].rearrange("p (a e) -> p a e", e=E),
                   pattern=[[0, NTILE], [1, E]], base=0, channel_multiplier=0,
                   allow_small_or_imprecise_dtypes=True)
            g.memset(tri1[:], 0.0)
            g.affine_select(out=tri1[:], in_=tri1[:], compare_op=ALU.is_ge,
                            fill=1.0, base=0, pattern=[[-1, 128]], channel_multiplier=1)
            g.wait_ge(self.cc2_h, 1)
            dG.dma(g.dma_start(
                out=msb[:],
                in_=ap["mask_all"].rearrange("(p a) e -> p (a e)", p=128)))
            gp.inc(g.engine_nop())
        mall_mark = dG.n

        @self.sec("S")
        def _(sy):
            dS.dma(sy.dma_start(out=bid[:], in_=ap["batch_id"][:, :]))
        ld_mark = dS.n

        @self.sec("T")
        def _(t):
            dS.wait(t, ld_mark)
            dG.wait(t, mall_mark)
            gp.wait(t)
            pe.inc(t.matmul(pb128[:, :], c["ones_row1f"][0:1, :], bid[:, :],
                            start=True, stop=True))

        @self.sec("V")
        def _(v):
            pe.wait(v)
            v.tensor_copy(me1[:], pb128[:, :])
            v.tensor_copy(mall[:], msb[:])
            v.drain()
            v.tensor_scalar(iotaEr[:], iotaEr[:], me1[:, 0:1], None,
                            op0=ALU.is_equal)
            v.drain()
            v.tensor_tensor(mall[:], mall[:], iotaEr[:], op=ALU.mult)
            v.drain()
            m3 = mall[:].rearrange("p (a e) -> p a e", e=E)
            v.tensor_tensor(m3[:, :, 0:4], m3[:, :, 0:4], m3[:, :, 4:8], op=ALU.add)
            v.drain()
            v.tensor_tensor(m3[:, :, 0:2], m3[:, :, 0:2], m3[:, :, 2:4], op=ALU.add)
            v.drain()
            v.tensor_tensor(mcolP[:].rearrange("p (a one) -> p a one", one=1),
                            m3[:, :, 0:1], m3[:, :, 1:2], op=ALU.add)
            v.drain()
            dve.inc(v.nop())
        mcolp_mark = dve.n

        for hh in range(2):
            @self.sec("T")
            def _(t, hh=hh):
                dve.wait(t)
                pe.inc(t.transpose(ppos[:, 0:128],
                                   mcolP[:, hh * 128:(hh + 1) * 128], c["ident_f"][:]))

            @self.sec("V")
            def _(v, hh=hh):
                pe.wait(v)
                mv = mcol[:].rearrange("k (p two) -> k p two", two=2)
                v.tensor_copy(mv[:, :, hh:hh + 1],
                              ppos[:, 0:128].rearrange("k (p one) -> k p one", one=1))
                v.drain()
                dve.inc(v.nop())

        @self.sec("V")
        def _(v):
            v.tensor_copy(mcolb[:], mcol[:])
            v.drain()
            dve.inc(v.nop())
        mcol_mark = dve.n
        import os
        if os.environ.get("DX_CUT") == "A":
            return

        @self.sec("T")
        def _(t):
            dve.wait(t, mcol_mark)
            pe.inc(t.matmul(ppos[:, :], tri1[:], mcolb[:], start=True, stop=True))
            pe.inc(t.matmul(pts[0:1, :], c["ones_col_bf"][:, :], mcolb[:],
                            start=True, stop=True))
        pos_mark = pe.n

        @self.sec("V")
        def _(v):
            pe.wait(v, pos_mark)
            v.tensor_copy(posin[:], ppos[:, :])
            v.tensor_copy(tsrow[0:1, :], pts[0:1, :])
            v.drain()
            dve.inc(v.nop())
        ts_mark = dve.n
        import os
        if os.environ.get("DX_CUT") == "B":
            return

        for hh in range(2):
            @self.sec("T")
            def _(t, hh=hh):
                dve.wait(t)
                pe.inc(t.matmul(pb128[:, :], tsrow[0:1, hh * 128:(hh + 1) * 128],
                                c["ones_row1f"][0:1, 0:1], start=True, stop=True))

            @self.sec("V")
            def _(v, hh=hh):
                pe.wait(v)
                v.tensor_copy(tsT[:, hh:hh + 1], pb128[:, :])
                v.drain()
                dve.inc(v.nop())
        tsT_mark = dve.n
        import os
        if os.environ.get("DX_CUT") == "C":
            return

        @self.sec("T")
        def _(t):
            dve.wait(t, tsT_mark)
            pe.inc(t.matmul(pcol[:, :], tri1[:], tsT[:, :], start=True, stop=True))
            pe.inc(t.matmul(pts[0:1, 0:2], c["ones_col_bf"][:, :], tsT[:, :],
                            start=True, stop=True))
        pfx_mark = pe.n

        @self.sec("V")
        def _(v):
            pe.wait(v, pfx_mark)
            v.tensor_copy(pbase[:], pcol[:, :])
            v.tensor_copy(thalf[0:1, 0:2], pts[0:1, 0:2])
            v.drain()
            dve.inc(v.nop())
        pb_mark = dve.n
        import os
        if os.environ.get("DX_CUT") == "D":
            return

        @self.sec("T")
        def _(t):
            dve.wait(t, pb_mark)
            pe.inc(t.matmul(pb128[:, :], c["ones_row1f"][0:1, :], thalf[0:1, 0:1],
                            start=True, stop=True))

        @self.sec("V")
        def _(v):
            pe.wait(v)
            v.tensor_copy(tot0[:], pb128[:, :])
            v.drain()
            v.tensor_tensor(pbase[:, 1:2], pbase[:, 1:2], tot0[:], op=ALU.add)
            v.drain()
            dve.inc(v.nop())
        base_mark = dve.n
        import os
        if os.environ.get("DX_CUT") == "E":
            return

        for hh in range(2):
            @self.sec("T")
            def _(t, hh=hh):
                dve.wait(t)
                pe.inc(t.matmul(prow1[0:1, :], pbase[:, hh:hh + 1], c["ident_f"][:],
                                start=True, stop=True))

            @self.sec("V")
            def _(v, hh=hh):
                pe.wait(v)
                v.tensor_copy(brow[0:1, hh * 128:(hh + 1) * 128], prow1[0:1, :])
                v.drain()
                dve.inc(v.nop())
        brow_mark = dve.n
        import os
        if os.environ.get("DX_CUT") == "F":
            return

        @self.sec("T")
        def _(t):
            dve.wait(t, brow_mark)
            pe.inc(t.matmul(ppos[:, :], c["ones_row1f"][0:1, :], brow[0:1, :],
                            start=True, stop=True))
        bbc_mark = pe.n

        @self.sec("V")
        def _(v):
            pe.wait(v, bbc_mark)
            v.tensor_copy(bbc[:], ppos[:, :])
            v.drain()
            dve.inc(v.nop())
        bbccp_mark = dve.n

        @self.sec("T")
        def _(t):
            dve.wait(t, bbccp_mark)
            pe.inc(t.matmul(ppos[:, :], c["ones_row1f"][0:1, :], tsrow[0:1, :],
                            start=True, stop=True))
        ttb_mark = pe.n

        @self.sec("V")
        def _(v):
            pe.wait(v, ttb_mark)
            v.tensor_copy(ttb[:], ppos[:, :])
            v.drain()
            pav = pairadd[:].rearrange("p (a two) -> p a two", two=2)
            ttv = ttb[:].rearrange("p (a two) -> p a two", two=2)
            v.tensor_scalar(pav[:, :, 0:1], ttv[:, :, 0:1], 0.0, None, op0=ALU.mult)
            v.tensor_copy(pav[:, :, 1:2], ttv[:, :, 0:1])
            v.drain()
            v.tensor_tensor(posg[:], posin[:], bbc[:], op=ALU.add)
            v.drain()
            v.tensor_scalar(t5[:], posg[:], float(CAP), None, op0=ALU.is_lt)
            v.drain()
            v.tensor_tensor(keepg[:], t5[:], mcol[:], op=ALU.mult)
            v.drain()
            v.tensor_tensor(posin[:], posin[:], pairadd[:], op=ALU.add)
            v.drain()
            v.tensor_scalar(t5[:], keepg[:], -BIGRANK, BIGRANK,
                            op0=ALU.mult, op1=ALU.add)
            v.tensor_tensor(grankd[:], posin[:], keepg[:], op=ALU.mult)
            v.drain()
            v.tensor_tensor(grankd[:], grankd[:], t5[:], op=ALU.add)
            v.drain()
            brv = brow[0:1, :].rearrange("one (a two) -> one a two", two=2)
            v.tensor_scalar(btab2_row[0:1, :].rearrange("one (a e) -> one a e", e=1),
                            brv[:, :, 0:1], float(CAP), None, op0=ALU.min)
            v.drain()
            v.tensor_copy(btab2_i32[:], btab2_row[:])
            v.drain()
            dve.inc(v.nop())
        rt_mark = dve.n
        import os
        if os.environ.get("DX_CUT") == "G":
            return

        @self.sec("S")
        def _(sy):
            dve.wait(sy, rt_mark)
            dS.dma(sy.dma_start(out=ap["btab2"][:, :], in_=btab2_i32[:]))
        bt2_mark = dS.n

        import os
        if os.environ.get("DX_CUT") == "1":
            return

        self._dx_ps.close()
        DRING = 4
        NPAIR = NTILE // 2
        h2t = [self.sbuf(f"dx_h2_{s}", [128, D], BF16) for s in range(4)]
        gtmp = self.sbuf("dx_gt", [128, 128])
        gdl0 = [self.sbuf(f"dx_gl0_{s}", [128, 128], BF16) for s in range(2)]
        gdl1 = [self.sbuf(f"dx_gl1_{s}", [128, 128], BF16) for s in range(2)]
        gdh1 = [self.sbuf(f"dx_gh1_{s}", [128, 128], BF16) for s in range(2)]
        iotaF128 = self.sbuf("dx_iF128", [128, 128])
        xstage = self.sbuf("dx_xs", [128, DRING * 2 * D], BF16)
        bcur = self.sbuf("dx_bcur", [1, DRING], I32)
        pplo = [self.psum(f"dx_pl_{s}", [128, D]) for s in range(2)]
        pphi = [self.psum(f"dx_ph_{s}", [128, D]) for s in range(2)]
        tld_h = self.tld_h
        tld_base = self.tld_n
        act_base = act.n
        dG_base = dG.n
        dxh = self.dxh_h
        gb_marks, mm_marks, ev_marks = {}, {}, {}

        @self.sec("G")
        def _(g):
            g.iota(iotaF128[:], pattern=[[1, 128]], base=128, channel_multiplier=0,
                   allow_small_or_imprecise_dtypes=True)
            gp.inc(g.engine_nop())

        @self.sec("S")
        def _(sy):
            sy.wait_ge(self.cc2_h, 2)
            for tt in range(4):
                sy.dma_start(out=h2t[tt][:],
                             in_=ap["h2_all"][tt * 128:(tt + 1) * 128, :]
                             ).then_inc(dxh, 16)

        for p in range(NPAIR):
            j0, j1 = 2 * p, 2 * p + 1

            @self.sec("V")
            def _(v, p=p, j0=j0, j1=j1):
                if p >= 2:
                    pe.wait(v, mm_marks[p - 2])
                v.tensor_scalar(gtmp[:], c["iotaF"][:], grankd[:, j0:j0 + 1], None,
                                op0=ALU.is_equal)
                v.drain()
                v.tensor_copy(gdl0[p % 2][:], gtmp[:])
                v.drain()
                v.tensor_scalar(gtmp[:], c["iotaF"][:], grankd[:, j1:j1 + 1], None,
                                op0=ALU.is_equal)
                v.drain()
                v.tensor_copy(gdl1[p % 2][:], gtmp[:])
                v.drain()
                v.tensor_scalar(gtmp[:], iotaF128[:], grankd[:, j1:j1 + 1], None,
                                op0=ALU.is_equal)
                v.drain()
                v.tensor_copy(gdh1[p % 2][:], gtmp[:])
                v.drain()
                dve.inc(v.nop())
            gb_marks[p] = dve.n

            @self.sec("T")
            def _(t, p=p, j0=j0, j1=j1):
                if p == 0:
                    gp.wait(t)
                t.wait_ge(dxh, 16 * (j1 + 1))
                dve.wait(t, gb_marks[p])
                if p >= 2:
                    act.wait(t, ev_marks[p - 2])
                for cH in range(2):
                    sl = slice(cH * 512, (cH + 1) * 512)
                    t.matmul(pplo[p % 2][:, sl], gdl0[p % 2][:],
                             h2t[j0 % 4][:, sl], start=True, stop=False)
                    t.matmul(pplo[p % 2][:, sl], gdl1[p % 2][:],
                             h2t[j1 % 4][:, sl], start=False, stop=True)
                    _lt = t.matmul(pphi[p % 2][:, sl], gdh1[p % 2][:],
                                   h2t[j1 % 4][:, sl], start=True, stop=True)
                pe.inc(_lt)
            mm_marks[p] = pe.n

            @self.sec("A")
            def _(a, p=p, j0=j0, j1=j1):
                pe.wait(a, mm_marks[p])
                if p >= DRING:
                    a.wait_ge(dG.h, dG_base + 16 * (p - DRING + 1))
                sl0 = (p % DRING) * 2 * D
                a.activation(xstage[:, sl0:sl0 + D], pplo[p % 2][:], AF.Identity)
                a.activation(xstage[:, sl0 + D:sl0 + 2 * D], pphi[p % 2][:],
                             AF.Identity)
                a.drain()
                act.inc(a.nop())
                if j1 + 4 < NTILE:
                    a.dma_start(out=h2t[j0 % 4][:],
                                in_=ap["h2_all"][(j0 + 4) * 128:(j0 + 5) * 128, :]
                                ).then_inc(dxh, 16)
                    a.dma_start(out=h2t[j1 % 4][:],
                                in_=ap["h2_all"][(j1 + 4) * 128:(j1 + 5) * 128, :]
                                ).then_inc(dxh, 16)
            ev_marks[p] = act.n

        regs = [nc.alloc_registers(f"dxr{k}", engines=[ET.Pool]).handles[0]
                for k in range(DRING)]
        vals = [RuntimeValue(r, min_val=0, max_val=CAP) for r in regs]

        @self.sec("G")
        def _(g):
            g.wait_ge(dS.h, bt2_mark)
            with g.Fori(0, NPAIR // DRING) as ih:
                g.dma_start(out=bcur[0:1, 0:DRING],
                            in_=ap["btab2"][0:1, DynSlice(ih * DRING, DRING)]
                            ).then_inc(tld_h, 16)
                g.wait_ge(tld_h, tld_base + (ih + 1) * 16)
                g.wait_ge(act.h, act_base + (ih + 1) * DRING)
                for k in range(DRING):
                    nc.reg_load(regs[k], bcur[0:1, k:k + 1])
                    g.dma_start(
                        out=ap["x_ex"][DynSlice(vals[k], 256), :]
                        .rearrange("(a p) d -> p a d", p=128),
                        in_=xstage[:, k * 2 * D:(k + 1) * 2 * D]
                        .rearrange("p (a d) -> p a d", d=D),
                        accum_op=ALU.add).then_inc(dG.h, 16)
                    g.wait_ge(dG.h, dG_base + (ih * DRING + k + 1) * 16)
        dG.n += 16 * DRING * (NPAIR // DRING)
        self.tld_n += 16 * (NPAIR // DRING)

    def st_experts_ex(self):
        nc, ap, c = self.nc, self.ap, self.c
        dS, dG, pe, dve, act, gp = (self.sem[k] for k in
                                    ["dS", "dG", "pe", "dve", "act", "gp"])
        NCHUNK = CAP // 512
        w1sb = self.sbuf("exx_w1", [128, ND * FF], BF16)
        w2sb = self.sbuf("exx_w2", [128, NF * D], BF16)
        b1col = self.sbuf("exx_b1c", [128, NF])
        b2row = self.sbuf("exx_b2", [1, D], BF16)
        onesr = self.sbuf("exx_ones", [1, 512], BF16)
        xet = [self.sbuf(f"exx_x{s}", [128, ND * 512], BF16) for s in range(2)]
        midt = self.sbuf("exx_mid", [128, NF * 512], BF16)
        ysb = self.sbuf("exx_y", [128, 4 * D], BF16)
        pg = [self.psum(f"exx_pg{i}", [128, 512]) for i in range(2)]
        pg2 = [self.psum(f"exx_pq{i}", [128, 512]) for i in range(2)]

        @self.sec("G")
        def _(g):
            g.memset(ysb[:, 0:D], 0.0)
            dG.dma(g.dma_start(out=ap["y_ex"][CAP:CAP + 128, :], in_=ysb[:, 0:D]))
            for kk in range(ND):
                dG.dma(g.dma_start(out=w1sb[:, kk * FF:(kk + 1) * FF],
                                   in_=ap["w1"][kk * 128:(kk + 1) * 128, :]))
            for kk in range(NF):
                dG.dma(g.dma_start(out=w2sb[:, kk * D:(kk + 1) * D],
                                   in_=ap["w2"][kk * 128:(kk + 1) * 128, :]))
            dG.dma(g.dma_start(out=b1col[:],
                               in_=ap["b1"].rearrange("(a p) -> p a", p=128),
                               allow_slow_non_contiguous=True))
            dG.dma(g.dma_start(out=b2row[:],
                               in_=ap["b2"].rearrange("(one d) -> one d", one=1)))
            g.memset(onesr[:], 1.0)
            gp.inc(g.engine_nop())
        w_mark = dG.n

        x_marks, g1_done, ev_marks, y_marks, st_marks = {}, {}, {}, {}, {}
        for ch in range(NCHUNK):
            rows = slice(ch * 512, (ch + 1) * 512)

            @self.sec("S")
            def _(sy, ch=ch, rows=rows):
                if ch >= 2:
                    pe.wait(sy, g1_done[ch - 2])
                for kk in range(ND):
                    dS.dma(sy.dma_start(
                        out=xet[ch % 2][:, kk * 512:(kk + 1) * 512],
                        in_=ap["x_ex"][rows, kk * 128:(kk + 1) * 128],
                        transpose=True))
            x_marks[ch] = dS.n

            for mt in range(NF):
                @self.sec("T")
                def _(t, ch=ch, mt=mt):
                    if ch == 0 and mt == 0:
                        dG.wait(t, w_mark)
                        gp.wait(t)
                    if mt == 0:
                        dS.wait(t, x_marks[ch])
                    key = ch * NF + mt
                    if key >= 2:
                        act.wait(t, ev_marks[key - 2])
                    ps = pg[mt % 2]
                    for kk in range(ND):
                        _lt = t.matmul(ps[:, :],
                                 w1sb[:, kk * FF + mt * 128:kk * FF + (mt + 1) * 128],
                                 xet[ch % 2][:, kk * 512:(kk + 1) * 512],
                                 start=(kk == 0), stop=(kk == ND - 1))
                    pe.inc(_lt)
                mm = pe.n

                @self.sec("A")
                def _(a, ch=ch, mt=mt, mm=mm):
                    pe.wait(a, mm)
                    a.activation(midt[:, mt * 512:(mt + 1) * 512],
                                 pg[mt % 2][:], AF.Gelu, bias=b1col[:, mt:mt + 1])
                    a.drain()
                    act.inc(a.nop())
                ev_marks[ch * NF + mt] = act.n
            g1_done[ch] = pe.n
            gelu_all = act.n

            for st in range(4):
                for c2 in range(2):
                    @self.sec("T")
                    def _(t, ch=ch, st=st, c2=c2, gelu_all=gelu_all):
                        act.wait(t, gelu_all)
                        key = st * 2 + c2
                        if key >= 2 or ch > 0:
                            dve.wait(t, y_marks.get((ch, key - 2),
                                                    y_marks.get((ch - 1, 6 + key), 0)))
                        ps = pg2[c2]
                        for kk in range(NF):
                            t.matmul(ps[:, :],
                                     midt[:, kk * 512 + st * 128:kk * 512 + (st + 1) * 128],
                                     w2sb[:, kk * D + c2 * 512:kk * D + (c2 + 1) * 512],
                                     start=(kk == 0), stop=False)
                        _lt = t.matmul(ps[:, :], onesr[0:1, 0:128],
                                       b2row[0:1, c2 * 512:(c2 + 1) * 512],
                                       start=False, stop=True)
                        pe.inc(_lt)
                    ymm = pe.n

                    @self.sec("V")
                    def _(v, ch=ch, st=st, c2=c2, ymm=ymm):
                        pe.wait(v, ymm)
                        if st == 0 and c2 == 0 and ch == 0:
                            dG.wait(v, w_mark)
                        if st == 0 and c2 == 0 and ch >= 1:
                            dS.wait(v, st_marks[ch - 1])
                        v.tensor_copy(ysb[:, st * D + c2 * 512:
                                          st * D + (c2 + 1) * 512],
                                      pg2[c2][:])
                        v.drain()
                        dve.inc(v.nop())
                    y_marks[(ch, st * 2 + c2)] = dve.n

            @self.sec("S")
            def _(sy, ch=ch, rows=rows):
                dve.wait(sy, y_marks[(ch, 7)])
                dS.dma(sy.dma_start(
                    out=ap["y_ex"][rows, :].rearrange("(a p) d -> p a d", p=128),
                    in_=ysb[:].rearrange("p (a d) -> p a d", d=D)))
            st_marks[ch] = dS.n

        NCH_A = YSPLIT // 512

        @self.sec("G")
        def _(g):
            g.wait_ge(dS.h, st_marks[NCH_A - 1])
            g.collective_compute(
                "AllGather", ALU.bypass, replica_groups=[list(range(B))],
                ins=[ap["y_ex"][0:YSPLIT, :].opt()],
                outs=[ap["y_gat2"][0:YAOFF, :].opt()]).then_inc(self.cc2_h, 1)
            g.wait_ge(dS.h, st_marks[NCHUNK - 1])
            g.wait_ge(dG.h, w_mark)
            g.collective_compute(
                "AllGather", ALU.bypass, replica_groups=[list(range(B))],
                ins=[ap["y_ex"][YSPLIT - 128:ESTRIDE, :].opt()],
                outs=[ap["y_gat2"][YAOFF:YG2, :].opt()]).then_inc(self.cc2_h, 1)

    def build(self):
        self._start(["dS", "dG", "pe", "dve", "act", "gp", "cc"])
        stages = ["constants", "st_ln_outer", "st_mingru", "st_diffusion",
                  "st_resid_ln", "st_qkv", "st_attention", "st_proj_r3",
                  "st_ln2_logits", "st_gather_moe",
                  "st_route", "st_dispatch_ex", "st_experts_ex", "st_combine"]
        if self.upto is not None:
            stages = stages[:stages.index(self.upto) + 1]
        with self.es:
            self.grank_all = self.es.enter_context(
                self.nc.sbuf_tensor("l2_grank", [128, NT * E], F32))
            self.gts_all = self.es.enter_context(
                self.nc.sbuf_tensor("l2_gts", [128, NT * E], F32))
            self.c["iotaF"] = self.es.enter_context(
                self.nc.sbuf_tensor("l2_iotaF", [128, 128], F32))
            self.c["identb"] = self.es.enter_context(
                self.nc.sbuf_tensor("l2_identb", [128, 128], BF16))
            self.tld_h = self.es.enter_context(self.nc.semaphore("tld"))
            self.tld_n = 0
            self.cc2_h = self.es.enter_context(self.nc.semaphore("cc2"))
            self.dxh_h = self.es.enter_context(self.nc.semaphore("dxh"))
            self.tld2_h = self.es.enter_context(self.nc.semaphore("tld2"))
            self.dx2_h = self.es.enter_context(self.nc.semaphore("dx2"))
            for st in stages:
                if st == "constants":
                    getattr(self, st)()
                else:
                    with self.stage():
                        getattr(self, st)()
                if st == "st_proj_r3":
                    self._attn_es.close()
            self.finale()
        return self.ap


def kernel(**inputs):
    f32 = lambda a: np.ascontiguousarray(np.asarray(a, dtype=np.float32))
    x = f32(inputs["L0_attn_x"])
    shared = dict(
        wz=f32(inputs["L0_attn_Wz"]), bz=f32(inputs["L0_attn_bz"]),
        wh=f32(inputs["L0_attn_Wh"]), bh=f32(inputs["L0_attn_bh"]),
        logd=f32(inputs["L0_attn_log_D"]),
        ln1g=f32(inputs["L0_ffn_ln1_g"]), ln1b=f32(inputs["L0_ffn_ln1_b"]),
        wqkv=f32(inputs["L0_ffn_Wqkv"]), wproj=f32(inputs["L0_ffn_Wproj"]),
        ln2g=f32(inputs["L0_ffn_ln2_g"]), ln2b=f32(inputs["L0_ffn_ln2_b"]),
        wgate=f32(inputs["L0_ffn_Wgate"]),
    )
    W1, B1 = f32(inputs["L0_ffn_W1"]), f32(inputs["L0_ffn_b1"])
    W2, B2 = f32(inputs["L0_ffn_W2"]), f32(inputs["L0_ffn_b2"])
    nc = bass.Bass("TRN2", target_bir_lowering=False)
    MK(nc).build()
    maps = [dict(x=x[b], batch_id=np.full((1, 1), float(b), np.float32),
                 w1=np.ascontiguousarray(W1[b]), b1=np.ascontiguousarray(B1[b]),
                 w2=np.ascontiguousarray(W2[b]), b2=np.ascontiguousarray(B2[b]),
                 **shared)
            for b in range(B)]
    res = run_bass_kernel_spmd(nc, maps, core_ids=list(range(B)))
    out = np.stack([np.asarray(res.results[b]["out"], dtype=np.float32) for b in range(B)])
    return out


# version 34
# speedup vs baseline: 1.0113x; 1.0113x over previous
import contextlib
import math
import numpy as np
import ml_dtypes
import concourse.bass as bass
import concourse.mybir as mybir
from concourse.bass import IndirectOffsetOnAxis
from concourse.bass_types import DynSlice
from concourse.expressions import RuntimeValue
from concourse.bass_utils import run_bass_kernel_spmd

F32 = mybir.dt.float32
BF16 = mybir.dt.bfloat16
I32 = mybir.dt.int32
AF = mybir.ActivationFunctionType
ALU = mybir.AluOpType
AX = mybir.AxisListType
ET = mybir.EngineType

B, T, D = 8, 4096, 1024
H, HD, HALF = 16, 64, 32
E, FF = 8, 4096
CAPACITY = 8192
SLOTB = 2048
NSLOT = E * SLOTB + 128
BIGRANK = 1.0e6
RING = 8
NT = T // 128
ND = D // 128
NF = FF // 128
EPS = 1e-5
BIGSLOT = 4_000_000.0
LN10K = math.log(10000.0)
TWO_PI = 2.0 * math.pi
CW1 = float(np.float32(6.28125))
CW2 = float(np.float32(TWO_PI - 6.28125))
CW3 = float(TWO_PI - CW1 - float(np.float32(TWO_PI - 6.28125)))
RNDC = float(np.float32(1.5 * 2 ** 23))


class Sem:
    def __init__(self, es, nc, name):
        self.h = es.enter_context(nc.semaphore(name))
        self.n = 0

    def inc(self, instr, k=1):
        self.n += k
        return instr.then_inc(self.h, k)

    def dma(self, instr):
        return self.inc(instr, 16)

    def wait(self, eng, n=None):
        v = self.n if n is None else n
        if v > 0:
            eng.wait_ge(self.h, v)


def make_identity(nc, g, ap):
    g.memset(ap, 0.0)
    g.affine_select(out=ap, in_=ap, compare_op=ALU.not_equal, fill=1.0,
                    base=0, pattern=[[-1, ap.shape[0]]], channel_multiplier=1)


class Builder:

    ENG = {"T": "tensor", "V": "vector", "A": "scalar", "G": "gpsimd", "S": "sync"}

    def __init__(self, nc, dbg=()):
        self.nc = nc
        self.dbg = dbg
        self.ap = {}
        self.c = {}

    def _start(self, semnames):
        self.es = contextlib.ExitStack()
        self.sem = {k: Sem(self.es, self.nc, k) for k in semnames}
        self.block = self.es.enter_context(self.nc.Block())

    def sec(self, which):
        return getattr(self.block, self.ENG[which])

    def sbuf(self, name, shape, dt=F32):
        es = getattr(self, "cur_es", None) or self.es
        return es.enter_context(self.nc.sbuf_tensor(name, shape, dt))

    def psum(self, name, shape, dt=F32):
        es = getattr(self, "cur_es", None) or self.es
        return es.enter_context(self.nc.psum_tensor(name, shape, dt))

    @contextlib.contextmanager
    def stage(self):
        self.barrier("T", "V", "A", "G", "S")
        prev = getattr(self, "cur_es", None)
        self.cur_es = contextlib.ExitStack()
        try:
            with self.cur_es:
                yield
        finally:
            self.cur_es = prev

    def barrier(self, *whiches):
        snapshot = [(s, s.n) for s in self.sem.values()]
        for w in whiches:
            @self.sec(w)
            def _(e, snapshot=snapshot):
                for s, n in snapshot:
                    s.wait(e, n)

    def din(self, name, shape, dt=F32):
        self.ap[name] = self.nc.dram_tensor(name, shape, dt, kind="ExternalInput").ap()

    def dout(self, name, shape, dt=F32):
        self.ap[name] = self.nc.dram_tensor(name, shape, dt, kind="ExternalOutput").ap()

    def dint(self, name, shape, dt=F32):
        self.ap[name] = self.nc.dram_tensor(name, shape, dt, kind="Internal").ap()

    def dshr(self, name, shape, dt=F32):
        self.ap[name] = self.nc.dram_tensor(
            name, shape, dt, kind="Internal", addr_space="Shared").ap()

    def mk_top2_scratch(self, prefix, G):
        sc = {}
        sc["s4"] = self.sbuf(prefix + "_s4", [128, G * 4])
        sc["s2"] = self.sbuf(prefix + "_s2", [128, G * 2])
        sc["m1"] = self.sbuf(prefix + "_m1", [128, G])
        sc["rep"] = self.sbuf(prefix + "_rep", [128, G * E])
        sc["t3"] = self.sbuf(prefix + "_t3", [128, G * E])
        sc["o1"] = self.sbuf(prefix + "_o1", [128, G * E])
        sc["o2"] = self.sbuf(prefix + "_o2", [128, G * E])
        sc["lgm"] = self.sbuf(prefix + "_lgm", [128, G * E])
        sc["iotaCr"] = self.sbuf(prefix + "_iC", [128, G * E])
        return sc

    def init_top2_iota(self, g, sc, G):
        g.iota(sc["iotaCr"][:].rearrange("p (a e) -> p a e", e=E),
               pattern=[[0, G], [-1, E]], base=E, channel_multiplier=0,
               allow_small_or_imprecise_dtypes=True)

    def _emit_top2(self, v, lgbuf, G, sc, mkout):
        def tree_max(buf):
            b3 = buf[:].rearrange("p (g e) -> p g e", e=E)
            s4v = sc["s4"][:].rearrange("p (g e) -> p g e", e=4)
            s2v = sc["s2"][:].rearrange("p (g e) -> p g e", e=2)
            m1v = sc["m1"][:].rearrange("p (g e) -> p g e", e=1)
            v.tensor_tensor(s4v, b3[:, :, 0:4], b3[:, :, 4:8], op=ALU.max)
            v.drain()
            v.tensor_tensor(s2v, s4v[:, :, 0:2], s4v[:, :, 2:4], op=ALU.max)
            v.drain()
            v.tensor_tensor(m1v, s2v[:, :, 0:1], s2v[:, :, 1:2], op=ALU.max)
            v.drain()

        def rep_m1():
            repv = sc["rep"][:].rearrange("p (g e) -> p g e", e=E)
            m1v = sc["m1"][:].rearrange("p (g e) -> p g e", e=1)
            v.tensor_copy(repv[:, :, 0:1], m1v)
            v.drain()
            v.tensor_copy(repv[:, :, 1:2], repv[:, :, 0:1])
            v.drain()
            v.tensor_copy(repv[:, :, 2:4], repv[:, :, 0:2])
            v.drain()
            v.tensor_copy(repv[:, :, 4:8], repv[:, :, 0:4])
            v.drain()

        def onehot(srcbuf, dstbuf):
            tree_max(srcbuf)
            rep_m1()
            v.tensor_tensor(sc["t3"][:], srcbuf[:], sc["rep"][:], op=ALU.is_equal)
            v.drain()
            v.tensor_tensor(sc["t3"][:], sc["t3"][:], sc["iotaCr"][:], op=ALU.mult)
            v.drain()
            tree_max(sc["t3"])
            rep_m1()
            v.tensor_tensor(dstbuf[:], sc["iotaCr"][:], sc["rep"][:],
                            op=ALU.is_equal)
            v.drain()

        onehot(lgbuf, sc["o1"])
        v.scalar_tensor_tensor(sc["lgm"][:], sc["o1"][:], -1e9, lgbuf[:],
                               op0=ALU.mult, op1=ALU.add)
        v.drain()
        onehot(sc["lgm"], sc["o2"])
        v.tensor_tensor(mkout[:], sc["o1"][:], sc["o2"][:], op=ALU.add)
        v.drain()

    def finale(self):
        nc = self.nc
        dS = self.sem["dS"]
        ap = self.ap
        dbg = self.dbg

        @self.sec("S")
        def _(sy):
            for s in self.sem.values():
                s.wait(sy)
            for name in dbg:
                if name not in ap:
                    continue
                src = ap[name]
                o = nc.dram_tensor("dbg_" + name, list(src.shape), src.dtype,
                                   kind="ExternalOutput").ap()
                dS.dma(sy.dma_start(out=o[:, :], in_=src[:, :]))
            sy.wait_ge(dS.h, dS.n)



class L1(Builder):
    def __init__(self, nc, dbg=(), upto=None):
        super().__init__(nc, dbg)
        self.upto = upto
        self._declare()

    def _declare(self):
        self.din("x", [T, D])
        self.din("wz", [D, D]); self.din("bz", [D])
        self.din("wh", [D, D]); self.din("bh", [D])
        self.din("logd", [D])
        self.din("ln1g", [D]); self.din("ln1b", [D])
        self.din("wqkv", [D, 3 * D]); self.din("wproj", [D, D])
        self.din("ln2g", [D]); self.din("ln2b", [D]); self.din("wgate", [D, E])
        self.dout("r3", [T, D])
        self.dout("h2", [T, D], BF16)
        self.dout("logits", [T, E])
        self.dout("counts", [1, E])
        self.dint("xn_h", [T, D], BF16)
        self.dint("h_h", [D, T])
        self.dint("hd_h", [T, D])
        self.dint("r2_h", [T, D])
        self.dint("xp_h", [T, D])
        self.dint("xpp_h", [T, D])
        self.dint("h1_h", [T, D], BF16)
        self.dint("qk_h", [2 * D, T], BF16)
        self.dint("v_h", [T, D], BF16)

    def build(self):
        self._start(["dS", "dG", "pe", "dve", "act", "gp"])
        stages = ["constants", "st_ln_outer", "st_mingru", "st_diffusion",
                  "st_resid_ln", "st_qkv", "st_attention", "st_proj_r3",
                  "st_ln2_logits"]
        if self.upto is not None:
            stages = stages[:stages.index(self.upto) + 1]
        with self.es:
            for st in stages:
                if st == "constants":
                    getattr(self, st)()
                else:
                    with self.stage():
                        getattr(self, st)()
            self.finale()
        return self.ap

    def constants(self):
        nc, ap, c = self.nc, self.ap, self.c
        dS, gp, dve, act, pe = (self.sem[k] for k in ["dS", "gp", "dve", "act", "pe"])
        c["bz"] = self.sbuf("c_bz", [128, ND])
        c["negbz"] = self.sbuf("c_negbz", [128, ND])
        c["bh"] = self.sbuf("c_bh", [128, ND])
        c["logd"] = self.sbuf("c_logd", [128, ND])
        c["dcoef"] = self.sbuf("c_dcoef", [128, ND])
        c["one_m2d"] = self.sbuf("c_one_m2d", [128, ND])
        c["ln1g"] = self.sbuf("c_ln1g", [128, ND])
        c["ln1b_f"] = self.sbuf("c_ln1b_f", [128, ND])
        c["ln1bg_tmp"] = self.sbuf("c_ln1bg_tmp", [128, ND])
        c["ln1b_bf"] = self.sbuf("c_ln1b_bf", [128, ND], BF16)
        c["ident_bf"] = self.sbuf("c_ident_bf", [128, 128], BF16)
        c["ident_f"] = self.sbuf("c_ident_f", [128, 128], F32)
        c["shiftc"] = self.sbuf("c_shiftc", [128, 128], BF16)
        c["corner_lo"] = self.sbuf("c_corner_lo", [128, 128], BF16)
        c["corner_hi"] = self.sbuf("c_corner_hi", [128, 128], BF16)
        c["ones_col_bf"] = self.sbuf("c_ones_col_bf", [128, 1], BF16)
        c["ones_row1"] = self.sbuf("c_ones_row1", [1, 128], BF16)
        c["ones_row1f"] = self.sbuf("c_ones_row1f", [1, 128], F32)
        c["epscol"] = self.sbuf("c_epscol", [128, 1], F32)

        @self.sec("S")
        def _(sy):
            for nm, dst in (("bz", c["bz"]), ("bh", c["bh"]), ("logd", c["logd"]),
                            ("ln1g", c["ln1g"]), ("ln1b", c["ln1b_f"])):
                src = ap[nm]
                for j in range(ND):
                    dS.dma(sy.dma_start(out=dst[:, j:j + 1],
                                        in_=src[j * 128:(j + 1) * 128].rearrange("(p one) -> p one", one=1)))

        self._const_tmp = contextlib.ExitStack()

        @self.sec("G")
        def _(g):
            make_identity(nc, g, c["ident_bf"][:])
            make_identity(nc, g, c["ident_f"][:])
            g.memset(c["shiftc"][:], 0.0)
            g.affine_select(out=c["shiftc"][:], in_=c["shiftc"][:],
                            compare_op=ALU.not_equal, fill=1.0, base=-1,
                            pattern=[[-1, 128]], channel_multiplier=1)
            g.affine_select(out=c["shiftc"][:], in_=c["shiftc"][:],
                            compare_op=ALU.not_equal, fill=1.0, base=1,
                            pattern=[[-1, 128]], channel_multiplier=1)
            g.memset(c["corner_lo"][:], 0.0)
            g.memset(c["corner_hi"][:], 0.0)
            g.affine_select(out=c["corner_lo"][:], in_=c["corner_lo"][:],
                            compare_op=ALU.not_equal, fill=1.0, base=-127,
                            pattern=[[200, 128]], channel_multiplier=1)
            g.affine_select(out=c["corner_hi"][:], in_=c["corner_hi"][:],
                            compare_op=ALU.not_equal, fill=1.0, base=200 * 127,
                            pattern=[[-200, 128]], channel_multiplier=1)
            g.memset(c["ones_col_bf"][:], 1.0)
            g.memset(c["ones_row1"][:], 1.0)
            g.memset(c["ones_row1f"][:], 1.0)
            g.memset(c["epscol"][:], EPS)
            gp.inc(g.engine_nop())


        @self.sec("A")
        def _(a):
            gp.wait(a)
            dS.wait(a)
            a.activation(c["dcoef"][:], c["logd"][:], AF.Exp)
            a.drain()
            act.inc(a.nop())

        @self.sec("V")
        def _(v):
            act.wait(v)
            v.tensor_scalar(c["one_m2d"][:], c["dcoef"][:], -2.0, 1.0,
                            op0=ALU.mult, op1=ALU.add)
            v.tensor_scalar(c["negbz"][:], c["bz"][:], -1.0, None,
                            op0=ALU.mult)
            v.drain()
            v.reciprocal(c["ln1bg_tmp"][:], c["ln1g"][:])
            v.drain()
            v.tensor_tensor(c["ln1b_bf"][:], c["ln1b_f"][:], c["ln1bg_tmp"][:],
                            op=ALU.mult)
            v.drain()
            dve.inc(v.nop())

        self._const_tmp.close()


    def _mk_rope_tables(self, ang, iot, kq, invf):
        nc, c = self.nc, self.c
        dS, gp, dve, act = (self.sem[k] for k in ["dS", "gp", "dve", "act"])

        @self.sec("G")
        def _(g):
            g.iota(iot[:], pattern=[[1, T]], base=0, channel_multiplier=0,
                   allow_small_or_imprecise_dtypes=True)
            for blk in range(4):
                g.iota(invf[blk * 32:(blk + 1) * 32, :], pattern=[[0, 1]], base=0,
                       channel_multiplier=1, allow_small_or_imprecise_dtypes=True)
            gp.inc(g.engine_nop())

        @self.sec("A")
        def _(a):
            gp.wait(a)
            a.activation(invf[:], invf[:], AF.Exp, scale=-LN10K / 32.0)
            a.drain()
            act.inc(a.nop())

        for which, phase in (("cosrep", 0.5 * math.pi), ("sinrep", 0.0)):
            @self.sec("V")
            def _(v, which=which, phase=phase):
                act.wait(v)
                v.tensor_scalar(ang[:], iot[:], invf[:, 0:1], float(phase),
                                op0=ALU.mult, op1=ALU.add)
                v.drain()
                v.tensor_scalar(kq[:], ang[:], 1.0 / TWO_PI, RNDC,
                                op0=ALU.mult, op1=ALU.add)
                v.tensor_scalar(kq[:], kq[:], RNDC, None, op0=ALU.subtract)
                v.drain()
                for cw in (CW1, CW2, CW3):
                    v.scalar_tensor_tensor(ang[:], kq[:], -cw, ang[:],
                                           op0=ALU.mult, op1=ALU.add)
                v.drain()
                v.tensor_scalar(ang[:], ang[:], float(math.pi), None, op0=ALU.min)
                v.tensor_scalar(ang[:], ang[:], float(-math.pi), None, op0=ALU.max)
                v.drain()
                dve.inc(v.nop())

            @self.sec("A")
            def _(a, which=which):
                dve.wait(a)
                a.activation(c[which][:], ang[:], AF.Sin)
                a.drain()
                act.inc(a.nop())

    def _mk_cmask(self):
        c = self.c
        gp = self.sem["gp"]
        c["cmask"] = self.sbuf("at_cmask", [128, 4 * 512], BF16)

        @self.sec("G")
        def _(g):
            g.memset(c["cmask"][:], 1.0)
            for a in range(4):
                g.affine_select(out=c["cmask"][:, a * 512:(a + 1) * 512],
                                in_=c["cmask"][:, a * 512:(a + 1) * 512],
                                compare_op=ALU.is_ge, fill=0.0,
                                base=-a * 128, pattern=[[1, 512]], channel_multiplier=-1)
            gp.inc(g.engine_nop())

    def _mk_ln2_consts(self):
        nc, ap, c = self.nc, self.ap, self.c
        dS, pe, dve, gp = (self.sem[k] for k in ["dS", "pe", "dve", "gp"])
        c["g2rep"] = self.sbuf("lz_g2rep", [128, D])
        c["b2rep"] = self.sbuf("lz_b2rep", [128, D])
        c["wgate_sb"] = self.sbuf("lz_wgate", [128, ND * E])
        g2row = self.sbuf("lz_g2row", [1, D])
        b2row = self.sbuf("lz_b2row", [1, D])
        pbc = self.psum("lz_pbc", [128, 512])

        @self.sec("S")
        def _(sy):
            dS.dma(sy.dma_start(out=g2row[:], in_=ap["ln2g"].rearrange("(one d) -> one d", one=1)))
            dS.dma(sy.dma_start(out=b2row[:], in_=ap["ln2b"].rearrange("(one d) -> one d", one=1)))
            for j in range(ND):
                dS.dma(sy.dma_start(out=c["wgate_sb"][:, j * E:(j + 1) * E],
                                    in_=ap["wgate"][j * 128:(j + 1) * 128, :]))
        ld_mark = dS.n

        for which, dst in ((g2row, "g2rep"), (b2row, "b2rep")):
            for ch in range(2):
                @self.sec("T")
                def _(t, which=which, ch=ch, ld_mark=ld_mark):
                    dS.wait(t, ld_mark)
                    dve.wait(t)
                    pe.inc(t.matmul(pbc[:, :], self.c["ones_row1f"][:, :],
                                    which[:, ch * 512:(ch + 1) * 512],
                                    start=True, stop=True))

                @self.sec("V")
                def _(v, dst=dst, ch=ch):
                    pe.wait(v)
                    v.tensor_copy(self.c[dst][:, ch * 512:(ch + 1) * 512], pbc[:, :])
                    v.drain()
                    dve.inc(v.nop())

    def st_ln_outer(self):
        ap, c = self.ap, self.c
        dS, dve, act = self.sem["dS"], self.sem["dve"], self.sem["act"]
        NB = 8
        x_sb = [self.sbuf(f"ln_x{s}", [128, NB * D]) for s in range(2)]
        xn_sb = [self.sbuf(f"ln_xn{s}", [128, NB * D], BF16) for s in range(2)]
        scr = self.sbuf("ln_scr", [128, D])
        st_sq = [self.sbuf(f"ln_sq{s}", [128, NB]) for s in range(2)]
        st_sm = self.sbuf("ln_sm", [128, NB])
        st_m = [self.sbuf(f"ln_m{s}", [128, NB]) for s in range(2)]
        st_v = [self.sbuf(f"ln_v{s}", [128, NB]) for s in range(2)]
        st_r = [self.sbuf(f"ln_r{s}", [128, NB]) for s in range(2)]
        tmp = self.sbuf("ln_tmp", [128, NB])
        xn_marks, st_marks, var_marks = {}, {}, {}
        for s in range(NT // NB):
            rows = slice(s * NB * 128, (s + 1) * NB * 128)
            p2 = s % 2

            @self.sec("S")
            def _(sy, rows=rows, s=s, p2=p2):
                if s >= 2:
                    dve.wait(sy, xn_marks[s - 2])
                    dS.wait(sy, st_marks[s - 2])
                dS.dma(sy.dma_start(
                    out=x_sb[p2][:].rearrange("p (a d) -> p a d", d=D),
                    in_=ap["x"][rows, :].rearrange("(a p) d -> p a d", p=128)))
            ld_mark = dS.n

            @self.sec("A")
            def _(a, s=s, p2=p2, ld_mark=ld_mark):
                dS.wait(a, ld_mark)
                if s >= 2:
                    dve.wait(a, var_marks[s - 2])
                for i in range(NB):
                    a.activation(scr[:], x_sb[p2][:, i * D:(i + 1) * D], AF.Square,
                                 accum_out=st_sq[p2][:, i:i + 1])
                a.drain()
                act.inc(a.nop())
            sq_mark = act.n

            @self.sec("V")
            def _(v, s=s, p2=p2, ld_mark=ld_mark, sq_mark=sq_mark):
                dS.wait(v, ld_mark)
                act.wait(v, sq_mark)
                for i in range(NB):
                    v.reduce_sum(st_sm[:, i:i + 1], x_sb[p2][:, i * D:(i + 1) * D],
                                 axis=AX.X)
                v.drain()
                v.tensor_scalar(st_m[p2][:], st_sm[:], 1.0 / D, None, op0=ALU.mult)
                v.drain()
                v.tensor_tensor(tmp[:], st_m[p2][:], st_m[p2][:], op=ALU.mult)
                v.tensor_scalar(st_v[p2][:], st_sq[p2][:], 1.0 / D, None, op0=ALU.mult)
                v.drain()
                v.tensor_tensor(st_v[p2][:], st_v[p2][:], tmp[:], op=ALU.subtract)
                v.drain()
                dve.inc(v.nop())
            var_marks[s] = dve.n

            @self.sec("A")
            def _(a, s=s, p2=p2):
                dve.wait(a, var_marks[s])
                a.activation(st_r[p2][:], st_v[p2][:], AF.Ln, bias=c["epscol"][:, 0:1])
                a.drain()
                a.activation(st_r[p2][:], st_r[p2][:], AF.Exp, scale=-0.5)
                a.drain()
                act.inc(a.nop())
            r_mark = act.n

            @self.sec("V")
            def _(v, s=s, p2=p2, r_mark=r_mark):
                act.wait(v, r_mark)
                for i in range(NB):
                    v.tensor_scalar(xn_sb[p2][:, i * D:(i + 1) * D],
                                    x_sb[p2][:, i * D:(i + 1) * D],
                                    st_m[p2][:, i:i + 1], st_r[p2][:, i:i + 1],
                                    op0=ALU.subtract, op1=ALU.mult)
                v.drain()
                dve.inc(v.nop())
            xn_marks[s] = dve.n

            @self.sec("S")
            def _(sy, rows=rows, s=s, p2=p2):
                dve.wait(sy, xn_marks[s])
                dS.dma(sy.dma_start(
                    out=ap["xn_h"][rows, :].rearrange("(a p) d -> p a d", p=128),
                    in_=xn_sb[p2][:].rearrange("p (a d) -> p a d", d=D)))
            st_marks[s] = dS.n

    def st_mingru(self):
        nc, ap, c = self.nc, self.ap, self.c
        dS, dG, pe, dve, act = (self.sem[k] for k in ["dS", "dG", "pe", "dve", "act"])
        xnT = self.sbuf("mg_xnT", [128, ND * T], BF16)
        wz_bf = self.sbuf("mg_wz", [128, ND * D], BF16)
        wh_bf = self.sbuf("mg_wh", [128, ND * D], BF16)
        aT = self.sbuf("mg_aT", [128, T])
        bT = self.sbuf("mg_bT", [128, T])
        zT = self.sbuf("mg_zT", [128, T])
        htT = self.sbuf("mg_htT", [128, T])
        hT = self.sbuf("mg_hT", [128, T])
        pz = [self.psum(f"mg_pz{i}", [128, 1024]) for i in range(2)]
        ph = [self.psum(f"mg_ph{i}", [128, 1024]) for i in range(2)]
        self.barrier("S", "G")

        @self.sec("S")
        def _(sy):
            for j in range(ND):
                dS.dma(sy.dma_start(out=xnT[:, j * T:(j + 1) * T],
                                    in_=ap["xn_h"][:, j * 128:(j + 1) * 128],
                                    transpose=True))
        xnT_mark = dS.n

        @self.sec("G")
        def _(g):
            for nm, dst in (("wz", wz_bf), ("wh", wh_bf)):
                for k in range(ND):
                    dG.dma(g.dma_start(out=dst[:, k * D:(k + 1) * D],
                                       in_=ap[nm][k * 128:(k + 1) * 128, :]))
        w_mark = dG.n

        scan_marks = []
        amarks = []
        for j in range(ND):
            for g2 in range(2):
                for q in range(2):
                    idx = len(amarks)

                    @self.sec("T")
                    def _(t, j=j, g2=g2, q=q, idx=idx):
                        if idx == 0:
                            dS.wait(t, xnT_mark)
                            dG.wait(t, w_mark)
                        if idx >= 2:
                            act.wait(t, amarks[idx - 2])
                        if g2 == 0 and q == 0 and j > 0:
                            dve.wait(t, scan_marks[-1])
                        base = g2 * 2048 + q * 1024
                        for kk in range(ND):
                            for cc in range(2):
                                rhs = xnT[:, kk * T + base + cc * 512:
                                          kk * T + base + (cc + 1) * 512]
                                _lt = t.matmul(pz[idx % 2][:, cc * 512:(cc + 1) * 512],
                                         wz_bf[:, kk * D + j * 128:kk * D + (j + 1) * 128],
                                         rhs, start=(kk == 0), stop=(kk == ND - 1))
                        for kk in range(ND):
                            for cc in range(2):
                                rhs = xnT[:, kk * T + base + cc * 512:
                                          kk * T + base + (cc + 1) * 512]
                                _lt = t.matmul(ph[idx % 2][:, cc * 512:(cc + 1) * 512],
                                         wh_bf[:, kk * D + j * 128:kk * D + (j + 1) * 128],
                                         rhs, start=(kk == 0), stop=(kk == ND - 1))
                        pe.inc(_lt)
                    mmm = pe.n

                    @self.sec("A")
                    def _(a, j=j, g2=g2, q=q, idx=idx, mmm=mmm):
                        pe.wait(a, mmm)
                        o = g2 * 2048 + q * 1024
                        a.activation(aT[:, o:o + 1024], pz[idx % 2][:, :], AF.Sigmoid,
                                     scale=-1.0, bias=c["negbz"][:, j:j + 1])
                        a.activation(zT[:, o:o + 1024], pz[idx % 2][:, :], AF.Sigmoid,
                                     bias=c["bz"][:, j:j + 1])
                        a.activation(htT[:, o:o + 1024], ph[idx % 2][:, :], AF.Identity,
                                     bias=c["bh"][:, j:j + 1])
                        a.drain()
                        act.inc(a.nop())
                    amarks.append(act.n)

                @self.sec("V")
                def _(v, j=j, g2=g2):
                    act.wait(v)
                    o = g2 * 2048
                    v.tensor_tensor(bT[:, o:o + 2048], zT[:, o:o + 2048],
                                            htT[:, o:o + 2048], op=ALU.mult)
                    v.drain()
                    dve.inc(v.nop())

            @self.sec("V")
            def _(v, j=j):
                v.drain()
                v.tensor_tensor_scan(hT[:], aT[:], bT[:], 0.0,
                                     op0=ALU.mult, op1=ALU.add)
                v.drain()
                v.nop()
                v.drain()
                dve.inc(v.nop())
            scan_marks.append(dve.n)

            @self.sec("S")
            def _(sy, j=j):
                dve.wait(sy, scan_marks[-1])
                dS.dma(sy.dma_start(out=ap["h_h"][j * 128:(j + 1) * 128, :], in_=hT[:]))

    def st_diffusion(self):
        import os
        nc, ap, c = self.nc, self.ap, self.c
        dS, pe, dve = self.sem["dS"], self.sem["pe"], self.sem["dve"]
        hf = [self.sbuf(f"df_hf{i}", [128, T]) for i in range(3)]
        hb = [self.sbuf(f"df_hb{i}", [128, T], BF16) for i in range(3)]
        hdT = self.sbuf("df_hdT", [128, T])
        tr_sb = self.sbuf("df_tr", [128, NT * 128])
        pn = [self.psum(f"df_pn{i}", [128, 512]) for i in range(2)]
        pt = [self.psum(f"df_pt{i}", [128, 512]) for i in range(2)]
        gevac = []
        gtc = []
        self.barrier("S")

        def slot(jj):
            return jj % 3

        load_marks = {}
        cast_marks = {}
        last_evac = [0]

        def emit_load(jj):
            dve_now, pe_now = dve.n, pe.n

            @self.sec("S")
            def _(sy, jj=jj, dve_now=dve_now):
                dve.wait(sy, dve_now)
                dS.dma(sy.dma_start(out=hf[slot(jj)][:],
                                    in_=ap["h_h"][(jj % ND) * 128:((jj % ND) + 1) * 128, :]))
            load_marks[jj] = dS.n

            @self.sec("V")
            def _(v, jj=jj, pe_now=pe_now):
                dS.wait(v, load_marks[jj])
                pe.wait(v, pe_now)
                v.tensor_copy(hb[slot(jj)][:], hf[slot(jj)][:])
                v.drain()
                dve.inc(v.nop())
            cast_marks[jj] = dve.n

        emit_load(-1)
        emit_load(0)
        emit_load(1)
        for j in range(ND):
            if j + 1 <= ND:
                pass
            jp, jn = j - 1, j + 1
            if jn == ND:
                jn = 0
            if (j + 1) < ND and (j + 1) not in load_marks:
                emit_load(j + 1)
            if j + 1 == ND and 8 not in load_marks:
                emit_load(8)

            jp_key = j - 1
            jn_key = j + 1
            evac_marks = []
            import os
            if int(os.environ.get("DF_PART", "3")) < 1:
                continue
            for cc in range(ND):
                gidx = len(gevac)

                @self.sec("T")
                def _(t, j=j, cc=cc, gidx=gidx, jp_key=jp_key, jn_key=jn_key):
                    for k in (j, jp_key, jn_key):
                        dve.wait(t, cast_marks[k])
                    if gidx >= 2:
                        dve.wait(t, gevac[gidx - 2])
                    ps = pn[gidx % 2]
                    sl = slice(cc * 512, (cc + 1) * 512)
                    _lt = t.matmul(ps[:, :], c["shiftc"][:], hb[slot(j)][:, sl],
                             start=True, stop=False)
                    _lt = t.matmul(ps[:, :], c["corner_lo"][:], hb[slot(jp_key)][:, sl],
                             start=False, stop=False)
                    _lt = t.matmul(ps[:, :], c["corner_hi"][:], hb[slot(jn_key)][:, sl],
                             start=False, stop=True)
                    pe.inc(_lt)
                mmk = pe.n

                @self.sec("V")
                def _(v, j=j, cc=cc, gidx=gidx, mmk=mmk):
                    pe.wait(v, mmk)
                    sl = slice(cc * 512, (cc + 1) * 512)
                    v.tensor_scalar(hdT[:, sl], hf[slot(j)][:, sl],
                                    c["one_m2d"][:, j:j + 1], None,
                                    op0=ALU.mult)
                    v.scalar_tensor_tensor(hdT[:, sl], pn[gidx % 2][:, :],
                                           c["dcoef"][:, j:j + 1], hdT[:, sl],
                                           op0=ALU.mult, op1=ALU.add)
                    v.drain()
                    dve.inc(v.nop())
                evac_marks.append(dve.n)
                gevac.append(dve.n)
                last_evac[0] = dve.n

            import os
            _dfpart = int(os.environ.get("DF_PART", "3"))
            if _dfpart < 2:
                continue
            tcopy_marks = []
            for gqs in range(ND):
                tidx = len(gtc)

                @self.sec("T")
                def _(t, gq=gqs, tidx=tidx):
                    dve.wait(t, evac_marks[-1])
                    if tidx >= 2:
                        dve.wait(t, gtc[tidx - 2])
                    for q in range(4):
                        a = gq * 4 + q
                        _lt = t.transpose(pt[tidx % 2][:, q * 128:(q + 1) * 128],
                                    hdT[:, a * 128:(a + 1) * 128], c["ident_f"][:])
                    pe.inc(_lt)
                trk = pe.n

                @self.sec("V")
                def _(v, gq=gqs, tidx=tidx, trk=trk):
                    pe.wait(v, trk)
                    v.tensor_copy(tr_sb[:, gq * 512:(gq + 1) * 512], pt[tidx % 2][:, :])
                    v.drain()
                    dve.inc(v.nop())
                tcopy_marks.append(dve.n)
                gtc.append(dve.n)

            if _dfpart < 3:
                continue

            @self.sec("S")
            def _(sy, j=j):
                dve.wait(sy, tcopy_marks[-1])
                dS.dma(sy.dma_start(
                    out=ap["hd_h"].rearrange("(a p) d -> p a d", p=128)[:, :, j * 128:(j + 1) * 128],
                    in_=tr_sb[:].rearrange("p (a d) -> p a d", d=128)))

    def st_resid_ln(self):
        ap, c = self.ap, self.c
        dS, dve, act = self.sem["dS"], self.sem["dve"], self.sem["act"]
        NB = 4
        xb = self.sbuf("rl_x", [128, NB * D])
        hdb = self.sbuf("rl_hd", [128, NB * D])
        r1b = self.sbuf("rl_r1", [128, NB * D])
        r2b = self.sbuf("rl_r2", [128, NB * D])
        xpb = self.sbuf("rl_xp", [128, NB * D])
        h1b = self.sbuf("rl_h1", [128, NB * D], BF16)
        scr = self.sbuf("rl_scr", [128, D])
        sq = self.sbuf("rl_sq", [128, NB])
        sm = self.sbuf("rl_sm", [128, NB])
        m_ = self.sbuf("rl_m", [128, NB])
        v_ = self.sbuf("rl_v", [128, NB])
        rstd = self.sbuf("rl_rstd", [128, NB])
        l1_ = self.sbuf("rl_l1", [128, NB])
        l2_ = self.sbuf("rl_l2", [128, NB])
        num = self.sbuf("rl_num", [128, NB])
        Sc = self.sbuf("rl_S", [128, NB])
        A1 = self.sbuf("rl_A1", [128, NB])
        Bv = self.sbuf("rl_Bv", [128, NB])
        mS = self.sbuf("rl_mS", [128, NB])
        tmp = self.sbuf("rl_tmp", [128, NB])
        self.barrier("S")

        store_mark = [0]
        for s in range(NT // NB):
            rows = slice(s * NB * 128, (s + 1) * NB * 128)

            @self.sec("S")
            def _(sy, s=s, rows=rows):
                if s > 0:
                    dS.wait(sy, store_mark[0])
                dS.dma(sy.dma_start(
                    out=xb[:].rearrange("p (a d) -> p a d", d=D),
                    in_=ap["x"][rows, :].rearrange("(a p) d -> p a d", p=128)))
                dS.dma(sy.dma_start(
                    out=hdb[:].rearrange("p (a d) -> p a d", d=D),
                    in_=ap["hd_h"][rows, :].rearrange("(a p) d -> p a d", p=128)))
            ld_mark = dS.n

            @self.sec("V")
            def _(v, s=s):
                dS.wait(v, ld_mark)
                act.wait(v)
                v.tensor_tensor(r1b[:], xb[:], hdb[:], op=ALU.add)
                for i in range(NB):
                    v.reduce_sum(sm[:, i:i + 1], r1b[:, i * D:(i + 1) * D], axis=AX.X)
                v.drain()
                v.nop()
                v.drain()
                dve.inc(v.nop())
            r1_mark = dve.n

            @self.sec("A")
            def _(a, s=s):
                dve.wait(a, r1_mark)
                for i in range(NB):
                    a.activation(scr[:], r1b[:, i * D:(i + 1) * D], AF.Square,
                                 accum_out=sq[:, i:i + 1])
                a.drain()
                a.nop()
                a.drain()
                act.inc(a.nop())
            sq_mark = act.n

            @self.sec("V")
            def _(v, s=s):
                act.wait(v, sq_mark)
                v.tensor_scalar(m_[:], sm[:], 1.0 / D, None, op0=ALU.mult)
                v.drain()
                v.tensor_tensor(tmp[:], m_[:], m_[:], op=ALU.mult)
                v.tensor_scalar(v_[:], sq[:], 1.0 / D, None, op0=ALU.mult)
                v.drain()
                v.tensor_tensor(v_[:], v_[:], tmp[:], op=ALU.subtract)
                v.drain()
                v.tensor_scalar(num[:], v_[:], 1.0 + EPS, EPS + EPS * EPS,
                                        op0=ALU.mult, op1=ALU.add)
                v.drain()
                dve.inc(v.nop())
            var_mark = dve.n

            @self.sec("A")
            def _(a, s=s):
                dve.wait(a, var_mark)
                a.activation(l1_[:], v_[:], AF.Ln, bias=c["epscol"][:, 0:1])
                a.activation(l2_[:], num[:], AF.Ln)
                a.activation(rstd[:], l1_[:], AF.Exp, scale=-0.5)
                a.drain()
                act.inc(a.nop())
            ln_mark = act.n

            @self.sec("V")
            def _(v, s=s):
                act.wait(v, ln_mark)
                v.tensor_tensor(tmp[:], l2_[:], l1_[:], op=ALU.subtract)
                v.drain()
                v.nop()
                v.drain()
                dve.inc(v.nop())

            @self.sec("A")
            def _(a, s=s):
                dve.wait(a)
                a.activation(tmp[:], tmp[:], AF.Exp, scale=-0.5)
                a.drain()
                act.inc(a.nop())

            @self.sec("V")
            def _(v, s=s):
                act.wait(v)
                v.tensor_tensor(Sc[:], rstd[:], tmp[:], op=ALU.mult)
                v.tensor_scalar(A1[:], rstd[:], 1.0, None, op0=ALU.add)
                v.tensor_tensor(Bv[:], m_[:], rstd[:], op=ALU.mult)
                v.drain()
                v.tensor_tensor(mS[:], m_[:], Sc[:], op=ALU.mult)
                v.drain()
                for i in range(NB):
                    sl = slice(i * D, (i + 1) * D)
                    v.tensor_scalar(r2b[:, sl], r1b[:, sl], A1[:, i:i + 1],
                                    Bv[:, i:i + 1], op0=ALU.mult, op1=ALU.subtract)
                    v.tensor_scalar(xpb[:, sl], r1b[:, sl], rstd[:, i:i + 1],
                                    Bv[:, i:i + 1], op0=ALU.mult, op1=ALU.subtract)
                    v.tensor_scalar(h1b[:, sl], r1b[:, sl], Sc[:, i:i + 1],
                                    mS[:, i:i + 1], op0=ALU.mult, op1=ALU.subtract)
                v.drain()
                v.nop()
                v.drain()
                dve.inc(v.nop())
            out_mark = dve.n

            @self.sec("S")
            def _(sy, s=s, rows=rows):
                dve.wait(sy, out_mark)
                dS.dma(sy.dma_start(
                    out=ap["r2_h"][rows, :].rearrange("(a p) d -> p a d", p=128),
                    in_=r2b[:].rearrange("p (a d) -> p a d", d=D)))
                dS.dma(sy.dma_start(
                    out=ap["xp_h"][rows, :].rearrange("(a p) d -> p a d", p=128),
                    in_=xpb[:].rearrange("p (a d) -> p a d", d=D)))
                dS.dma(sy.dma_start(
                    out=ap["h1_h"][rows, :].rearrange("(a p) d -> p a d", p=128),
                    in_=h1b[:].rearrange("p (a d) -> p a d", d=D)))
            store_mark[0] = dS.n

    def st_qkv(self):
        nc, ap, c = self.nc, self.ap, self.c
        dS, dG, pe, dve, act = (self.sem[k] for k in ["dS", "dG", "pe", "dve", "act"])
        c["cosrep"] = self.sbuf("rp_cos", [128, T], BF16)
        c["sinrep"] = self.sbuf("rp_sin", [128, T], BF16)
        wqk = self.sbuf("qk_wqk", [128, ND * 2048], BF16)
        wv = self.sbuf("qk_wv", [128, ND * D], BF16)
        biasqk = self.sbuf("qk_bias", [128, 16])
        bvrow = self.sbuf("qk_bvrow", [1, D])
        bvrep = self.sbuf("qk_bvrep", [128, D])
        qa = self.sbuf("qk_qa", [128, T], BF16)
        qb = self.sbuf("qk_qb", [128, T], BF16)
        ra = self.sbuf("qk_ra", [128, T], BF16)
        rb = self.sbuf("qk_rb", [128, T], BF16)
        tt = self.sbuf("qk_tt", [128, T], BF16)
        vblk = self.sbuf("qk_vblk", [128, 4 * D], BF16)
        pq = self.psum("qk_pq", [128, 512])
        pq2 = self.psum("qk_pq2", [128, 512])
        pb = self.psum("qk_pb", [128, 16])
        pr = self.psum("qk_pr", [1, 512])
        self._qkv_tmp = contextlib.ExitStack()
        _tmp = lambda nm, shape, dt=F32: self._qkv_tmp.enter_context(
            self.nc.sbuf_tensor(nm, shape, dt))
        wq_f = [_tmp(f"qk_wqf{s}", [128, 3 * D]) for s in range(2)]
        rp_ang = _tmp("rp_ang", [128, T])
        rp_iot = _tmp("rp_iot", [128, T])
        rp_kq = _tmp("rp_kq", [128, T])
        rp_invf = _tmp("rp_invf", [128, 1])
        self._mk_rope_tables(rp_ang, rp_iot, rp_kq, rp_invf)
        self.barrier("S", "G")

        wqf_vmarks = {}
        for kk in range(ND):
            @self.sec("S")
            def _(sy, kk=kk):
                if kk >= 2:
                    dve.wait(sy, wqf_vmarks[kk - 2])
                dS.dma(sy.dma_start(out=wq_f[kk % 2][:],
                                    in_=ap["wqkv"][kk * 128:(kk + 1) * 128, :]))
            ldm = dS.n

            @self.sec("V")
            def _(v, kk=kk, ldm=ldm):
                dS.wait(v, ldm)
                wview = wq_f[kk % 2][:].rearrange("p (h c) -> p h c", h=16)
                for mt in range(4):
                    v.tensor_scalar(
                        wqk[:, kk * 2048 + mt * 128:kk * 2048 + (mt + 1) * 128]
                        .rearrange("p (a b) -> p a b", b=32),
                        wview[:, 4 * mt:4 * mt + 4, 0:32],
                        c["ln1g"][:, kk:kk + 1], None, op0=ALU.mult)
                    v.tensor_scalar(
                        wqk[:, kk * 2048 + 512 + mt * 128:kk * 2048 + 512 + (mt + 1) * 128]
                        .rearrange("p (a b) -> p a b", b=32),
                        wview[:, 4 * mt:4 * mt + 4, 32:64],
                        c["ln1g"][:, kk:kk + 1], None, op0=ALU.mult)
                    v.tensor_scalar(
                        wqk[:, kk * 2048 + 1024 + mt * 128:kk * 2048 + 1024 + (mt + 1) * 128]
                        .rearrange("p (a b) -> p a b", b=32),
                        wview[:, 4 * mt:4 * mt + 4, 64:96],
                        c["ln1g"][:, kk:kk + 1], None, op0=ALU.mult)
                    v.tensor_scalar(
                        wqk[:, kk * 2048 + 1536 + mt * 128:kk * 2048 + 1536 + (mt + 1) * 128]
                        .rearrange("p (a b) -> p a b", b=32),
                        wview[:, 4 * mt:4 * mt + 4, 96:128],
                        c["ln1g"][:, kk:kk + 1], None, op0=ALU.mult)
                v.tensor_scalar(
                    wv[:, kk * D:(kk + 1) * D].rearrange("p (a b) -> p a b", b=64),
                    wview[:, :, 128:192],
                    c["ln1g"][:, kk:kk + 1], None, op0=ALU.mult)
                v.drain()
                v.nop()
                v.drain()
                dve.inc(v.nop())
            wqf_vmarks[kk] = dve.n
        wbf_mark = dve.n

        @self.sec("T")
        def _(t):
            dve.wait(t, wbf_mark)
            dS.wait(t)
            for mt in range(16):
                for kk in range(ND):
                    _lt = t.matmul(pb[:, mt:mt + 1],
                             wqk[:, kk * 2048 + mt * 128:kk * 2048 + (mt + 1) * 128],
                             c["ln1b_bf"][:, kk:kk + 1],
                             start=(kk == 0), stop=(kk == ND - 1))
            pe.inc(_lt)
        bias_mark = pe.n

        @self.sec("V")
        def _(v):
            pe.wait(v, bias_mark)
            v.tensor_copy(biasqk[:], pb[:, :])
            v.drain()
            dve.inc(v.nop())

        for ch in range(2):
            @self.sec("T")
            def _(t, ch=ch):
                dve.wait(t)
                for kk in range(ND):
                    _lt = t.matmul(pr[:, :], c["ln1b_bf"][:, kk:kk + 1],
                             wv[:, kk * D + ch * 512:kk * D + (ch + 1) * 512],
                             start=(kk == 0), stop=(kk == ND - 1))
                pe.inc(_lt)

            @self.sec("V")
            def _(v, ch=ch):
                pe.wait(v)
                v.tensor_copy(bvrow[:, ch * 512:(ch + 1) * 512], pr[:, :])
                v.drain()
                dve.inc(v.nop())

        for ch in range(2):
            @self.sec("T")
            def _(t, ch=ch):
                dve.wait(t)
                pe.inc(t.matmul(pq2[:, :], c["ones_row1f"][:, :],
                                bvrow[:, ch * 512:(ch + 1) * 512], start=True, stop=True))

            @self.sec("V")
            def _(v, ch=ch):
                pe.wait(v)
                v.tensor_copy(bvrep[:, ch * 512:(ch + 1) * 512], pq2[:, :])
                v.drain()
                dve.inc(v.nop())
        prep_mark = dve.n

        self.barrier("T", "V", "A", "G", "S")
        self._qkv_tmp.close()
        h1T = self.sbuf("qk_h1T", [128, ND * T], BF16)

        @self.sec("S")
        def _(sy):
            for kk in range(ND):
                dS.dma(sy.dma_start(out=h1T[:, kk * T:(kk + 1) * T],
                                    in_=ap["h1_h"][:, kk * 128:(kk + 1) * 128],
                                    transpose=True))
        h1T_mark = dS.n

        pairs = [(i, i + 4) for i in range(4)] + [(8 + i, 12 + i) for i in range(4)]
        prev_rope = [0]
        prev_store = [0]
        for (ma, mb) in pairs:
            for which, mt, dstq in (("a", ma, qa), ("b", mb, qb)):
                for ccg in range(4):
                    @self.sec("T")
                    def _(t, mt=mt, ccg=ccg):
                        dve.wait(t, prep_mark)
                        dS.wait(t, h1T_mark)
                        act.wait(t)
                        for half in range(2):
                            cc = ccg * 2 + half
                            ps = pq if half == 0 else pq2
                            for kk in range(ND):
                                _lt = t.matmul(ps[:, :],
                                         wqk[:, kk * 2048 + mt * 128:kk * 2048 + (mt + 1) * 128],
                                         h1T[:, kk * T + cc * 512:kk * T + (cc + 1) * 512],
                                         start=(kk == 0), stop=(kk == ND - 1))
                        pe.inc(_lt)

                    @self.sec("A")
                    def _(a, mt=mt, ccg=ccg, dstq=dstq):
                        pe.wait(a)
                        dve.wait(a, prev_rope[0])
                        for half in range(2):
                            cc = ccg * 2 + half
                            ps = pq if half == 0 else pq2
                            a.activation(dstq[:, cc * 512:(cc + 1) * 512], ps[:, :],
                                         AF.Identity, bias=biasqk[:, mt:mt + 1])
                        a.drain()
                        a.nop()
                        a.drain()
                        act.inc(a.nop())
            gemm_mark = act.n

            @self.sec("V")
            def _(v, ma=ma, mb=mb):
                act.wait(v, gemm_mark)
                dS.wait(v, prev_store[0])
                v.tensor_tensor(tt[:], qa[:], c["cosrep"][:], op=ALU.mult)
                v.scalar_tensor_tensor(ra[:], qb[:], -1.0, c["sinrep"][:],
                                       op0=ALU.mult, op1=ALU.mult)
                v.tensor_tensor(ra[:], ra[:], tt[:], op=ALU.add)
                v.drain()
                v.tensor_tensor(tt[:], qa[:], c["sinrep"][:], op=ALU.mult)
                v.tensor_tensor(rb[:], qb[:], c["cosrep"][:], op=ALU.mult)
                v.tensor_tensor(rb[:], rb[:], tt[:], op=ALU.add)
                v.drain()
                v.nop()
                v.drain()
                dve.inc(v.nop())
            rope_mark = dve.n
            prev_rope[0] = rope_mark

            @self.sec("S")
            def _(sy, ma=ma, mb=mb, rope_mark=rope_mark):
                dve.wait(sy, rope_mark)
                dS.dma(sy.dma_start(out=ap["qk_h"][ma * 128:(ma + 1) * 128, :], in_=ra[:]))
                dS.dma(sy.dma_start(out=ap["qk_h"][mb * 128:(mb + 1) * 128, :], in_=rb[:]))
            prev_store[0] = dS.n

        st_mark = [dS.n]
        for blk in range(8):
            for sub in range(4):
                a = blk * 4 + sub

                @self.sec("T")
                def _(t, a=a, sub=sub):
                    dve.wait(t)
                    if sub == 0:
                        dS.wait(t, st_mark[0])
                    for half in range(2):
                        ps = pq if half == 0 else pq2
                        for kk in range(ND):
                            _lt = t.matmul(ps[:, :],
                                     h1T[:, kk * T + a * 128:kk * T + (a + 1) * 128],
                                     wv[:, kk * D + half * 512:kk * D + (half + 1) * 512],
                                     start=(kk == 0), stop=(kk == ND - 1))
                    pe.inc(_lt)

                @self.sec("V")
                def _(v, a=a, sub=sub):
                    pe.wait(v)
                    for half in range(2):
                        ps = pq if half == 0 else pq2
                        sl = slice(sub * D + half * 512, sub * D + (half + 1) * 512)
                        v.scalar_tensor_tensor(vblk[:, sl], ps[:, :], 1.0,
                                               bvrep[:, half * 512:(half + 1) * 512],
                                               op0=ALU.mult, op1=ALU.add)
                    v.drain()
                    v.nop()
                    v.drain()
                    dve.inc(v.nop())
            vb_mark = dve.n

            @self.sec("S")
            def _(sy, blk=blk, vb_mark=vb_mark):
                dve.wait(sy, vb_mark)
                rows = slice(blk * 512, (blk + 1) * 512)
                dS.dma(sy.dma_start(
                    out=ap["v_h"][rows, :].rearrange("(a p) d -> p a d", p=128),
                    in_=vblk[:].rearrange("p (a d) -> p a d", d=D)))
            st_mark[0] = dS.n

    def st_attention(self):
        nc, ap, c = self.nc, self.ap, self.c
        dS, dG, pe, dve, act = (self.sem[k] for k in ["dS", "dG", "pe", "dve", "act"])
        self._attn_es = contextlib.ExitStack()
        attnT = self._attn_es.enter_context(self.nc.sbuf_tensor("at_attnT", [128, ND * T], BF16))
        qh2 = [self.sbuf(f"at_q{s}", [128, T], BF16) for s in range(2)]
        kh2 = [self.sbuf(f"at_k{s}", [128, T], BF16) for s in range(2)]
        vh2 = [self.sbuf(f"at_v{s}", [128, NT * 65], BF16) for s in range(2)]
        pt_sb = [self.sbuf(f"at_pt{i}", [128, 512], BF16) for i in range(6)]
        ptm_sb = [self.sbuf(f"at_ptm{i}", [128, 512], BF16) for i in range(4)]
        lnrow = self.sbuf("at_ln", [1, 512])
        recrow = self.sbuf("at_rec", [1, 512])
        rb_sb = self.sbuf("at_rb", [64, 512])
        self._mk_cmask()
        pst = [self.psum(f"at_pst{i}", [128, 512]) for i in range(2)]
        pot = [self.psum(f"at_pot{i}", [128, 512]) for i in range(2)]
        prb = self.psum("at_prb", [64, 512])
        self.ap["_attnT"] = attnT
        self.barrier("S", "G")

        apply_marks = [0, 0]
        bc_hold = [0]
        exp_marks = {}
        pv_marks = {}
        ld_mark = [0]

        head_end_pe = {}
        for h in range(H):
            qh, kh, vh = qh2[h % 2], kh2[h % 2], vh2[h % 2]

            @self.sec("G")
            def _(g, h=h):
                if h >= 2:
                    pe.wait(g, head_end_pe[h - 2])
                g.memset(vh[:], 1.0)
                dG.dma(g.dma_start(
                    out=vh[:].rearrange("p (a w) -> p a w", w=65)[:, :, 0:64],
                    in_=ap["v_h"].rearrange("(a p) w -> p a w", p=128)[:, :, h * 64:(h + 1) * 64]))
            vh_mark = dG.n

            @self.sec("S")
            def _(sy, h=h):
                if h >= 2:
                    pe.wait(sy, head_end_pe[h - 2])
                for half in range(2):
                    hb = 64 * half
                    dS.dma(sy.dma_start(out=qh[hb:hb + 32, :],
                                        in_=ap["qk_h"][h * 32:(h + 1) * 32, :]))
                    dS.dma(sy.dma_start(out=qh[hb + 32:hb + 64, :],
                                        in_=ap["qk_h"][512 + h * 32:512 + (h + 1) * 32, :]))
                    dS.dma(sy.dma_start(out=kh[hb:hb + 32, :],
                                        in_=ap["qk_h"][1024 + h * 32:1024 + (h + 1) * 32, :]))
                    dS.dma(sy.dma_start(out=kh[hb + 32:hb + 64, :],
                                        in_=ap["qk_h"][1536 + h * 32:1536 + (h + 1) * 32, :]))
            qk_mark = dS.n

            for cch in range(8):
                pc = cch % 2
                njt = 4 * cch + 4
                exp_marks.clear()
                pv_marks.clear()
                act_base = act.n
                dve_base = dve.n
                smarks, pvmarks = {}, {}

                @self.sec("T")
                def _(t, h=h, cch=cch, pc=pc, njt=njt, act_base=act_base, dve_base=dve_base):
                    dG.wait(t, vh_mark)
                    dS.wait(t, qk_mark)

                    def emit_pv(jj):
                        act.wait(t, act_base + jj + 1)
                        if jj >= 4 * cch:
                            dve.wait(t, dve_base + (jj - 4 * cch) + 1)
                        if jj == 0 and apply_marks[pc]:
                            dve.wait(t, apply_marks[pc])
                        src_pt = ptm_sb[jj % 4] if jj >= 4 * cch else pt_sb[jj % 6]
                        _lt = t.matmul(pot[pc][0:65, :], vh[:, jj * 65:(jj + 1) * 65],
                                 src_pt[:, :],
                                 start=(jj == 0), stop=(jj == njt - 1))
                        pe.inc(_lt)
                        pvmarks[jj] = pe.n

                    for jp in range(njt // 2):
                        for half in range(2):
                            j = 2 * jp + half
                            if j >= 2:
                                act.wait(t, act_base + j - 1)
                            elif cch > 0:
                                act.wait(t, act_base)
                            hb = 64 * half
                            _lt = t.matmul(pst[half][:, :],
                                     kh[hb:hb + 64, j * 128:(j + 1) * 128],
                                     qh[hb:hb + 64, cch * 512:(cch + 1) * 512],
                                     start=True, stop=True,
                                     tile_position=(hb, 0))
                            pe.inc(_lt)
                            smarks[j] = pe.n
                        if jp >= 2:
                            emit_pv(2 * jp - 4)
                            emit_pv(2 * jp - 3)
                    for jj in range(max(0, njt - 4), njt):
                        emit_pv(jj)

                @self.sec("A")
                def _(a, h=h, cch=cch, pc=pc, njt=njt):
                    for j in range(njt):
                        sj = j % 2
                        pe.wait(a, smarks[j])
                        if j >= 6:
                            pe.wait(a, pvmarks[j - 6])
                        a.activation(pt_sb[j % 6][:, :], pst[sj][:, :], AF.Exp, scale=0.125)
                        a.drain()
                        act.inc(a.nop())
                    pe.wait(a, pvmarks[njt - 1])
                    pe.wait(a, bc_hold[0])
                    a.activation(lnrow[:, :], pot[pc][64:65, :], AF.Ln)
                    a.drain()
                    a.nop()
                    a.drain()
                    act.inc(a.nop())
                    a.activation(recrow[:, :], lnrow[:, :], AF.Exp, scale=-1.0)
                    a.drain()
                    a.nop()
                    a.drain()
                    act.inc(a.nop())
                tail_mark = act.n

                @self.sec("V")
                def _(v, h=h, cch=cch, njt=njt, act_base=act_base):
                    for idx, j in enumerate(range(4 * cch, njt)):
                        act.wait(v, act_base + j + 1)
                        v.tensor_tensor(ptm_sb[j % 4][:, :], pt_sb[j % 6][:, :],
                                        c["cmask"][:, (j - 4 * cch) * 512:(j - 4 * cch + 1) * 512],
                                        op=ALU.mult)
                        v.drain()
                        v.nop()
                        v.drain()
                        dve.inc(v.nop())

                @self.sec("T")
                def _(t, tail_mark=tail_mark):
                    act.wait(t, tail_mark)
                    dve.wait(t)
                    _lt = t.matmul(prb[0:64, :], c["ones_row1f"][0:1, 0:64], recrow[:, :],
                             start=True, stop=True)
                    pe.inc(_lt)
                bc_mark = pe.n
                bc_hold[0] = bc_mark

                @self.sec("V")
                def _(v, h=h, cch=cch, pc=pc, bc_mark=bc_mark):
                    pe.wait(v, bc_mark)
                    v.tensor_copy(rb_sb[:, :], prb[0:64, :])
                    kb = h // 2
                    off = (h % 2) * 64
                    dst = attnT[off:off + 64, kb * T + cch * 512:kb * T + (cch + 1) * 512]
                    v.tensor_tensor(dst, pot[pc][0:64, :], rb_sb[:, :], op=ALU.mult)
                    v.drain()
                    v.nop()
                    v.drain()
                    dve.inc(v.nop())
                apply_marks[pc] = dve.n
            head_end_pe[h] = pe.n

        if "attnT_dbg" in self.dbg:
            o = nc.dram_tensor("dbg_attnT", [128, ND * T], F32, kind="ExternalOutput").ap()
            att_f = self.sbuf("at_dbgf", [128, T])
            for kb in range(max(1, H // 2)):
                @self.sec("V")
                def _(v, kb=kb):
                    for s in self.sem.values():
                        s.wait(v)
                    v.tensor_copy(att_f[:], attnT[:, kb * T:(kb + 1) * T])
                    v.drain()
                    dve.inc(v.nop())

                @self.sec("S")
                def _(sy, kb=kb):
                    dve.wait(sy)
                    dS.dma(sy.dma_start(out=o[:, kb * T:(kb + 1) * T], in_=att_f[:]))

    def st_proj_r3(self):
        nc, ap, c = self.nc, self.ap, self.c
        dS, pe, dve, act = (self.sem[k] for k in ["dS", "pe", "dve", "act"])
        attnT = ap["_attnT"]
        wp = self.sbuf("pj_wp", [128, ND * D], BF16)
        pj_sb = [self.sbuf(f"pj_sb{s}", [128, 512]) for s in range(2)]
        r2b = self.sbuf("pj_r2", [128, 4 * D])
        r3b = self.sbuf("pj_r3", [128, 4 * D])
        xpb = self.sbuf("pj_xp", [128, 4 * D])
        xppb = self.sbuf("pj_xpp", [128, 4 * D])
        pp = [self.psum(f"pj_pp{s}", [128, 512]) for s in range(2)]
        ptr = [self.psum(f"pj_ptr{s}", [128, 512]) for s in range(2)]
        ev_marks, tr_marks, add_marks = {}, {}, {}
        self.barrier("S", "G")

        @self.sec("G")
        def _(g):
            for kk in range(ND):
                self.sem["dG"].dma(g.dma_start(out=wp[:, kk * D:(kk + 1) * D],
                                               in_=ap["wproj"][kk * 128:(kk + 1) * 128, :]))
        wp_mark = self.sem["dG"].n
        dG = self.sem["dG"]

        st_mark = [0]
        for cch in range(8):
            rows = slice(cch * 512, (cch + 1) * 512)

            @self.sec("S")
            def _(sy, cch=cch, rows=rows):
                if st_mark[0]:
                    dS.wait(sy, st_mark[0])
                dS.dma(sy.dma_start(
                    out=r2b[:].rearrange("p (a d) -> p a d", d=D),
                    in_=ap["r2_h"][rows, :].rearrange("(a p) d -> p a d", p=128)))
                dS.dma(sy.dma_start(
                    out=xpb[:].rearrange("p (a d) -> p a d", d=D),
                    in_=ap["xp_h"][rows, :].rearrange("(a p) d -> p a d", p=128)))
            r2_mark = dS.n

            for j in range(ND):
                gg = cch * ND + j

                @self.sec("T")
                def _(t, cch=cch, j=j, gg=gg):
                    dG.wait(t, wp_mark)
                    if gg >= 2:
                        act.wait(t, ev_marks[gg - 2])
                    for kk in range(ND):
                        _lt = t.matmul(pp[gg % 2][:, :],
                                 wp[:, kk * D + j * 128:kk * D + (j + 1) * 128],
                                 attnT[:, kk * T + cch * 512:kk * T + (cch + 1) * 512],
                                 start=(kk == 0), stop=(kk == ND - 1))
                    pe.inc(_lt)
                mm_mark = pe.n

                @self.sec("A")
                def _(a, gg=gg, mm_mark=mm_mark):
                    pe.wait(a, mm_mark)
                    if gg >= 2:
                        pe.wait(a, tr_marks[gg - 2])
                    a.activation(pj_sb[gg % 2][:], pp[gg % 2][:, :], AF.Copy)
                    a.drain()
                    act.inc(a.nop())
                ev_marks[gg] = act.n

                @self.sec("T")
                def _(t, gg=gg):
                    act.wait(t, ev_marks[gg])
                    if gg >= 2:
                        dve.wait(t, add_marks[gg - 2])
                    for q in range(4):
                        _lt = t.transpose(ptr[gg % 2][:, q * 128:(q + 1) * 128],
                                    pj_sb[gg % 2][:, q * 128:(q + 1) * 128],
                                    c["ident_f"][:])
                    pe.inc(_lt)
                tr_marks[gg] = pe.n

                @self.sec("V")
                def _(v, cch=cch, j=j, gg=gg, r2_mark=r2_mark):
                    pe.wait(v, tr_marks[gg])
                    dS.wait(v, r2_mark)
                    for q in range(4):
                        sl = slice(q * D + j * 128, q * D + (j + 1) * 128)
                        v.tensor_tensor(r3b[:, sl], ptr[gg % 2][:, q * 128:(q + 1) * 128],
                                        r2b[:, sl], op=ALU.add)
                        v.tensor_tensor(xppb[:, sl], ptr[gg % 2][:, q * 128:(q + 1) * 128],
                                        xpb[:, sl], op=ALU.add)
                    v.drain()
                    dve.inc(v.nop())
                add_marks[gg] = dve.n
            blk_mark = dve.n

            @self.sec("S")
            def _(sy, rows=rows, blk_mark=blk_mark):
                dve.wait(sy, blk_mark)
                dS.dma(sy.dma_start(
                    out=ap["r3"][rows, :].rearrange("(a p) d -> p a d", p=128),
                    in_=r3b[:].rearrange("p (a d) -> p a d", d=D)))
                dS.dma(sy.dma_start(
                    out=ap["xpp_h"][rows, :].rearrange("(a p) d -> p a d", p=128),
                    in_=xppb[:].rearrange("p (a d) -> p a d", d=D)))
            st_mark[0] = dS.n

    def _lz_tile(self, i, h2fi, h2T, lg_all, ph2, plg, h2_mark, lgcp_marks):
        c = self.c
        pe, dve, act, gp = (self.sem[k] for k in ["pe", "dve", "act", "gp"])

        @self.sec("T")
        def _(t):
            dve.wait(t, h2_mark)
            act.wait(t)
            for q in range(8):
                _lt = t.transpose(ph2[:, q * 128:(q + 1) * 128],
                                  h2fi[:, q * 128:(q + 1) * 128], c["ident_f"][:])
            pe.inc(_lt)
        tr1 = pe.n

        @self.sec("A")
        def _(a):
            pe.wait(a, tr1)
            a.activation(h2T[:, 0:1024], ph2[:, :], AF.Copy)
            a.drain()
            act.inc(a.nop())
        ev2 = act.n

        @self.sec("T")
        def _(t):
            act.wait(t, ev2)
            gp.wait(t)
            if i > 0:
                dve.wait(t, lgcp_marks[i - 1])
            for kk in range(ND):
                _lt = t.matmul(plg[:, :], h2T[:, kk * 128:(kk + 1) * 128],
                               c["wgate_sb"][:, kk * E:(kk + 1) * E],
                               start=(kk == 0), stop=(kk == ND - 1))
            pe.inc(_lt)
        lg_mark = pe.n

        @self.sec("V")
        def _(v):
            pe.wait(v, lg_mark)
            v.tensor_copy(lg_all[:, i * E:(i + 1) * E], plg[:, :])
            v.drain()
            dve.inc(v.nop())
        lgcp_marks[i] = dve.n

    def st_ln2_logits(self):
        nc, ap, c = self.nc, self.ap, self.c
        dS, pe, dve, act = (self.sem[k] for k in ["dS", "pe", "dve", "act"])
        x4 = self.sbuf("lz_x4", [128, 4 * D])
        h2f4 = self.sbuf("lz_h2f4", [128, 4 * D])
        h2b = self.sbuf("lz_h2b", [128, 4 * D], BF16)
        h2T = self.sbuf("lz_h2T", [128, D])
        lg_all = self.sbuf("lz_lg", [128, NT * E])
        scr = self.sbuf("lz_scr", [128, D])
        sq = self.sbuf("lz_sq", [128, 4])
        sm = self.sbuf("lz_sm", [128, 4])
        m_ = self.sbuf("lz_m", [128, 4])
        v_ = self.sbuf("lz_v", [128, 4])
        r_ = self.sbuf("lz_r", [128, 4])
        tmp = self.sbuf("lz_tmp", [128, 4])
        mk = self.sbuf("lz_mk", [128, NT * E], BF16)
        cnt_sb = self.sbuf("lz_cnt", [1, E])
        crow = self.sbuf("lz_crow", [1, NT * E])
        t2sc = self.mk_top2_scratch("lz", NT)
        self._mk_ln2_consts()
        ph2 = self.psum("lz_ph2", [128, 1024])
        plg = self.psum("lz_plg", [128, E])
        pcnt = self.psum("lz_pcnt", [1, NT * E])
        self.barrier("S", "G")

        @self.sec("G")
        def _(g):
            self.init_top2_iota(g, t2sc, NT)
            self.sem["gp"].inc(g.engine_nop())
        gp = self.sem["gp"]
        lgcp_marks = {}

        h2st_mark = [0]
        for s in range(NT // 4):
            rows4 = slice(s * 512, (s + 1) * 512)

            @self.sec("S")
            def _(sy, s=s):
                if s >= 1:
                    dve.wait(sy, lgcp_marks[4 * s - 1])
                dS.dma(sy.dma_start(
                    out=x4[:].rearrange("p (a d) -> p a d", d=D),
                    in_=ap["xpp_h"][rows4, :].rearrange("(a p) d -> p a d", p=128)))
            ld_mark = dS.n

            @self.sec("A")
            def _(a, s=s, ld_mark=ld_mark):
                dS.wait(a, ld_mark)
                for ii in range(4):
                    a.activation(scr[:], x4[:, ii * D:(ii + 1) * D], AF.Square,
                                 accum_out=sq[:, ii:ii + 1])
                a.drain()
                act.inc(a.nop())
            sq_mark = act.n

            @self.sec("V")
            def _(v, s=s, ld_mark=ld_mark, sq_mark=sq_mark):
                dS.wait(v, ld_mark)
                act.wait(v, sq_mark)
                for ii in range(4):
                    v.reduce_sum(sm[:, ii:ii + 1], x4[:, ii * D:(ii + 1) * D],
                                 axis=AX.X)
                v.drain()
                v.tensor_scalar(m_[:], sm[:], 1.0 / D, None, op0=ALU.mult)
                v.drain()
                v.tensor_tensor(tmp[:], m_[:], m_[:], op=ALU.mult)
                v.tensor_scalar(v_[:], sq[:], 1.0 / D, None, op0=ALU.mult)
                v.drain()
                v.tensor_tensor(v_[:], v_[:], tmp[:], op=ALU.subtract)
                v.drain()
                dve.inc(v.nop())
            var_mark = dve.n

            @self.sec("A")
            def _(a, var_mark=var_mark):
                dve.wait(a, var_mark)
                a.activation(r_[:], v_[:], AF.Ln, bias=c["epscol"][:, 0:1])
                a.drain()
                a.activation(r_[:], r_[:], AF.Exp, scale=-0.5)
                a.drain()
                act.inc(a.nop())
            r_mark = act.n

            @self.sec("V")
            def _(v, s=s, r_mark=r_mark):
                act.wait(v, r_mark)
                for ii in range(4):
                    sl = slice(ii * D, (ii + 1) * D)
                    v.tensor_scalar(h2f4[:, sl], x4[:, sl], m_[:, ii:ii + 1],
                                    r_[:, ii:ii + 1],
                                    op0=ALU.subtract, op1=ALU.mult)
                    v.tensor_tensor(h2f4[:, sl], h2f4[:, sl], c["g2rep"][:],
                                    op=ALU.mult)
                    v.tensor_tensor(h2f4[:, sl], h2f4[:, sl], c["b2rep"][:],
                                    op=ALU.add)
                if h2st_mark[0]:
                    dS.wait(v, h2st_mark[0])
                v.tensor_copy(h2b[:], h2f4[:])
                v.drain()
                dve.inc(v.nop())
            h2_mark = dve.n

            @self.sec("S")
            def _(sy, s=s, h2_mark=h2_mark):
                dve.wait(sy, h2_mark)
                dS.dma(sy.dma_start(
                    out=ap["h2"][rows4, :].rearrange("(a p) d -> p a d", p=128),
                    in_=h2b[:].rearrange("p (a d) -> p a d", d=D)))
            h2st_mark[0] = dS.n

            for i in range(4 * s, 4 * s + 4):
                self._lz_tile(i, h2f4[:, (i % 4) * D:(i % 4 + 1) * D],
                              h2T, lg_all, ph2, plg, h2_mark, lgcp_marks)

        @self.sec("V")
        def _(v):
            self._emit_top2(v, lg_all, NT, t2sc, mk)
            dve.inc(v.nop())
        mkv_mark = dve.n

        @self.sec("T")
        def _(t):
            dve.wait(t, mkv_mark)
            pe.inc(t.matmul(pcnt[:, :], c["ones_col_bf"][:, :], mk[:],
                            start=True, stop=True))
        pct_mark = pe.n

        @self.sec("V")
        def _(v):
            pe.wait(v, pct_mark)
            v.tensor_copy(crow[0:1, :], pcnt[0:1, :])
            v.drain()
            w = NT * E
            while w > E:
                h = w // 2
                v.tensor_tensor(crow[0:1, 0:h], crow[0:1, 0:h], crow[0:1, h:w],
                                op=ALU.add)
                v.drain()
                w = h
            v.tensor_copy(cnt_sb[:, :], crow[0:1, 0:E])
            v.drain()
            dve.inc(v.nop())
        cnt_mark = dve.n

        @self.sec("S")
        def _(sy):
            dve.wait(sy, cnt_mark)
            dS.dma(sy.dma_start(out=ap["counts"][:, :], in_=cnt_sb[:]))
            dS.dma(sy.dma_start(
                out=ap["logits"].rearrange("(a p) e -> p a e", p=128),
                in_=lg_all[:].rearrange("p (a e) -> p a e", e=E)))
            if "mask_my" in ap:
                dS.dma(sy.dma_start(
                    out=ap["mask_my"].rearrange("(a p) e -> p a e", p=128),
                    in_=mk[:].rearrange("p (a e) -> p a e", e=E)))





class L2(Builder):

    CH = [(0, 512), (512, 512), (1024, 512), (1536, 512)]
    RT_EBASE = SLOTB
    RT_CLAMP = NSLOT - 128
    RT_ADD_OFFS = False
    RT_KEEP_ONLY = False
    RT_ZERO = ("x_all", NSLOT, True)

    def __init__(self, nc, dbg=(), upto=None):
        super().__init__(nc, dbg)
        self.upto = upto
        self._declare()

    def _declare(self):
        self.din("r3", [T, D])
        self.din("h2", [T, D], BF16)
        self.din("logits", [T, E])
        self.din("counts_all", [E, E])
        self.din("batch_id", [1, 1])
        self.din("w1", [E, D, FF])
        self.din("b1", [E, FF])
        self.din("w2", [E, FF, D])
        self.din("b2", [E, D])
        self.dout("out", [T, D])
        self.dint("x_all", [NSLOT, D], BF16)
        self.dint("y_all", [NSLOT, D], BF16)
        self.dint("btab", [1, NT * E], I32)

    def build(self):
        self._start(["dS", "dG", "pe", "dve", "act", "gp"])
        stages = ["st_route", "st_dispatch", "st_experts", "st_combine"]
        if self.upto is not None:
            stages = stages[:stages.index(self.upto) + 1]
        with self.es:
            self.grank_all = self.es.enter_context(
                self.nc.sbuf_tensor("l2_grank", [128, NT * E], F32))
            self.gts_all = self.es.enter_context(
                self.nc.sbuf_tensor("l2_gts", [128, NT * E], F32))
            self.c["iotaF"] = self.es.enter_context(
                self.nc.sbuf_tensor("l2_iotaF", [128, 128], F32))
            self.c["identb"] = self.es.enter_context(
                self.nc.sbuf_tensor("l2_identb", [128, 128], BF16))
            self.tld_h = self.es.enter_context(self.nc.semaphore("tld"))
            self.tld_n = 0
            for st in stages:
                with self.stage():
                    getattr(self, st)()
            self.finale()
        return self.ap

    def st_route(self):
        nc, ap, c = self.nc, self.ap, self.c
        dS, dG, pe, dve, act, gp = (self.sem[k] for k in ["dS", "dG", "pe", "dve", "act", "gp"])
        lg_all = self.sbuf("rt_lg", [128, NT * E])
        cnts = self.sbuf("rt_cnts", [E, E])
        bid = self.sbuf("rt_bid", [1, 1])
        bidr = self.sbuf("rt_bidr", [E, 1])
        sel = self.sbuf("rt_sel", [E, 1])
        offs = self.sbuf("rt_offs", [128, E])
        iota8 = self.sbuf("rt_iota8", [128, E])
        iotaC = self.sbuf("rt_iotaC", [128, E])
        ebase = self.sbuf("rt_ebase", [128, E])
        iotaP = self.sbuf("rt_iotaP", [E, 1])
        m1 = self.sbuf("rt_m1", [128, 1])
        e1 = self.sbuf("rt_e1", [128, 1])
        eq1 = self.sbuf("rt_eq1", [128, E])
        eq2 = self.sbuf("rt_eq2", [128, E])
        lgm = self.sbuf("rt_lgm", [128, E])
        gts = self.sbuf("rt_gts", [128, E])
        sume = self.sbuf("rt_sume", [128, 1])
        negm = self.sbuf("rt_negm", [128, 1])
        maskc = self.sbuf("rt_maskc", [128, E], BF16)
        maskf = self.sbuf("rt_maskf", [128, E])
        keepm = self.sbuf("rt_keepm", [128, E])
        pos_ = self.sbuf("rt_pos", [128, E])
        t3 = self.sbuf("rt_t3", [128, E])
        t4 = self.sbuf("rt_t4", [128, E])
        kf = self.sbuf("rt_kf", [128, E])
        rel = self.sbuf("rt_rel", [128, E])
        tb = self.sbuf("rt_tb", [128, E])
        relbc = self.sbuf("rt_relbc", [128, E])
        btab_row = self.sbuf("rt_btrow", [1, NT * E])
        btab_i32 = self.sbuf("rt_bti", [1, NT * E], I32)
        if getattr(self, "RT_SPLIT_Y", False):
            ebA = self.sbuf("rt_ebA", [128, E])
            ebB = self.sbuf("rt_ebB", [128, E])
            selr = self.sbuf("rt_selr", [1, E])
            tra = self.sbuf("rt_tra", [1, E])
            trb = self.sbuf("rt_trb", [1, E])
        zt = self.sbuf("rt_z", [128, 2 * D], BF16)
        sc1 = self.sbuf("rt_sc1", [128, 1])
        ustrict = self.sbuf("rt_ustrict", [128, 128], BF16)
        lincl = self.sbuf("rt_lincl", [128, 128], BF16)
        _mk_ones = "ones_row1f" not in c
        if _mk_ones:
            c["ones_row1f"] = self.sbuf("rt_ones1f", [1, 128], F32)
        pcnt = self.psum("rt_pcnt", [128, E])
        poff = self.psum("rt_poff", [128, E])

        @self.sec("S")
        def _(sy):
            dS.dma(sy.dma_start(
                out=lg_all[:].rearrange("p (a e) -> p a e", e=E),
                in_=ap["logits"].rearrange("(a p) e -> p a e", p=128)))
            dS.dma(sy.dma_start(out=cnts[:], in_=ap["counts_all"][:, :]))
            dS.dma(sy.dma_start(out=bid[:], in_=ap["batch_id"][:, :]))
        ld_mark = dS.n

        @self.sec("G")
        def _(g):
            g.iota(iota8[:], pattern=[[1, E]], base=0, channel_multiplier=0,
                   allow_small_or_imprecise_dtypes=True)
            g.iota(iotaC[:], pattern=[[-1, E]], base=E, channel_multiplier=0,
                   allow_small_or_imprecise_dtypes=True)
            g.iota(ebase[:], pattern=[[self.RT_EBASE, E]], base=0, channel_multiplier=0,
                   allow_small_or_imprecise_dtypes=True)
            if getattr(self, "RT_SPLIT_Y", False):
                g.iota(ebA[:], pattern=[[YSPLIT, E]], base=0, channel_multiplier=0,
                       allow_small_or_imprecise_dtypes=True)
                g.iota(ebB[:], pattern=[[YBROWS, E]], base=YAOFF - (YSPLIT - 128),
                       channel_multiplier=0, allow_small_or_imprecise_dtypes=True)
            g.iota(iotaP[:], pattern=[[0, 1]], base=0, channel_multiplier=1,
                   allow_small_or_imprecise_dtypes=True)
            g.memset(ustrict[:], 0.0)
            g.affine_select(out=ustrict[:], in_=ustrict[:], compare_op=ALU.is_ge,
                            fill=1.0, base=0, pattern=[[-1, 128]], channel_multiplier=1)
            if _mk_ones:
                g.memset(c["ones_row1f"][:], 1.0)
            g.memset(lincl[:], 0.0)
            g.affine_select(out=lincl[:], in_=lincl[:], compare_op=ALU.is_ge,
                            fill=1.0, base=-1, pattern=[[1, 128]], channel_multiplier=-1)
            g.iota(c["iotaF"][:], pattern=[[1, 128]], base=0, channel_multiplier=0,
                   allow_small_or_imprecise_dtypes=True)
            make_identity(self.nc, g, c["identb"][:])
            g.memset(zt[:], 0.0)
            gp.inc(g.engine_nop())

        zname, zrows, zpad = self.RT_ZERO

        @self.sec("S")
        def _(sy):
            gp.wait(sy)
            for r in range(0, (zrows // 256) * 256, 256):
                dS.dma(sy.dma_start(
                    out=ap[zname][r:r + 256, :].rearrange("(a p) d -> p a d", p=128),
                    in_=zt[:].rearrange("p (a d) -> p a d", d=D)))
            if zrows % 256:
                dS.dma(sy.dma_start(out=ap[zname][(zrows // 256) * 256:zrows, :],
                                    in_=zt[:, 0:D]))
            if zpad:
                dS.dma(sy.dma_start(out=ap["y_all"][E * SLOTB:NSLOT, :], in_=zt[:, 0:D]))

        pb1 = self.psum("rt_pb1", [E, 1])

        @self.sec("T")
        def _(t):
            dS.wait(t, ld_mark)
            gp.wait(t)
            pe.inc(t.matmul(pb1[0:E, :], c["ones_row1f"][0:1, 0:E], bid[:, :],
                            start=True, stop=True))
        bid_mark = pe.n

        @self.sec("V")
        def _(v):
            pe.wait(v, bid_mark)
            v.tensor_copy(bidr[:], pb1[0:E, :])
            v.drain()
            v.tensor_scalar(sel[:], iotaP[:], bidr[:, 0:1], None, op0=ALU.is_lt)
            v.drain()
            dve.inc(v.nop())
        sel_mark = dve.n

        @self.sec("T")
        def _(t):
            dve.wait(t, sel_mark)
            pe.inc(t.matmul(poff[0:1, :], sel[:, :], cnts[:, :], start=True, stop=True))
        offr_mark = pe.n

        pb2 = self.psum("rt_pb2", [128, E])

        @self.sec("V")
        def _(v):
            pe.wait(v, offr_mark)
            v.tensor_copy(offs[0:1, :], poff[0:1, :])
            v.drain()
            dve.inc(v.nop())
        offrow_mark = dve.n

        @self.sec("T")
        def _(t):
            dve.wait(t, offrow_mark)
            pe.inc(t.matmul(pb2[:, :], c["ones_row1f"][:, :], offs[0:1, :],
                            start=True, stop=True))
        offb_mark = pe.n

        @self.sec("V")
        def _(v):
            pe.wait(v, offb_mark)
            v.tensor_copy(offs[:], pb2[:, :])
            v.drain()
            dve.inc(v.nop())
        offs_mark = dve.n

        import os
        if os.environ.get("RT_CUT") == "1":
            return
        prev_rel_mark = [0]
        for i in range(NT):
            @self.sec("V")
            def _(v, i=i):
                if i == 0:
                    dve.wait(v, offs_mark)
                lg = lg_all[:, i * E:(i + 1) * E]
                v.reduce_max(m1[:], lg, axis=AX.X)
                v.drain()
                v.tensor_scalar(negm[:], m1[:], -1.0, None, op0=ALU.mult)
                v.drain()
                dve.inc(v.nop())
            nm_mark = dve.n

            @self.sec("A")
            def _(a, i=i, nm_mark=nm_mark):
                dve.wait(a, nm_mark)
                lg = lg_all[:, i * E:(i + 1) * E]
                a.activation(gts[:], lg, AF.Exp, bias=negm[:, 0:1], accum_out=sume[:])
                a.drain()
                act.inc(a.nop())
            exp_mark = act.n

            @self.sec("V")
            def _(v, i=i, exp_mark=exp_mark):
                act.wait(v, exp_mark)
                lg = lg_all[:, i * E:(i + 1) * E]
                v.reciprocal(sc1[:], sume[:])
                v.drain()
                v.tensor_scalar(gts[:], gts[:], sc1[:, 0:1], None, op0=ALU.mult)
                v.tensor_scalar(eq1[:], lg, m1[:, 0:1], None, op0=ALU.is_equal)
                v.drain()
                v.tensor_tensor(t3[:], eq1[:], iotaC[:], op=ALU.mult)
                v.drain()
                v.reduce_max(e1[:], t3[:], axis=AX.X)
                v.drain()
                v.tensor_scalar(e1[:], e1[:], -1.0, float(E), op0=ALU.mult, op1=ALU.add)
                v.drain()
                v.tensor_scalar(eq1[:], iota8[:], e1[:, 0:1], None, op0=ALU.is_equal)
                v.drain()
                v.scalar_tensor_tensor(lgm[:], eq1[:], -1e9, lg, op0=ALU.mult, op1=ALU.add)
                v.drain()
                v.reduce_max(m1[:], lgm[:], axis=AX.X)
                v.drain()
                v.tensor_scalar(eq2[:], lgm[:], m1[:, 0:1], None, op0=ALU.is_equal)
                v.drain()
                v.tensor_tensor(t3[:], eq2[:], iotaC[:], op=ALU.mult)
                v.drain()
                v.reduce_max(e1[:], t3[:], axis=AX.X)
                v.drain()
                v.tensor_scalar(e1[:], e1[:], -1.0, float(E), op0=ALU.mult, op1=ALU.add)
                v.drain()
                v.tensor_scalar(eq2[:], iota8[:], e1[:, 0:1], None, op0=ALU.is_equal)
                v.drain()
                v.tensor_tensor(maskf[:], eq1[:], eq2[:], op=ALU.add)
                v.drain()
                v.tensor_copy(maskc[:], maskf[:])
                v.drain()
                dve.inc(v.nop())
            mk_mark = dve.n

            @self.sec("T")
            def _(t, i=i, mk_mark=mk_mark):
                dve.wait(t, mk_mark)
                pe.inc(t.matmul(pcnt[:, :], ustrict[:], maskc[:],
                                start=(i == 0), stop=False, skip_group_check=True))
            pos_mark = pe.n

            @self.sec("V")
            def _(v, i=i, pos_mark=pos_mark):
                pe.wait(v, pos_mark)
                v.tensor_copy(pos_[:], pcnt[:, :])
                v.drain()
                dve.inc(v.nop())
            posr_mark = dve.n

            @self.sec("T")
            def _(t, i=i, posr_mark=posr_mark):
                dve.wait(t, posr_mark)
                pe.inc(t.matmul(pcnt[:, :], lincl[:], maskc[:],
                                start=False, stop=(i == NT - 1), skip_group_check=True))

            @self.sec("V")
            def _(v, i=i):
                v.tensor_tensor(t3[:], pos_[:], offs[:], op=ALU.add)
                v.drain()
                v.tensor_scalar(t3[:], t3[:], float(CAPACITY), None, op0=ALU.is_lt)
                v.drain()
                v.tensor_tensor(keepm[:], t3[:], maskf[:], op=ALU.mult)
                v.drain()
                dve.inc(v.nop())
            keep_mark = dve.n

            @self.sec("V")
            def _(v, i=i):
                if self.RT_KEEP_ONLY:
                    v.tensor_copy(kf[:], keepm[:])
                    v.drain()
                else:
                    v.tensor_scalar(t3[:], pos_[:], float(SLOTB), None, op0=ALU.is_lt)
                    v.drain()
                    v.tensor_tensor(kf[:], t3[:], keepm[:], op=ALU.mult)
                    v.drain()
                dve.inc(v.nop())
            kf_mark = dve.n

            @self.sec("T")
            def _(t, i=i, kf_mark=kf_mark, prm=prev_rel_mark[0]):
                dve.wait(t, kf_mark)
                if prm:
                    dve.wait(t, prm)
                pe.inc(t.matmul(pb2[:, :], c["ones_row1f"][0:1, :], pos_[0:1, :],
                                start=True, stop=True))
            bc_mark = pe.n

            @self.sec("V")
            def _(v, i=i, bc_mark=bc_mark):
                pe.wait(v, bc_mark)
                v.tensor_copy(relbc[:], pb2[:, :])
                v.drain()
                v.tensor_tensor(rel[:], pos_[:], relbc[:], op=ALU.subtract)
                bte = btab_row[0:1, E * i:E * (i + 1)]
                if getattr(self, "RT_SPLIT_Y", False):
                    v.tensor_tensor(bte, relbc[0:1, :], offs[0:1, :], op=ALU.add)
                    v.drain()
                    v.tensor_scalar(selr[0:1, :], bte, float(YSPLIT - 127), None,
                                    op0=ALU.is_lt)
                    v.tensor_tensor(tra[0:1, :], bte, ebA[0:1, :], op=ALU.add)
                    v.tensor_tensor(trb[0:1, :], bte, ebB[0:1, :], op=ALU.add)
                    v.drain()
                    v.tensor_tensor(tra[0:1, :], tra[0:1, :], trb[0:1, :],
                                    op=ALU.subtract)
                    v.drain()
                    v.tensor_tensor(tra[0:1, :], tra[0:1, :], selr[0:1, :],
                                    op=ALU.mult)
                    v.drain()
                    v.tensor_tensor(bte, trb[0:1, :], tra[0:1, :], op=ALU.add)
                    v.drain()
                else:
                    v.tensor_tensor(bte, relbc[0:1, :], ebase[0:1, :], op=ALU.add)
                    v.drain()
                    if self.RT_ADD_OFFS:
                        v.tensor_tensor(bte, bte, offs[0:1, :], op=ALU.add)
                        v.drain()
                v.tensor_scalar(bte, bte, float(self.RT_CLAMP), None, op0=ALU.min)
                v.tensor_scalar(tb[:], kf[:], -BIGRANK, BIGRANK,
                                op0=ALU.mult, op1=ALU.add)
                v.tensor_tensor(t4[:], rel[:], kf[:], op=ALU.mult)
                v.drain()
                v.tensor_tensor(self.grank_all[:, E * i:E * (i + 1)], t4[:], tb[:],
                                op=ALU.add)
                v.tensor_copy(self.gts_all[:, E * i:E * (i + 1)], gts[:])
                v.drain()
                dve.inc(v.nop())
            prev_rel_mark[0] = dve.n

        @self.sec("V")
        def _(v):
            v.tensor_copy(btab_i32[:], btab_row[:])
            v.drain()
            dve.inc(v.nop())
        bt_mark = self.sem["dve"].n

        @self.sec("S")
        def _(sy):
            dve.wait(sy, bt_mark)
            dS.dma(sy.dma_start(out=ap["btab"][:, :], in_=btab_i32[:]))

    def st_dispatch(self):
        nc, ap, c = self.nc, self.ap, self.c
        dS, dG, pe, dve, act, gp = (self.sem[k] for k in ["dS", "dG", "pe", "dve", "act", "gp"])
        h2t = [self.sbuf(f"dp_h2_{s}", [128, D], BF16) for s in range(2)]
        gtmp = self.sbuf("dp_gt", [128, 128])
        gd = [self.sbuf(f"dp_gd_{s}", [128, 128], BF16) for s in range(2)]
        xstage = self.sbuf("dp_xs", [128, RING * D], BF16)
        bcur = self.sbuf("dp_bcur", [1, RING], I32)
        pp = [self.psum(f"dp_pp_{s}", [128, D]) for s in range(2)]
        tld_h = self.tld_h
        tld_base = self.tld_n
        act_base = act.n
        dG_base = dG.n
        h2_marks, gb_marks, mm_marks, ev_marks = {}, {}, {}, {}
        for i in range(NT):
            @self.sec("S")
            def _(sy, i=i):
                if i >= 2:
                    pe.wait(sy, mm_marks[(i - 2) * E + E - 1])
                dS.dma(sy.dma_start(out=h2t[i % 2][:],
                                    in_=ap["h2"][i * 128:(i + 1) * 128, :]))
            h2_marks[i] = dS.n

            for e in range(E):
                g_ = i * E + e

                @self.sec("V")
                def _(v, g_=g_):
                    if g_ >= 2:
                        pe.wait(v, mm_marks[g_ - 2])
                    v.tensor_scalar(gtmp[:], c["iotaF"][:],
                                    self.grank_all[:, g_:g_ + 1], None,
                                    op0=ALU.is_equal)
                    v.drain()
                    v.tensor_copy(gd[g_ % 2][:], gtmp[:])
                    v.drain()
                    dve.inc(v.nop())
                gb_marks[g_] = dve.n

                @self.sec("T")
                def _(t, i=i, g_=g_):
                    dS.wait(t, h2_marks[i])
                    dve.wait(t, gb_marks[g_])
                    if g_ >= 2:
                        act.wait(t, ev_marks[g_ - 2])
                    t.matmul(pp[g_ % 2][:, 0:512], gd[g_ % 2][:],
                             h2t[i % 2][:, 0:512], start=True, stop=True)
                    _lt = t.matmul(pp[g_ % 2][:, 512:1024], gd[g_ % 2][:],
                                   h2t[i % 2][:, 512:1024], start=True, stop=True)
                    pe.inc(_lt)
                mm_marks[g_] = pe.n

                @self.sec("A")
                def _(a, g_=g_):
                    pe.wait(a, mm_marks[g_])
                    if g_ >= RING:
                        a.wait_ge(dG.h, dG_base + 16 * (g_ - RING + 1))
                    a.activation(xstage[:, (g_ % RING) * D:(g_ % RING + 1) * D],
                                 pp[g_ % 2][:], AF.Identity)
                    a.drain()
                    act.inc(a.nop())
                ev_marks[g_] = act.n

        regs = [nc.alloc_registers(f"dpr{k}", engines=[ET.Pool]).handles[0]
                for k in range(RING)]
        vals = [RuntimeValue(r, min_val=0, max_val=NSLOT - 128) for r in regs]

        @self.sec("G")
        def _(g):
            with g.Fori(0, NT) as ih:
                g.dma_start(out=bcur[0:1, 0:RING],
                            in_=ap["btab"][0:1, DynSlice(ih * RING, RING)]
                            ).then_inc(tld_h, 16)
                g.wait_ge(tld_h, tld_base + (ih + 1) * 16)
                g.wait_ge(act.h, act_base + (ih + 1) * RING)
                for k in range(RING):
                    nc.reg_load(regs[k], bcur[0:1, k:k + 1])
                    g.dma_start(out=ap["x_all"][DynSlice(vals[k], 128), :],
                                in_=xstage[:, k * D:(k + 1) * D],
                                accum_op=ALU.add).then_inc(dG.h, 16)
        dG.n += 16 * RING * NT
        self.tld_n += 16 * NT

    def st_experts(self):
        nc, ap, c = self.nc, self.ap, self.c
        dS, dG, pe, dve, act, gp = (self.sem[k] for k in ["dS", "dG", "pe", "dve", "act", "gp"])
        w1sb = self.sbuf("ex_w1", [128, ND * 2048], BF16)
        w2sb = self.sbuf("ex_w2", [128, NF * D], BF16)
        b1row = self.sbuf("ex_b1", [1, FF], BF16)
        b2row = self.sbuf("ex_b2", [1, D], BF16)
        onesr = self.sbuf("ex_ones", [1, 512], BF16)
        xet = self.sbuf("ex_xet", [128, ND * 640], BF16)
        midt = self.sbuf("ex_mid", [128, NF * 640], BF16)
        ysb = self.sbuf("ex_y", [128, 5 * D], BF16)
        pg = [self.psum(f"ex_pg{i}", [128, 512]) for i in range(2)]

        @self.sec("G")
        def _(g):
            g.memset(onesr[:], 1.0)
            gp.inc(g.engine_nop())

        prev_done = {"mid": 0, "y": 0, "xet": 0, "w": 0}
        for e in range(E):
            @self.sec("G")
            def _(g, e=e):
                if e > 0:
                    pe.wait(g)
                for kk in range(NF):
                    dG.dma(g.dma_start(out=w2sb[:, kk * D:(kk + 1) * D],
                                       in_=ap["w2"][e, kk * 128:(kk + 1) * 128, :]))
                dG.dma(g.dma_start(out=b1row[:], in_=ap["b1"][e:e + 1, :]))
                dG.dma(g.dma_start(out=b2row[:], in_=ap["b2"][e:e + 1, :]))
            w_mark = dG.n

            for (cb, csz) in self.CH:
                nst = csz // 128
                @self.sec("S")
                def _(sy, e=e, cb=cb, csz=csz):
                    pe.wait(sy)
                    dG.wait(sy)
                    for kk in range(ND):
                        dS.dma(sy.dma_start(
                            out=xet[:, kk * csz:(kk + 1) * csz],
                            in_=ap["x_all"][e * SLOTB + cb:e * SLOTB + cb + csz,
                                            kk * 128:(kk + 1) * 128],
                            transpose=True))
                x_mark = dS.n

                nch = (csz + 511) // 512
                for half in range(2):
                    @self.sec("G")
                    def _(g, e=e, half=half):
                        pe.wait(g)
                        for kk in range(ND):
                            dG.dma(g.dma_start(
                                out=w1sb[:, kk * 2048:(kk + 1) * 2048],
                                in_=ap["w1"][e, kk * 128:(kk + 1) * 128,
                                             half * 2048:(half + 1) * 2048]))
                    w1_mark = dG.n
                    for mtl in range(16):
                        mt = half * 16 + mtl

                        @self.sec("T")
                        def _(t, e=e, mt=mt, mtl=mtl, csz=csz, nch=nch,
                              x_mark=x_mark, w_mark=w_mark, w1_mark=w1_mark):
                            dS.wait(t, x_mark)
                            dG.wait(t, max(w_mark, w1_mark))
                            act.wait(t)
                            for ci in range(nch):
                                cw = min(512, csz - ci * 512)
                                ps = pg[ci % 2]
                                for kk in range(ND):
                                    _lt = t.matmul(
                                        ps[:, 0:cw],
                                        w1sb[:, kk * 2048 + mtl * 128:kk * 2048 + (mtl + 1) * 128],
                                        xet[:, kk * csz + ci * 512:kk * csz + ci * 512 + cw],
                                        start=(kk == 0), stop=False)
                                _lt = t.matmul(ps[:, 0:cw],
                                               b1row[0:1, mt * 128:(mt + 1) * 128],
                                               onesr[0:1, 0:cw], start=False, stop=True)
                            pe.inc(_lt)
                        mm_mark = pe.n

                        @self.sec("A")
                        def _(a, mt=mt, csz=csz, nch=nch, mm_mark=mm_mark):
                            pe.wait(a, mm_mark)
                            for ci in range(nch):
                                cw = min(512, csz - ci * 512)
                                ps = pg[ci % 2]
                                a.activation(midt[:, mt * csz + ci * 512:mt * csz + ci * 512 + cw],
                                             ps[:, 0:cw], AF.Gelu)
                            a.drain()
                            act.inc(a.nop())

                gelu_done = self.sem["act"].n
                for st in range(nst):
                    for c2 in range(2):
                        @self.sec("T")
                        def _(t, e=e, st=st, c2=c2, csz=csz, gelu_done=gelu_done):
                            act.wait(t, gelu_done)
                            dve.wait(t)
                            ps = pg[c2]
                            for kk in range(NF):
                                _lt = t.matmul(
                                    ps[:, :],
                                    midt[:, kk * csz + st * 128:kk * csz + (st + 1) * 128],
                                    w2sb[:, kk * D + c2 * 512:kk * D + (c2 + 1) * 512],
                                    start=(kk == 0), stop=False)
                            _lt = t.matmul(ps[:, :], onesr[0:1, 0:128],
                                           b2row[0:1, c2 * 512:(c2 + 1) * 512],
                                           start=False, stop=True)
                            pe.inc(_lt)
                        y_mark = pe.n

                        @self.sec("V")
                        def _(v, st=st, c2=c2, y_mark=y_mark):
                            pe.wait(v, y_mark)
                            v.tensor_copy(ysb[:, st * D + c2 * 512:st * D + (c2 + 1) * 512],
                                          pg[c2][:, :])
                            v.drain()
                            dve.inc(v.nop())
                ych_mark = self.sem["dve"].n

                @self.sec("S")
                def _(sy, e=e, cb=cb, csz=csz, nst=nst, ych_mark=ych_mark):
                    dve.wait(sy, ych_mark)
                    dS.dma(sy.dma_start(
                        out=ap["y_all"][e * SLOTB + cb:e * SLOTB + cb + csz, :]
                        .rearrange("(a p) d -> p a d", p=128),
                        in_=ysb[:, 0:nst * D].rearrange("p (a d) -> p a d", d=D)))

    def st_combine(self):
        nc, ap, c = self.nc, self.ap, self.c
        dS, dG, pe, dve, act, gp = (self.sem[k] for k in ["dS", "dG", "pe", "dve", "act", "gp"])
        ystage = self.sbuf("cm_ys", [128, RING * D], BF16)
        gtmp = self.sbuf("cm_gt", [128, 128])
        gtmp2 = self.sbuf("cm_gt2", [128, 128])
        NDEEP = 4
        gc = [self.sbuf(f"cm_gc_{s}", [128, 128], BF16) for s in range(NDEEP)]
        gct = [self.sbuf(f"cm_gct_{s}", [128, 128], BF16) for s in range(NDEEP)]
        r3r = [self.sbuf(f"cm_r3_{s}", [128, D]) for s in range(2)]
        ost = [self.sbuf(f"cm_o_{s}", [128, D]) for s in range(2)]
        bcur = self.sbuf("cm_bcur", [1, RING], I32)
        pp = [self.psum(f"cm_pp_{s}", [128, D]) for s in range(2)]
        ptr = [self.psum(f"cm_ptr_{s}", [128, 1024], BF16) for s in range(NDEEP)]
        tld_h = self.tld_h
        tld_base = self.tld_n
        dG_base = dG.n
        pe_base = pe.n

        regs = [nc.alloc_registers(f"cmr{k}", engines=[ET.Activation]).handles[0]
                for k in range(RING)]
        vals = [RuntimeValue(r, min_val=0, max_val=self.RT_CLAMP) for r in regs]
        ysrc = getattr(self, "CM_YSRC", "y_all")

        @self.sec("A")
        def _(a):
            cc2 = getattr(self, "cc2_h", None)
            if cc2 is not None:
                a.wait_ge(cc2, 4)
            with a.Fori(0, NT) as ih:
                a.dma_start(out=bcur[0:1, 0:RING],
                            in_=ap["btab"][0:1, DynSlice(ih * RING, RING)]
                            ).then_inc(tld_h, 16)
                a.wait_ge(tld_h, tld_base + (ih + 1) * 16)
                a.wait_ge(pe.h, pe_base + ih * 2 * RING)
                for k in range(RING):
                    nc.reg_load(regs[k], bcur[0:1, k:k + 1])
                    a.dma_start(out=ystage[:, k * D:(k + 1) * D],
                                in_=ap[ysrc][DynSlice(vals[k], 128), :]
                                ).then_inc(dG.h, 16)
        dG.n += 16 * RING * NT
        self.tld_n += 16 * NT

        r3_marks, gb_marks, tr_marks, ge_marks, mm_marks = {}, {}, {}, {}, {}
        out_marks, cons_marks = {}, {}
        for i in range(NT):
            @self.sec("S")
            def _(sy, i=i):
                if i >= 2:
                    dve.wait(sy, cons_marks[i - 2])
                dS.dma(sy.dma_start(out=r3r[i % 2][:],
                                    in_=ap["r3"][i * 128:(i + 1) * 128, :]))
            r3_marks[i] = dS.n

            for e in range(E):
                jj = i * E + e

                @self.sec("V")
                def _(v, jj=jj):
                    if jj >= NDEEP:
                        pe.wait(v, tr_marks[jj - NDEEP])
                    v.tensor_scalar(gtmp[:], c["iotaF"][:],
                                    self.grank_all[:, jj:jj + 1], None,
                                    op0=ALU.is_equal)
                    v.drain()
                    v.tensor_scalar(gtmp2[:], gtmp[:],
                                    self.gts_all[:, jj:jj + 1], None,
                                    op0=ALU.mult)
                    v.drain()
                    v.tensor_copy(gc[jj % NDEEP][:], gtmp2[:])
                    v.drain()
                    dve.inc(v.nop())
                gb_marks[jj] = dve.n

                @self.sec("T")
                def _(t, jj=jj):
                    dve.wait(t, gb_marks[jj])
                    if jj >= NDEEP:
                        dve.wait(t, ge_marks[jj - NDEEP])
                    pe.inc(t.transpose(ptr[jj % NDEEP][:, 0:128], gc[jj % NDEEP][:], c["identb"][:]))
                tr_marks[jj] = pe.n

                @self.sec("V")
                def _(v, jj=jj):
                    pe.wait(v, tr_marks[jj])
                    if jj >= NDEEP:
                        pe.wait(v, mm_marks[jj - NDEEP])
                    v.tensor_copy(gct[jj % NDEEP][:], ptr[jj % NDEEP][:, 0:128])
                    v.drain()
                    dve.inc(v.nop())
                ge_marks[jj] = dve.n

                @self.sec("T")
                def _(t, i=i, e=e, jj=jj):
                    dve.wait(t, ge_marks[jj])
                    t.wait_ge(dG.h, dG_base + 16 * RING * (jj // RING + 1))
                    if e == 0 and i >= 2:
                        dve.wait(t, cons_marks[i - 2])
                    ys = ystage[:, (jj % RING) * D:(jj % RING + 1) * D]
                    t.matmul(pp[i % 2][:, 0:512], gct[jj % NDEEP][:], ys[:, 0:512],
                             start=(e == 0), stop=(e == E - 1))
                    _lt = t.matmul(pp[i % 2][:, 512:1024], gct[jj % NDEEP][:],
                                   ys[:, 512:1024],
                                   start=(e == 0), stop=(e == E - 1))
                    pe.inc(_lt)
                mm_marks[jj] = pe.n

            @self.sec("V")
            def _(v, i=i):
                pe.wait(v, mm_marks[i * E + E - 1])
                dS.wait(v, r3_marks[i])
                if i >= 2:
                    dS.wait(v, out_marks[i - 2])
                v.tensor_tensor(ost[i % 2][:], pp[i % 2][:], r3r[i % 2][:],
                                op=ALU.add)
                v.drain()
                dve.inc(v.nop())
            cons_marks[i] = dve.n

            @self.sec("S")
            def _(sy, i=i):
                dve.wait(sy, cons_marks[i])
                dS.dma(sy.dma_start(out=ap["out"][i * 128:(i + 1) * 128, :],
                                    in_=ost[i % 2][:]))
            out_marks[i] = dS.n



NTOK = B * T
NTILE = NTOK // 128
CAP = CAPACITY
ESTRIDE = CAP + 128
YG = E * ESTRIDE
YSPLIT = 4096
YBROWS = ESTRIDE - (YSPLIT - 128)
YAOFF = E * YSPLIT
YG2 = YAOFF + E * YBROWS


class MK(Builder):

    CH = L2.CH
    RT_EBASE = ESTRIDE
    RT_CLAMP = YG2 - 128
    RT_ADD_OFFS = True
    RT_KEEP_ONLY = True
    RT_SPLIT_Y = True
    RT_ZERO = ("x_ex", CAP + 256, False)
    CM_YSRC = "y_gat2"

    def __init__(self, nc, dbg=(), upto=None):
        super().__init__(nc, dbg)
        self.upto = upto
        self._declare()

    def _declare(self):
        self.din("x", [T, D])
        self.din("wz", [D, D]); self.din("bz", [D])
        self.din("wh", [D, D]); self.din("bh", [D])
        self.din("logd", [D])
        self.din("ln1g", [D]); self.din("ln1b", [D])
        self.din("wqkv", [D, 3 * D]); self.din("wproj", [D, D])
        self.din("ln2g", [D]); self.din("ln2b", [D]); self.din("wgate", [D, E])
        self.din("batch_id", [1, 1])
        self.din("w1", [D, FF])
        self.din("b1", [FF])
        self.din("w2", [FF, D])
        self.din("b2", [D])
        self.dout("out", [T, D])
        self.dint("xn_h", [T, D], BF16)
        self.dint("h_h", [D, T])
        self.dint("hd_h", [T, D])
        self.dint("r2_h", [T, D])
        self.dint("xp_h", [T, D])
        self.dint("xpp_h", [T, D])
        self.dint("h1_h", [T, D], BF16)
        self.dint("qk_h", [2 * D, T], BF16)
        self.dint("v_h", [T, D], BF16)
        self.dint("r3", [T, D])
        self.dint("h2", [T, D], BF16)
        self.dint("logits", [T, E])
        self.dint("counts", [1, E])
        self.dshr("counts_all", [E, E])
        self.dint("mask_my", [T, E], BF16)
        self.dshr("mask_all", [NTOK, E], BF16)
        self.dshr("h2_all", [NTOK, D], BF16)
        self.dint("x_ex", [CAP + 256, D], BF16)
        self.dint("y_ex", [ESTRIDE, D], BF16)
        self.dshr("y_gat2", [YG2, D], BF16)
        self.dint("btab", [1, NT * E], I32)
        self.dint("btab2", [1, NTILE // 2], I32)

    constants = L1.constants
    _mk_rope_tables = L1._mk_rope_tables
    _mk_cmask = L1._mk_cmask
    _mk_ln2_consts = L1._mk_ln2_consts
    st_ln_outer = L1.st_ln_outer
    st_mingru = L1.st_mingru
    st_diffusion = L1.st_diffusion
    st_resid_ln = L1.st_resid_ln
    st_qkv = L1.st_qkv
    st_attention = L1.st_attention
    st_proj_r3 = L1.st_proj_r3
    st_ln2_logits = L1.st_ln2_logits
    _lz_tile = L1._lz_tile
    st_route = L2.st_route
    st_combine = L2.st_combine

    def st_gather_moe(self):
        cc = self.sem["cc"]
        rg = [list(range(B))]

        @self.sec("G")
        def _(g):
            cc.inc(g.collective_compute(
                "AllGather", ALU.bypass, replica_groups=rg,
                ins=[self.ap["counts"].opt()], outs=[self.ap["counts_all"].opt()]))
            cc.wait(g)
            g.collective_compute(
                "AllGather", ALU.bypass, replica_groups=rg,
                ins=[self.ap["mask_my"].opt()],
                outs=[self.ap["mask_all"].opt()]).then_inc(self.cc2_h, 1)
            g.collective_compute(
                "AllGather", ALU.bypass, replica_groups=rg,
                ins=[self.ap["h2"].opt()],
                outs=[self.ap["h2_all"].opt()]).then_inc(self.cc2_h, 1)

    def st_dispatch_ex(self):
        nc, ap, c = self.nc, self.ap, self.c
        dS, dG, pe, dve, act, gp = (self.sem[k] for k in
                                    ["dS", "dG", "pe", "dve", "act", "gp"])
        msb = self.sbuf("dx_msb", [128, NTILE * E], BF16)
        mall = self.sbuf("dx_mall", [128, NTILE * E])
        iotaEr = self.sbuf("dx_iotaEr", [128, NTILE * E])
        me1 = self.sbuf("dx_me", [128, 1])
        mcolP = self.sbuf("dx_mcolP", [128, NTILE])
        mcol = self.sbuf("dx_mcol", [128, NTILE])
        mcolb = self.sbuf("dx_mcolb", [128, NTILE], BF16)
        posin = self.sbuf("dx_posin", [128, NTILE])
        tsrow = self.sbuf("dx_tsrow", [1, NTILE])
        tsT = self.sbuf("dx_tsT", [128, 2], BF16)
        pbase = self.sbuf("dx_pbase", [128, 2])
        tot0 = self.sbuf("dx_tot0", [128, 1])
        brow = self.sbuf("dx_brow", [1, NTILE])
        bbc = self.sbuf("dx_bbc", [128, NTILE])
        posg = self.sbuf("dx_posg", [128, NTILE])
        keepg = self.sbuf("dx_keepg", [128, NTILE])
        grankd = self.sbuf("dx_grankd", [128, NTILE])
        t5 = self.sbuf("dx_t5", [128, NTILE])
        thalf = self.sbuf("dx_thalf", [1, 2])
        ttb = self.sbuf("dx_ttb", [128, NTILE])
        pairadd = self.sbuf("dx_pairadd", [128, NTILE])
        btab2_row = self.sbuf("dx_bt2", [1, NTILE // 2])
        btab2_i32 = self.sbuf("dx_bt2i", [1, NTILE // 2], I32)
        tri1 = self.sbuf("dx_tri1", [128, 128], BF16)
        bid = self.sbuf("dx_bid", [1, 1])
        self._dx_ps = contextlib.ExitStack()
        _ps = lambda nm, shape: self._dx_ps.enter_context(
            self.nc.psum_tensor(nm, shape, F32))
        ppos = _ps("dx_ppos", [128, NTILE])
        pts = _ps("dx_pts", [1, NTILE])
        pcol = _ps("dx_pcol", [128, 2])
        prow1 = _ps("dx_prow1", [1, 128])
        pb128 = _ps("dx_pb128", [128, 1])

        @self.sec("G")
        def _(g):
            g.iota(iotaEr[:].rearrange("p (a e) -> p a e", e=E),
                   pattern=[[0, NTILE], [1, E]], base=0, channel_multiplier=0,
                   allow_small_or_imprecise_dtypes=True)
            g.memset(tri1[:], 0.0)
            g.affine_select(out=tri1[:], in_=tri1[:], compare_op=ALU.is_ge,
                            fill=1.0, base=0, pattern=[[-1, 128]], channel_multiplier=1)
            g.wait_ge(self.cc2_h, 1)
            dG.dma(g.dma_start(
                out=msb[:],
                in_=ap["mask_all"].rearrange("(p a) e -> p (a e)", p=128)))
            gp.inc(g.engine_nop())
        mall_mark = dG.n

        @self.sec("S")
        def _(sy):
            dS.dma(sy.dma_start(out=bid[:], in_=ap["batch_id"][:, :]))
        ld_mark = dS.n

        @self.sec("T")
        def _(t):
            dS.wait(t, ld_mark)
            dG.wait(t, mall_mark)
            gp.wait(t)
            pe.inc(t.matmul(pb128[:, :], c["ones_row1f"][0:1, :], bid[:, :],
                            start=True, stop=True))

        @self.sec("V")
        def _(v):
            pe.wait(v)
            v.tensor_copy(me1[:], pb128[:, :])
            v.tensor_copy(mall[:], msb[:])
            v.drain()
            v.tensor_scalar(iotaEr[:], iotaEr[:], me1[:, 0:1], None,
                            op0=ALU.is_equal)
            v.drain()
            v.tensor_tensor(mall[:], mall[:], iotaEr[:], op=ALU.mult)
            v.drain()
            m3 = mall[:].rearrange("p (a e) -> p a e", e=E)
            v.tensor_tensor(m3[:, :, 0:4], m3[:, :, 0:4], m3[:, :, 4:8], op=ALU.add)
            v.drain()
            v.tensor_tensor(m3[:, :, 0:2], m3[:, :, 0:2], m3[:, :, 2:4], op=ALU.add)
            v.drain()
            v.tensor_tensor(mcolP[:].rearrange("p (a one) -> p a one", one=1),
                            m3[:, :, 0:1], m3[:, :, 1:2], op=ALU.add)
            v.drain()
            dve.inc(v.nop())
        mcolp_mark = dve.n

        for hh in range(2):
            @self.sec("T")
            def _(t, hh=hh):
                dve.wait(t)
                pe.inc(t.transpose(ppos[:, 0:128],
                                   mcolP[:, hh * 128:(hh + 1) * 128], c["ident_f"][:]))

            @self.sec("V")
            def _(v, hh=hh):
                pe.wait(v)
                mv = mcol[:].rearrange("k (p two) -> k p two", two=2)
                v.tensor_copy(mv[:, :, hh:hh + 1],
                              ppos[:, 0:128].rearrange("k (p one) -> k p one", one=1))
                v.drain()
                dve.inc(v.nop())

        @self.sec("V")
        def _(v):
            v.tensor_copy(mcolb[:], mcol[:])
            v.drain()
            dve.inc(v.nop())
        mcol_mark = dve.n
        import os
        if os.environ.get("DX_CUT") == "A":
            return

        @self.sec("T")
        def _(t):
            dve.wait(t, mcol_mark)
            pe.inc(t.matmul(ppos[:, :], tri1[:], mcolb[:], start=True, stop=True))
            pe.inc(t.matmul(pts[0:1, :], c["ones_col_bf"][:, :], mcolb[:],
                            start=True, stop=True))
        pos_mark = pe.n

        @self.sec("V")
        def _(v):
            pe.wait(v, pos_mark)
            v.tensor_copy(posin[:], ppos[:, :])
            v.tensor_copy(tsrow[0:1, :], pts[0:1, :])
            v.drain()
            dve.inc(v.nop())
        ts_mark = dve.n
        import os
        if os.environ.get("DX_CUT") == "B":
            return

        for hh in range(2):
            @self.sec("T")
            def _(t, hh=hh):
                dve.wait(t)
                pe.inc(t.matmul(pb128[:, :], tsrow[0:1, hh * 128:(hh + 1) * 128],
                                c["ones_row1f"][0:1, 0:1], start=True, stop=True))

            @self.sec("V")
            def _(v, hh=hh):
                pe.wait(v)
                v.tensor_copy(tsT[:, hh:hh + 1], pb128[:, :])
                v.drain()
                dve.inc(v.nop())
        tsT_mark = dve.n
        import os
        if os.environ.get("DX_CUT") == "C":
            return

        @self.sec("T")
        def _(t):
            dve.wait(t, tsT_mark)
            pe.inc(t.matmul(pcol[:, :], tri1[:], tsT[:, :], start=True, stop=True))
            pe.inc(t.matmul(pts[0:1, 0:2], c["ones_col_bf"][:, :], tsT[:, :],
                            start=True, stop=True))
        pfx_mark = pe.n

        @self.sec("V")
        def _(v):
            pe.wait(v, pfx_mark)
            v.tensor_copy(pbase[:], pcol[:, :])
            v.tensor_copy(thalf[0:1, 0:2], pts[0:1, 0:2])
            v.drain()
            dve.inc(v.nop())
        pb_mark = dve.n
        import os
        if os.environ.get("DX_CUT") == "D":
            return

        @self.sec("T")
        def _(t):
            dve.wait(t, pb_mark)
            pe.inc(t.matmul(pb128[:, :], c["ones_row1f"][0:1, :], thalf[0:1, 0:1],
                            start=True, stop=True))

        @self.sec("V")
        def _(v):
            pe.wait(v)
            v.tensor_copy(tot0[:], pb128[:, :])
            v.drain()
            v.tensor_tensor(pbase[:, 1:2], pbase[:, 1:2], tot0[:], op=ALU.add)
            v.drain()
            dve.inc(v.nop())
        base_mark = dve.n
        import os
        if os.environ.get("DX_CUT") == "E":
            return

        for hh in range(2):
            @self.sec("T")
            def _(t, hh=hh):
                dve.wait(t)
                pe.inc(t.matmul(prow1[0:1, :], pbase[:, hh:hh + 1], c["ident_f"][:],
                                start=True, stop=True))

            @self.sec("V")
            def _(v, hh=hh):
                pe.wait(v)
                v.tensor_copy(brow[0:1, hh * 128:(hh + 1) * 128], prow1[0:1, :])
                v.drain()
                dve.inc(v.nop())
        brow_mark = dve.n
        import os
        if os.environ.get("DX_CUT") == "F":
            return

        @self.sec("T")
        def _(t):
            dve.wait(t, brow_mark)
            pe.inc(t.matmul(ppos[:, :], c["ones_row1f"][0:1, :], brow[0:1, :],
                            start=True, stop=True))
        bbc_mark = pe.n

        @self.sec("V")
        def _(v):
            pe.wait(v, bbc_mark)
            v.tensor_copy(bbc[:], ppos[:, :])
            v.drain()
            dve.inc(v.nop())
        bbccp_mark = dve.n

        @self.sec("T")
        def _(t):
            dve.wait(t, bbccp_mark)
            pe.inc(t.matmul(ppos[:, :], c["ones_row1f"][0:1, :], tsrow[0:1, :],
                            start=True, stop=True))
        ttb_mark = pe.n

        @self.sec("V")
        def _(v):
            pe.wait(v, ttb_mark)
            v.tensor_copy(ttb[:], ppos[:, :])
            v.drain()
            pav = pairadd[:].rearrange("p (a two) -> p a two", two=2)
            ttv = ttb[:].rearrange("p (a two) -> p a two", two=2)
            v.tensor_scalar(pav[:, :, 0:1], ttv[:, :, 0:1], 0.0, None, op0=ALU.mult)
            v.tensor_copy(pav[:, :, 1:2], ttv[:, :, 0:1])
            v.drain()
            v.tensor_tensor(posg[:], posin[:], bbc[:], op=ALU.add)
            v.drain()
            v.tensor_scalar(t5[:], posg[:], float(CAP), None, op0=ALU.is_lt)
            v.drain()
            v.tensor_tensor(keepg[:], t5[:], mcol[:], op=ALU.mult)
            v.drain()
            v.tensor_tensor(posin[:], posin[:], pairadd[:], op=ALU.add)
            v.drain()
            v.tensor_scalar(t5[:], keepg[:], -BIGRANK, BIGRANK,
                            op0=ALU.mult, op1=ALU.add)
            v.tensor_tensor(grankd[:], posin[:], keepg[:], op=ALU.mult)
            v.drain()
            v.tensor_tensor(grankd[:], grankd[:], t5[:], op=ALU.add)
            v.drain()
            brv = brow[0:1, :].rearrange("one (a two) -> one a two", two=2)
            v.tensor_scalar(btab2_row[0:1, :].rearrange("one (a e) -> one a e", e=1),
                            brv[:, :, 0:1], float(CAP), None, op0=ALU.min)
            v.drain()
            v.tensor_copy(btab2_i32[:], btab2_row[:])
            v.drain()
            dve.inc(v.nop())
        rt_mark = dve.n
        import os
        if os.environ.get("DX_CUT") == "G":
            return

        @self.sec("S")
        def _(sy):
            dve.wait(sy, rt_mark)
            dS.dma(sy.dma_start(out=ap["btab2"][:, :], in_=btab2_i32[:]))
        bt2_mark = dS.n

        import os
        if os.environ.get("DX_CUT") == "1":
            return

        self._dx_ps.close()
        DRING = 4
        NPAIR = NTILE // 2
        h2t = [self.sbuf(f"dx_h2_{s}", [128, D], BF16) for s in range(4)]
        gtmp = self.sbuf("dx_gt", [128, 128])
        gdl0 = [self.sbuf(f"dx_gl0_{s}", [128, 128], BF16) for s in range(2)]
        gdl1 = [self.sbuf(f"dx_gl1_{s}", [128, 128], BF16) for s in range(2)]
        gdh1 = [self.sbuf(f"dx_gh1_{s}", [128, 128], BF16) for s in range(2)]
        iotaF128 = self.sbuf("dx_iF128", [128, 128])
        xstage = self.sbuf("dx_xs", [128, DRING * 2 * D], BF16)
        bcur = self.sbuf("dx_bcur", [1, DRING], I32)
        pplo = [self.psum(f"dx_pl_{s}", [128, D]) for s in range(2)]
        pphi = [self.psum(f"dx_ph_{s}", [128, D]) for s in range(2)]
        tld_h = self.tld_h
        tld_base = self.tld_n
        act_base = act.n
        dG_base = dG.n
        dxh = self.dxh_h
        gb_marks, mm_marks, ev_marks = {}, {}, {}

        @self.sec("G")
        def _(g):
            g.iota(iotaF128[:], pattern=[[1, 128]], base=128, channel_multiplier=0,
                   allow_small_or_imprecise_dtypes=True)
            gp.inc(g.engine_nop())

        @self.sec("S")
        def _(sy):
            sy.wait_ge(self.cc2_h, 2)
            for tt in range(4):
                sy.dma_start(out=h2t[tt][:],
                             in_=ap["h2_all"][tt * 128:(tt + 1) * 128, :]
                             ).then_inc(dxh, 16)

        for p in range(NPAIR):
            j0, j1 = 2 * p, 2 * p + 1

            @self.sec("V")
            def _(v, p=p, j0=j0, j1=j1):
                if p >= 2:
                    pe.wait(v, mm_marks[p - 2])
                v.tensor_scalar(gtmp[:], c["iotaF"][:], grankd[:, j0:j0 + 1], None,
                                op0=ALU.is_equal)
                v.drain()
                v.tensor_copy(gdl0[p % 2][:], gtmp[:])
                v.drain()
                v.tensor_scalar(gtmp[:], c["iotaF"][:], grankd[:, j1:j1 + 1], None,
                                op0=ALU.is_equal)
                v.drain()
                v.tensor_copy(gdl1[p % 2][:], gtmp[:])
                v.drain()
                v.tensor_scalar(gtmp[:], iotaF128[:], grankd[:, j1:j1 + 1], None,
                                op0=ALU.is_equal)
                v.drain()
                v.tensor_copy(gdh1[p % 2][:], gtmp[:])
                v.drain()
                dve.inc(v.nop())
            gb_marks[p] = dve.n

            @self.sec("T")
            def _(t, p=p, j0=j0, j1=j1):
                if p == 0:
                    gp.wait(t)
                t.wait_ge(dxh, 16 * (j1 + 1))
                dve.wait(t, gb_marks[p])
                if p >= 2:
                    act.wait(t, ev_marks[p - 2])
                for cH in range(2):
                    sl = slice(cH * 512, (cH + 1) * 512)
                    t.matmul(pplo[p % 2][:, sl], gdl0[p % 2][:],
                             h2t[j0 % 4][:, sl], start=True, stop=False)
                    t.matmul(pplo[p % 2][:, sl], gdl1[p % 2][:],
                             h2t[j1 % 4][:, sl], start=False, stop=True)
                    _lt = t.matmul(pphi[p % 2][:, sl], gdh1[p % 2][:],
                                   h2t[j1 % 4][:, sl], start=True, stop=True)
                pe.inc(_lt)
            mm_marks[p] = pe.n

            @self.sec("A")
            def _(a, p=p, j0=j0, j1=j1):
                pe.wait(a, mm_marks[p])
                if p >= DRING:
                    a.wait_ge(dG.h, dG_base + 16 * (p - DRING + 1))
                sl0 = (p % DRING) * 2 * D
                a.activation(xstage[:, sl0:sl0 + D], pplo[p % 2][:], AF.Identity)
                a.activation(xstage[:, sl0 + D:sl0 + 2 * D], pphi[p % 2][:],
                             AF.Identity)
                a.drain()
                act.inc(a.nop())
                if j1 + 4 < NTILE:
                    a.dma_start(out=h2t[j0 % 4][:],
                                in_=ap["h2_all"][(j0 + 4) * 128:(j0 + 5) * 128, :]
                                ).then_inc(dxh, 16)
                    a.dma_start(out=h2t[j1 % 4][:],
                                in_=ap["h2_all"][(j1 + 4) * 128:(j1 + 5) * 128, :]
                                ).then_inc(dxh, 16)
            ev_marks[p] = act.n

        regs = [nc.alloc_registers(f"dxr{k}", engines=[ET.Pool]).handles[0]
                for k in range(DRING)]
        vals = [RuntimeValue(r, min_val=0, max_val=CAP) for r in regs]

        @self.sec("G")
        def _(g):
            g.wait_ge(dS.h, bt2_mark)
            with g.Fori(0, NPAIR // DRING) as ih:
                g.dma_start(out=bcur[0:1, 0:DRING],
                            in_=ap["btab2"][0:1, DynSlice(ih * DRING, DRING)]
                            ).then_inc(tld_h, 16)
                g.wait_ge(tld_h, tld_base + (ih + 1) * 16)
                g.wait_ge(act.h, act_base + (ih + 1) * DRING)
                for k in range(DRING):
                    nc.reg_load(regs[k], bcur[0:1, k:k + 1])
                    g.dma_start(
                        out=ap["x_ex"][DynSlice(vals[k], 256), :]
                        .rearrange("(a p) d -> p a d", p=128),
                        in_=xstage[:, k * 2 * D:(k + 1) * 2 * D]
                        .rearrange("p (a d) -> p a d", d=D),
                        accum_op=ALU.add).then_inc(dG.h, 16)
                    g.wait_ge(dG.h, dG_base + (ih * DRING + k + 1) * 16)
        dG.n += 16 * DRING * (NPAIR // DRING)
        self.tld_n += 16 * (NPAIR // DRING)

    def st_experts_ex(self):
        nc, ap, c = self.nc, self.ap, self.c
        dS, dG, pe, dve, act, gp = (self.sem[k] for k in
                                    ["dS", "dG", "pe", "dve", "act", "gp"])
        NCHUNK = CAP // 512
        w1sb = self.sbuf("exx_w1", [128, ND * FF], BF16)
        w2sb = self.sbuf("exx_w2", [128, NF * D], BF16)
        b1col = self.sbuf("exx_b1c", [128, NF])
        b2row = self.sbuf("exx_b2", [1, D], BF16)
        onesr = self.sbuf("exx_ones", [1, 512], BF16)
        xet = [self.sbuf(f"exx_x{s}", [128, ND * 512], BF16) for s in range(2)]
        midt = self.sbuf("exx_mid", [128, NF * 512], BF16)
        ysb = self.sbuf("exx_y", [128, 4 * D], BF16)
        pg = [self.psum(f"exx_pg{i}", [128, 512]) for i in range(2)]
        pg2 = [self.psum(f"exx_pq{i}", [128, 512]) for i in range(2)]

        @self.sec("G")
        def _(g):
            g.memset(ysb[:, 0:D], 0.0)
            dG.dma(g.dma_start(out=ap["y_ex"][CAP:CAP + 128, :], in_=ysb[:, 0:D]))
            for kk in range(ND):
                dG.dma(g.dma_start(out=w1sb[:, kk * FF:(kk + 1) * FF],
                                   in_=ap["w1"][kk * 128:(kk + 1) * 128, :]))
            for kk in range(NF):
                dG.dma(g.dma_start(out=w2sb[:, kk * D:(kk + 1) * D],
                                   in_=ap["w2"][kk * 128:(kk + 1) * 128, :]))
            dG.dma(g.dma_start(out=b1col[:],
                               in_=ap["b1"].rearrange("(a p) -> p a", p=128),
                               allow_slow_non_contiguous=True))
            dG.dma(g.dma_start(out=b2row[:],
                               in_=ap["b2"].rearrange("(one d) -> one d", one=1)))
            g.memset(onesr[:], 1.0)
            gp.inc(g.engine_nop())
        w_mark = dG.n

        x_marks, g1_done, ev_marks, y_marks, st_marks = {}, {}, {}, {}, {}
        for ch in range(NCHUNK):
            rows = slice(ch * 512, (ch + 1) * 512)

            @self.sec("S")
            def _(sy, ch=ch, rows=rows):
                if ch >= 2:
                    pe.wait(sy, g1_done[ch - 2])
                for kk in range(ND):
                    dS.dma(sy.dma_start(
                        out=xet[ch % 2][:, kk * 512:(kk + 1) * 512],
                        in_=ap["x_ex"][rows, kk * 128:(kk + 1) * 128],
                        transpose=True))
            x_marks[ch] = dS.n

            for mt in range(NF):
                @self.sec("T")
                def _(t, ch=ch, mt=mt):
                    if ch == 0 and mt == 0:
                        dG.wait(t, w_mark)
                        gp.wait(t)
                    if mt == 0:
                        dS.wait(t, x_marks[ch])
                    key = ch * NF + mt
                    if key >= 2:
                        act.wait(t, ev_marks[key - 2])
                    ps = pg[mt % 2]
                    for kk in range(ND):
                        _lt = t.matmul(ps[:, :],
                                 w1sb[:, kk * FF + mt * 128:kk * FF + (mt + 1) * 128],
                                 xet[ch % 2][:, kk * 512:(kk + 1) * 512],
                                 start=(kk == 0), stop=(kk == ND - 1))
                    pe.inc(_lt)
                mm = pe.n

                @self.sec("A")
                def _(a, ch=ch, mt=mt, mm=mm):
                    pe.wait(a, mm)
                    a.activation(midt[:, mt * 512:(mt + 1) * 512],
                                 pg[mt % 2][:], AF.Gelu, bias=b1col[:, mt:mt + 1])
                    a.drain()
                    act.inc(a.nop())
                ev_marks[ch * NF + mt] = act.n
            g1_done[ch] = pe.n
            gelu_all = act.n

            for st in range(4):
                for c2 in range(2):
                    @self.sec("T")
                    def _(t, ch=ch, st=st, c2=c2, gelu_all=gelu_all):
                        act.wait(t, gelu_all)
                        key = st * 2 + c2
                        if key >= 2 or ch > 0:
                            dve.wait(t, y_marks.get((ch, key - 2),
                                                    y_marks.get((ch - 1, 6 + key), 0)))
                        ps = pg2[c2]
                        for kk in range(NF):
                            t.matmul(ps[:, :],
                                     midt[:, kk * 512 + st * 128:kk * 512 + (st + 1) * 128],
                                     w2sb[:, kk * D + c2 * 512:kk * D + (c2 + 1) * 512],
                                     start=(kk == 0), stop=False)
                        _lt = t.matmul(ps[:, :], onesr[0:1, 0:128],
                                       b2row[0:1, c2 * 512:(c2 + 1) * 512],
                                       start=False, stop=True)
                        pe.inc(_lt)
                    ymm = pe.n

                    @self.sec("V")
                    def _(v, ch=ch, st=st, c2=c2, ymm=ymm):
                        pe.wait(v, ymm)
                        if st == 0 and c2 == 0 and ch == 0:
                            dG.wait(v, w_mark)
                        if st == 0 and c2 == 0 and ch >= 1:
                            dS.wait(v, st_marks[ch - 1])
                        v.tensor_copy(ysb[:, st * D + c2 * 512:
                                          st * D + (c2 + 1) * 512],
                                      pg2[c2][:])
                        v.drain()
                        dve.inc(v.nop())
                    y_marks[(ch, st * 2 + c2)] = dve.n

            @self.sec("S")
            def _(sy, ch=ch, rows=rows):
                dve.wait(sy, y_marks[(ch, 7)])
                dS.dma(sy.dma_start(
                    out=ap["y_ex"][rows, :].rearrange("(a p) d -> p a d", p=128),
                    in_=ysb[:].rearrange("p (a d) -> p a d", d=D)))
            st_marks[ch] = dS.n

        NCH_A = YSPLIT // 512

        @self.sec("G")
        def _(g):
            g.wait_ge(dS.h, st_marks[NCH_A - 1])
            g.collective_compute(
                "AllGather", ALU.bypass, replica_groups=[list(range(B))],
                ins=[ap["y_ex"][0:YSPLIT, :].opt()],
                outs=[ap["y_gat2"][0:YAOFF, :].opt()]).then_inc(self.cc2_h, 1)
            g.wait_ge(dS.h, st_marks[NCHUNK - 1])
            g.wait_ge(dG.h, w_mark)
            g.collective_compute(
                "AllGather", ALU.bypass, replica_groups=[list(range(B))],
                ins=[ap["y_ex"][YSPLIT - 128:ESTRIDE, :].opt()],
                outs=[ap["y_gat2"][YAOFF:YG2, :].opt()]).then_inc(self.cc2_h, 1)

    def build(self):
        self._start(["dS", "dG", "pe", "dve", "act", "gp", "cc"])
        stages = ["constants", "st_ln_outer", "st_mingru", "st_diffusion",
                  "st_resid_ln", "st_qkv", "st_attention", "st_proj_r3",
                  "st_ln2_logits", "st_gather_moe",
                  "st_route", "st_dispatch_ex", "st_experts_ex", "st_combine"]
        if self.upto is not None:
            stages = stages[:stages.index(self.upto) + 1]
        with self.es:
            self.grank_all = self.es.enter_context(
                self.nc.sbuf_tensor("l2_grank", [128, NT * E], F32))
            self.gts_all = self.es.enter_context(
                self.nc.sbuf_tensor("l2_gts", [128, NT * E], F32))
            self.c["iotaF"] = self.es.enter_context(
                self.nc.sbuf_tensor("l2_iotaF", [128, 128], F32))
            self.c["identb"] = self.es.enter_context(
                self.nc.sbuf_tensor("l2_identb", [128, 128], BF16))
            self.tld_h = self.es.enter_context(self.nc.semaphore("tld"))
            self.tld_n = 0
            self.cc2_h = self.es.enter_context(self.nc.semaphore("cc2"))
            self.dxh_h = self.es.enter_context(self.nc.semaphore("dxh"))
            self.tld2_h = self.es.enter_context(self.nc.semaphore("tld2"))
            self.dx2_h = self.es.enter_context(self.nc.semaphore("dx2"))
            for st in stages:
                if st == "constants":
                    getattr(self, st)()
                else:
                    with self.stage():
                        getattr(self, st)()
                if st == "st_proj_r3":
                    self._attn_es.close()
            self.finale()
        return self.ap


def kernel(**inputs):
    f32 = lambda a: np.ascontiguousarray(np.asarray(a, dtype=np.float32))
    x = f32(inputs["L0_attn_x"])
    shared = dict(
        wz=f32(inputs["L0_attn_Wz"]), bz=f32(inputs["L0_attn_bz"]),
        wh=f32(inputs["L0_attn_Wh"]), bh=f32(inputs["L0_attn_bh"]),
        logd=f32(inputs["L0_attn_log_D"]),
        ln1g=f32(inputs["L0_ffn_ln1_g"]), ln1b=f32(inputs["L0_ffn_ln1_b"]),
        wqkv=f32(inputs["L0_ffn_Wqkv"]), wproj=f32(inputs["L0_ffn_Wproj"]),
        ln2g=f32(inputs["L0_ffn_ln2_g"]), ln2b=f32(inputs["L0_ffn_ln2_b"]),
        wgate=f32(inputs["L0_ffn_Wgate"]),
    )
    W1, B1 = f32(inputs["L0_ffn_W1"]), f32(inputs["L0_ffn_b1"])
    W2, B2 = f32(inputs["L0_ffn_W2"]), f32(inputs["L0_ffn_b2"])
    nc = bass.Bass("TRN2", target_bir_lowering=False)
    MK(nc).build()
    maps = [dict(x=x[b], batch_id=np.full((1, 1), float(b), np.float32),
                 w1=np.ascontiguousarray(W1[b]), b1=np.ascontiguousarray(B1[b]),
                 w2=np.ascontiguousarray(W2[b]), b2=np.ascontiguousarray(B2[b]),
                 **shared)
            for b in range(B)]
    res = run_bass_kernel_spmd(nc, maps, core_ids=list(range(B)))
    out = np.stack([np.asarray(res.results[b]["out"], dtype=np.float32) for b in range(B)])
    return out
